# version 04
# baseline (speedup 1.0000x reference)
import contextlib
import numpy as np
import ml_dtypes
import concourse.bass as bass
import concourse.mybir as mybir
from concourse.bass_utils import run_bass_kernel_spmd

F32 = mybir.dt.float32
BF16 = mybir.dt.bfloat16
AF = mybir.ActivationFunctionType
ALU = mybir.AluOpType
AX = mybir.AxisListType
BF = ml_dtypes.bfloat16
INT32 = mybir.dt.int32

T = 4096
D = 1024
PT = D // 128
TC = 512
NCH = T // TC
SHIFT = 12.0
ENGINES = ("sync", "scalar", "vector", "tensor", "gpsimd")


class Prog:
    SEMS = ("sync", "sdma", "gdma", "scalar", "vector", "tensor", "gpsimd")
    DMA_SEM = {"sync": "sync", "scalar": "sdma", "gpsimd": "gdma"}

    def __init__(self, nc):
        self.nc = nc
        self.ops = {e: [] for e in ENGINES}
        self.counts = {s: 0 for s in self.SEMS}
        self.queue_extra = {}

    def add(self, eng, fn, waits=(), inc=1, sem=None):
        sem = sem or eng
        waits = [w for w in waits if w is not None]
        for w in waits:
            assert isinstance(w, tuple) and len(w) == 2 and w[0] in self.SEMS,                 f"bad wait token {w!r}"
        self.counts[sem] += inc
        self.ops[eng].append((fn, tuple(waits), inc, sem))
        return (sem, self.counts[sem])

    def dma(self, q, out, in_, waits=()):
        s = self.DMA_SEM[q]
        waits = list(waits) + ([(s, self.counts[s])] if self.counts[s] else [])
        if s in self.queue_extra:
            waits.append(self.queue_extra.pop(s))
        return self.add(q, lambda e, o=out, i=in_: e.dma_start(out=o, in_=i),
                        waits, inc=16, sem=s)

    def mm(self, out, lhsT, rhs, start, stop, waits=()):
        return self.add("tensor",
                        lambda e, o=out, l=lhsT, r=rhs, st=start, sp=stop:
                        e.matmul(o, l, r, start=st, stop=sp), waits)

    def act(self, out, in_, func, bias=0.0, scale=1.0, accum_out=None, waits=()):
        return self.add("scalar",
                        lambda e, o=out, i=in_, f=func, b=bias, s=scale, a=accum_out:
                        e.activation(out=o, in_=i, func=f, bias=b, scale=s,
                                     accum_out=a), waits)

    def tt(self, eng, out, in0, in1, op, waits=()):
        return self.add(eng,
                        lambda e, o=out, a=in0, b=in1, p=op:
                        e.tensor_tensor(out=o, in0=a, in1=b, op=p), waits)

    def ts(self, eng, out, in0, s1, s2=None, op0=ALU.mult, op1=ALU.bypass,
           accum_out=None, waits=()):
        return self.add(eng,
                        lambda e, o=out, a=in0, x=s1, y=s2, p0=op0, p1=op1, ac=accum_out:
                        e.tensor_scalar(out=o, in0=a, scalar1=x, scalar2=y,
                                        op0=p0, op1=p1, accum_out=ac), waits)

    def stt(self, eng, out, in0, s, in1, op0, op1, accum_out=None, waits=()):
        return self.add(eng,
                        lambda e, o=out, a=in0, x=s, b=in1, p0=op0, p1=op1, ac=accum_out:
                        e.scalar_tensor_tensor(out=o, in0=a, scalar=x, in1=b,
                                               op0=p0, op1=p1, accum_out=ac), waits)

    def copy(self, eng, out, in_, waits=()):
        return self.add(eng, lambda e, o=out, i=in_: e.tensor_copy(out=o, in_=i),
                        waits)

    def reduce(self, eng, out, in_, op, waits=(), negate=None, absval=None):
        return self.add(eng,
                        lambda e, o=out, i=in_, p=op, n=negate, a=absval:
                        e.tensor_reduce(o, i, AX.X, p, apply_absolute_value=a,
                                        negate=n), waits)

    def scan(self, eng, out, d0, d1, init, waits=()):
        return self.add(eng,
                        lambda e, o=out, a=d0, b=d1, i=init:
                        e.tensor_tensor_scan(out=o, data0=a, data1=b, initial=i,
                                             op0=ALU.mult, op1=ALU.add), waits)

    def transpose(self, out_psum, in_sbuf, ident, waits=()):
        return self.add("tensor",
                        lambda e, o=out_psum, i=in_sbuf, d=ident:
                        e.transpose(o, i, d), waits)

    def vtrans(self, out, in_, waits=()):
        return self.add("vector",
                        lambda e, o=out, i=in_: e.transpose(out=o, in_=i), waits)

    def memset(self, eng, ap, val, waits=()):
        return self.add(eng, lambda e, a=ap, v=val: e.memset(a, v), waits)

    def recip(self, out, in_, waits=()):
        return self.add("vector",
                        lambda e, o=out, i=in_: e.reciprocal(out=o, in_=i), waits)

    def barrier(self):
        toks = [(s, c) for s, c in self.counts.items() if c]
        for e in ENGINES:
            w = [t for t in toks if not (t[0] == e and e != "sync")]
            self.add(e, lambda eng: eng.nop(), w, inc=0)

    def all_tokens(self):
        return [(s, c) for s, c in self.counts.items() if c]

    def finalize(self, sems):
        nc = self.nc
        with nc.Block() as blk:
            for ename in ENGINES:
                ops = self.ops[ename]

                def make(ops=ops):
                    def body(eng):
                        for fn, waits, inc, sem in ops:
                            for (we, wv) in waits:
                                eng.wait_ge(sems[we], wv)
                            inst = fn(eng)
                            if inst is not None and inc:
                                inst.then_inc(sems[sem], inc)
                    return body
                getattr(blk, ename)(make())


def make_prog(nc, es):
    sems = {s: es.enter_context(nc.semaphore(f"sem_{s}")) for s in Prog.SEMS}
    return Prog(nc), sems


def sb(nc, es, name, shape, dtype):
    return es.enter_context(nc.sbuf_tensor(name, shape, dtype))


def ps(nc, es, name, shape, dtype=F32):
    return es.enter_context(nc.psum_tensor(name, shape, dtype))


class Consts:
    def __init__(self, nc, es, p):
        self.ident = sb(nc, es, "c_ident", [128, 128], BF16)
        self.ones_col = sb(nc, es, "c_ones_col", [128, 1], F32)
        self.ones_col_bf = sb(nc, es, "c_ones_col_bf", [128, 1], BF16)
        self.ones_row = sb(nc, es, "c_ones_row", [1, 128], F32)
        self.ones_row_bf = sb(nc, es, "c_ones_row_bf", [1, 128], BF16)
        d_id = nc.dram_tensor("cd_ident", [128, 128], F32, kind="ExternalInput").ap()
        d_oc = nc.dram_tensor("cd_ones", [128], F32, kind="ExternalInput").ap()
        self.SCALS = {0.125: 0, -SHIFT: 1, 0.03125: 2, 1e-6: 3, -1.0: 4,
                      0.5: 5, 0.1: 6, 1.0: 7}
        d_sc = nc.dram_tensor("cd_scal", [128, 8], F32, kind="ExternalInput").ap()
        self.scal = sb(nc, es, "c_scal", [128, 8], F32)
        self.tok_scal = p.dma("sync", self.scal[:], d_sc)
        self.tok_ones_col = p.dma("sync", self.ones_col[:, 0:1],
                                  d_oc.rearrange("(k o) -> k o", o=1))
        self.tok_ones_row = p.dma("sync", self.ones_row[:, :],
                                  d_oc.rearrange("(o k) -> o k", o=1))
        self.tok_ident = p.dma("gpsimd", self.ident[:], d_id)
        self.identF = sb(nc, es, "c_identF", [128, 128], F32)
        self.tok_identF = p.dma("sync", self.identF[:], d_id)
        self.tok_ones_col_bf = p.dma("gpsimd", self.ones_col_bf[:, 0:1],
                                     d_oc.rearrange("(k o) -> k o", o=1))
        self.tok_ones_row_bf = p.dma("gpsimd", self.ones_row_bf[:, :],
                                     d_oc.rearrange("(o k) -> o k", o=1))

    def sc(self, v):
        return self.scal[:, self.SCALS[v]:self.SCALS[v] + 1]


def load_vec_pf(nc, es, p, name, dram_ap, n, dtype=F32, q="sync"):
    t = sb(nc, es, name, [128, n], dtype)
    tok = p.dma(q, t[:], dram_ap)
    return t, tok


def emit_ln(nc, es, p, C, xt, xtok, tag, get_tile=None, mps_ap=None,
            qps_ap=None):
    if get_tile is None:
        def get_tile(c, i):
            return xt[i][:, c * TC:(c + 1) * TC], xtok[i]
    sq = [sb(nc, es, f"{tag}_sq{i}", [128, TC], F32) for i in range(2)]
    rstd = sb(nc, es, f"{tag}_rstd", [1, T], F32)
    negmr = sb(nc, es, f"{tag}_negmr", [1, T], F32)
    m_r = sb(nc, es, f"{tag}_mr", [1, TC], F32)
    q_r = sb(nc, es, f"{tag}_qr", [1, TC], F32)
    s_r = sb(nc, es, f"{tag}_sr", [1, TC], F32)
    mps = (mps_ap if mps_ap is not None
           else ps(nc, es, f"{tag}_mps", [1, TC])[:])
    qps = (qps_ap if qps_ap is not None
           else ps(nc, es, f"{tag}_qps", [1, TC])[:])
    ev_m = None
    ev_q = None
    sq_toks = []
    last = None
    for c in range(NCH):
        cs = slice(c * TC, (c + 1) * TC)
        tm = tq = None
        for i in range(PT):
            b = sq[i % 2]
            xap, xtk = get_tile(c, i)
            w = [xtk, sq_toks[-2] if len(sq_toks) >= 2 else None]
            tsq = p.act(b[:], xap, AF.Square, waits=w)
            tm = p.mm(mps, C.ones_col_bf[:], xap, start=(i == 0),
                      stop=(i == PT - 1), waits=[xtk, ev_m, C.tok_ones_col_bf])
            tq = p.mm(qps, C.ones_col[:], b[:], start=(i == 0),
                      stop=(i == PT - 1), waits=[tsq, ev_q, C.tok_ones_col])
            sq_toks.append(tq)
        t1 = p.ts("vector", m_r[:], mps, 1.0 / D, waits=[tm, ev_m])
        ev_m = t1
        t2 = p.act(s_r[:], m_r[:], AF.Square, waits=[t1, ev_q])
        t3 = p.ts("vector", q_r[:], qps, 1.0 / D, 1e-5, ALU.mult, ALU.add,
                  waits=[tq])
        ev_q = t3
        t4 = p.tt("vector", q_r[:], q_r[:], s_r[:], ALU.subtract, waits=[t2, t3])
        t5 = p.act(s_r[:], q_r[:], AF.Ln, waits=[t4])
        t5b = p.ts("vector", s_r[:], s_r[:], -0.5, waits=[t5])
        t6 = p.act(rstd[:, cs], s_r[:], AF.Exp, waits=[t5b])
        last = p.stt("vector", negmr[:, cs], m_r[:], -1.0, rstd[:, cs],
                     ALU.mult, ALU.mult, waits=[t6])
        ev_m = last
        ev_q = last
    return rstd, negmr, last


def emit_y_chunk(nc, es, p, C, xt, xtok, rstd, negmr, stok, c, ytiles, tag,
                 planes=None, ydtype_note=None):
    raise NotImplementedError


class LNPlanes:
    def __init__(self, nc, es, p, C, rstd, negmr, stok, tag, rpl_ps=None,
                 mpl_ps=None):
        self.nc, self.p, self.C = nc, p, C
        self.rstd, self.negmr, self.stok = rstd, negmr, stok
        self.rpl_ps = rpl_ps if rpl_ps is not None \
            else ps(nc, es, f"{tag}_rplps", [128, TC])[:]
        self.mpl_ps = mpl_ps if mpl_ps is not None \
            else ps(nc, es, f"{tag}_mplps", [128, TC])[:]
        self.rpl = sb(nc, es, f"{tag}_rpl", [128, TC], F32)
        self.mpl = sb(nc, es, f"{tag}_mpl", [128, TC], F32)
        self.scr = [sb(nc, es, f"{tag}_yscr{i}", [128, TC], F32) for i in range(2)]
        self.last_use = []
        self.scr_use = {}

    def chunk_planes(self, c):
        p, C = self.p, self.C
        cs = slice(c * TC, (c + 1) * TC)
        w = [self.stok, C.tok_ones_row] + self.last_use
        self.last_use = []
        t1 = p.mm(self.rpl_ps, C.ones_row[:], self.rstd[:, cs], True, True, waits=w)
        t2 = p.mm(self.mpl_ps, C.ones_row[:], self.negmr[:, cs], True, True, waits=w)
        e1 = p.act(self.rpl[:], self.rpl_ps, AF.Identity, waits=[t1])
        e2 = p.act(self.mpl[:], self.mpl_ps, AF.Identity, waits=[t2])
        self.ptok = [e1, e2]
        return [e1, e2]

    def norm_tile(self, i, c, xt, xtok, out_ap, eng="vector", war=None):
        p = self.p
        cs = slice(c * TC, (c + 1) * TC)
        s = self.scr[i % 2]
        war = [war] if (war is None or isinstance(war, tuple)) else list(war)
        w = [xtok, self.scr_use.get(i % 2)] + list(war) + self.ptok
        xap = xt[:, cs] if tuple(xt.shape)[1] > TC else xt
        t1 = p.tt(eng, s[:], xap, self.rpl[:], ALU.mult, waits=w)
        t2 = p.tt(eng, out_ap, s[:], self.mpl[:], ALU.add, waits=[t1])
        self.scr_use[i % 2] = t2
        self.last_use.append(t2)
        return t2


class LNChunk:
    def __init__(self, nc, es, p, C, tag, rpl_ps=None, mpl_ps=None,
                 mps_ap=None, qps_ap=None):
        self.p, self.C = p, C
        self.rstd = sb(nc, es, f"{tag}_rstd", [1, TC], F32)
        self.negmr = sb(nc, es, f"{tag}_negmr", [1, TC], F32)
        self.m_r = sb(nc, es, f"{tag}_mr", [1, TC], F32)
        self.q_r = sb(nc, es, f"{tag}_qr", [1, TC], F32)
        self.s_r = sb(nc, es, f"{tag}_sr", [1, TC], F32)
        self.sq = [sb(nc, es, f"{tag}_sq{i}", [128, TC], F32) for i in range(2)]
        self.rpl = sb(nc, es, f"{tag}_rpl", [128, TC], F32)
        self.mpl = sb(nc, es, f"{tag}_mpl", [128, TC], F32)
        self.scr = [sb(nc, es, f"{tag}_ys{i}", [128, TC], F32) for i in range(2)]
        self.mps = mps_ap if mps_ap is not None else ps(nc, es, f"{tag}_mps", [1, TC])[:]
        self.qps = qps_ap if qps_ap is not None else ps(nc, es, f"{tag}_qps", [1, TC])[:]
        self.rpl_ps = rpl_ps if rpl_ps is not None else ps(nc, es, f"{tag}_rpp", [128, TC])[:]
        self.mpl_ps = mpl_ps if mpl_ps is not None else ps(nc, es, f"{tag}_mpp", [128, TC])[:]
        self.sq_toks = []
        self.row_user = None
        self.ev_m = None
        self.ev_q = None
        self.ptok = []
        self.scr_use = {}
        self.pl_user = []

    def stats_and_planes(self, tiles):
        p, C = self.p, self.C
        tm = tq = None
        for i, (xap, xtk) in enumerate(tiles):
            b = self.sq[i % 2]
            w = [xtk, self.sq_toks[-2] if len(self.sq_toks) >= 2 else None]
            tsq = p.act(b[:], xap, AF.Square, waits=w)
            tm = p.mm(self.mps, C.ones_col_bf[:], xap, start=(i == 0),
                      stop=(i == PT - 1), waits=[xtk, self.ev_m,
                                                C.tok_ones_col_bf])
            tq = p.mm(self.qps, C.ones_col[:], b[:], start=(i == 0),
                      stop=(i == PT - 1), waits=[tsq, self.ev_q,
                                                C.tok_ones_col])
            self.sq_toks.append(tq)
        t1 = p.ts("vector", self.m_r[:], self.mps, 1.0 / D,
                  waits=[tm, self.ev_m, self.row_user])
        self.ev_m = t1
        t2 = p.act(self.s_r[:], self.m_r[:], AF.Square, waits=[t1, self.ev_q])
        t3 = p.ts("vector", self.q_r[:], self.qps, 1.0 / D, 1e-5, ALU.mult,
                  ALU.add, waits=[tq, self.row_user])
        self.ev_q = t3
        t4 = p.tt("vector", self.q_r[:], self.q_r[:], self.s_r[:],
                  ALU.subtract, waits=[t2, t3])
        t5 = p.act(self.s_r[:], self.q_r[:], AF.Ln, waits=[t4])
        t5b = p.ts("vector", self.s_r[:], self.s_r[:], -0.5, waits=[t5])
        t6 = p.act(self.rstd[:], self.s_r[:], AF.Exp, waits=[t5b])
        t7 = p.stt("vector", self.negmr[:], self.m_r[:], -1.0, self.rstd[:],
                   ALU.mult, ALU.mult, waits=[t6])
        w = [t7, C.tok_ones_row] + self.pl_user
        self.pl_user = []
        b1 = p.mm(self.rpl_ps, C.ones_row[:], self.rstd[:], True, True, waits=w)
        b2 = p.mm(self.mpl_ps, C.ones_row[:], self.negmr[:], True, True, waits=w)
        self.row_user = b2
        e1 = p.act(self.rpl[:], self.rpl_ps, AF.Identity, waits=[b1])
        e2 = p.act(self.mpl[:], self.mpl_ps, AF.Identity, waits=[b2])
        self.ptok = [e1, e2]

    def norm(self, i, xap, xtok, out_ap, eng="vector", war=None):
        p = self.p
        s = self.scr[i % 2]
        war = [war] if (war is None or isinstance(war, tuple)) else list(war)
        w = [xtok, self.scr_use.get(i % 2)] + list(war) + self.ptok
        t1 = p.tt(eng, s[:], xap, self.rpl[:], ALU.mult, waits=w)
        t2 = p.tt(eng, out_ap, s[:], self.mpl[:], ALU.add, waits=[t1])
        self.scr_use[i % 2] = t2
        self.pl_user.append(t2)
        return t2


class XStream:
    def __init__(self, nc, es, p, name, dram_ap, W=TC, q="sync", nbuf=2):
        self.p, self.W, self.q = p, W, q
        self.dram = dram_ap.rearrange("(a p) t -> p a t", p=128)
        self.bufs = [sb(nc, es, f"{name}{i}", [128, PT * W], BF16)
                     for i in range(nbuf)]
        self.nbuf = nbuf
        self.users = [None] * nbuf
        self.toks = [None] * nbuf

    def load(self, c):
        bi = c % self.nbuf
        b = self.bufs[bi]
        w = self.W
        tok = self.p.dma(self.q, b[:].rearrange("p (a k) -> p a k", a=PT),
                         self.dram[:, :, c * w:(c + 1) * w],
                         waits=[self.users[bi]])
        self.toks[bi] = tok
        self.users[bi] = None
        return bi

    def tile(self, bi, i):
        w = self.W
        return self.bufs[bi][:, i * w:(i + 1) * w], self.toks[bi]

    def use(self, bi, tok):
        self.users[bi] = tok


def _c(a):
    return np.ascontiguousarray(a)


def _pf2(v, inner):
    v = np.asarray(v, np.float32)
    nt = v.shape[0] // 128
    return _c(v.reshape(nt, 128, inner).transpose(1, 0, 2).reshape(128, nt * inner))


def _pf(v):
    v = np.asarray(v, np.float32)
    n = v.size // 128
    return _c(v.reshape(n, 128).T)


SCALS_HOST = [0.125, -SHIFT, 0.03125, 1e-6, -1.0, 0.5, 0.1, 1.0]


def host_consts():
    return {
        "cd_ident": np.eye(128, dtype=np.float32),
        "cd_ones": np.ones((128,), np.float32),
        "cd_scal": _c(np.tile(np.asarray(SCALS_HOST, np.float32)[None, :],
                              (128, 1))),
    }


MAGIC = float(3 * 2 ** 22)


def emit_quant(nc, es, p, tag, WMAX):
    ytr = sb(nc, es, f"{tag}_ytr", [128, WMAX], F32)
    yn = sb(nc, es, f"{tag}_yn", [128, WMAX], F32)
    st = sb(nc, es, f"{tag}_st", [128, WMAX // 2], F32)
    sc = sb(nc, es, f"{tag}_sc", [128, WMAX // 32], F32)
    si = sb(nc, es, f"{tag}_si", [128, WMAX // 32], F32)
    aa = sb(nc, es, f"{tag}_aa", [128, WMAX], F32)
    g2 = sb(nc, es, f"{tag}_g2", [128, WMAX], BF16)
    g4 = sb(nc, es, f"{tag}_g4", [128, WMAX], BF16)
    stp = sb(nc, es, f"{tag}_stp", [128, WMAX], BF16)
    inv = sb(nc, es, f"{tag}_inv", [128, WMAX], BF16)
    r1 = sb(nc, es, f"{tag}_r1", [128, WMAX], F32)
    xdt = sb(nc, es, f"{tag}_xdt", [128, WMAX], BF16)
    state = {"prev": None}

    def quant(in_ap, in_tok, out_ap, W, transpose_layout=True):
        nb = W // 32
        w0 = [in_tok, state["prev"]]
        t = p.vtrans(ytr[:, :W], in_ap, waits=w0) if transpose_layout             else p.copy("vector", ytr[:, :W], in_ap, waits=w0)
        v = ytr[:, :W].rearrange("p (b e) -> p b e", e=32)
        s3 = st[:, :W // 2].rearrange("p (b e) -> p b e", e=16)
        t = p.ts("vector", aa[:, :W].bitcast(INT32), ytr[:, :W].bitcast(INT32),
                 0x7FFFFFFF, None, ALU.bitwise_and, waits=[t])
        va = aa[:, :W].rearrange("p (b e) -> p b e", e=32)
        t = p.tt("vector", s3[:, :, 0:16], va[:, :, 0:16], va[:, :, 16:32],
                 ALU.max, waits=[t])
        for hw in (8, 4, 2, 1):
            t = p.tt("vector", s3[:, :, 0:hw], s3[:, :, 0:hw],
                     s3[:, :, hw:2 * hw], ALU.max, waits=[t])
        t = p.ts("vector", sc[:, :nb], s3[:, :, 0:1].squeeze(2), 1.0 / 7.5,
                 1e-30, ALU.mult, ALU.max, waits=[t])
        t = p.add("vector", lambda e: e.reciprocal(out=si[:, :nb], in_=sc[:, :nb]), [t])
        scb = sc[:, :nb].unsqueeze(2).broadcast_to([128, nb, 32])
        sib = si[:, :nb].unsqueeze(2).broadcast_to([128, nb, 32])
        t = p.tt("vector", yn[:, :W].rearrange("p (b e) -> p b e", e=32), v, sib,
                 ALU.mult, waits=[t])
        ta = p.ts("vector", aa[:, :W].bitcast(INT32), yn[:, :W].bitcast(INT32),
                  0x7FFFFFFF, None, ALU.bitwise_and, waits=[t])
        t2 = p.ts("vector", g2[:, :W], aa[:, :W], 2.0, None, ALU.is_ge, waits=[ta])
        t4 = p.ts("vector", g4[:, :W], aa[:, :W], 4.0, None, ALU.is_ge, waits=[ta])
        ts1 = p.ts("vector", stp[:, :W], g4[:, :W], 0.25, 0.125, ALU.mult,
                   ALU.add, waits=[t4])
        ts2 = p.stt("vector", stp[:, :W], g2[:, :W], 0.125, stp[:, :W],
                    ALU.mult, ALU.add, waits=[ts1, t2])
        ti1 = p.ts("vector", inv[:, :W], g4[:, :W], -2.0, 8.0, ALU.mult,
                   ALU.add, waits=[t4])
        ti2 = p.stt("vector", inv[:, :W], g2[:, :W], -4.0, inv[:, :W],
                    ALU.mult, ALU.add, waits=[ti1, t2])
        t = p.tt("vector", r1[:, :W], yn[:, :W], inv[:, :W], ALU.mult,
                 waits=[ti2, t])
        t = p.ts("vector", r1[:, :W], r1[:, :W], MAGIC, None, ALU.add, waits=[t])
        t = p.ts("vector", r1[:, :W], r1[:, :W], -MAGIC, None, ALU.add, waits=[t])
        t = p.tt("vector", r1[:, :W], r1[:, :W], stp[:, :W], ALU.mult,
                 waits=[t, ts2])
        t = p.tt("vector", xdt[:, :W].rearrange("p (b e) -> p b e", e=32),
                 r1[:, :W].rearrange("p (b e) -> p b e", e=32), scb, ALU.mult,
                 waits=[t])
        t = p.vtrans(out_ap, xdt[:, :W], waits=[t]) if transpose_layout             else p.copy("vector", out_ap, xdt[:, :W], waits=[t])
        state["prev"] = t
        return t

    return quant


def emit_l1_ffn(nc, es, p, C, xt, xtok, dram, tag, final=False):
    rstd, negmr, stok = emit_ln(nc, es, p, C, xt, xtok, f"{tag}ln")
    planes = LNPlanes(nc, es, p, C, rstd, negmr, stok, f"{tag}pl")
    quant = emit_quant(nc, es, p, f"{tag}q", D)

    w32 = [sb(nc, es, f"{tag}_w32_{i}", [128, D], F32) for i in range(2)]
    wdbuf = sb(nc, es, f"{tag}_wd", [128, D], BF16)
    wdT = [sb(nc, es, f"{tag}_wdT{i}", [128, D], BF16) for i in range(PT)]
    tps = ps(nc, es, f"{tag}_tps", [128, 128], BF16)
    ttok = [None] * PT
    w32tok = {}
    ev_prev = None
    for i in range(PT):
        b = w32[i % 2]
        lt = p.dma("sync", b[:], dram["lw"].rearrange("(a p) k -> a p k", p=128)[i],
                   waits=[w32tok.get(i % 2)])
        qt = quant(b[:], lt, wdbuf[:], D, transpose_layout=False)
        w32tok[i % 2] = qt
        for j in range(PT):
            tt = p.transpose(tps[:], wdbuf[:, j * 128:(j + 1) * 128], C.ident[:],
                             waits=[qt, C.tok_ident, ev_prev])
            ev_prev = p.copy("vector", wdT[j][:, i * 128:(i + 1) * 128], tps[:],
                             waits=[tt])
            ttok[j] = ev_prev
        w32tok[i % 2] = ev_prev
    bias, btok = load_vec_pf(nc, es, p, f"{tag}_b", dram["lb"], PT)

    yc = sb(nc, es, f"{tag}_yc", [128, TC], F32)
    xd = [sb(nc, es, f"{tag}_xd{i}", [128, TC], BF16) for i in range(PT)]
    osb = [sb(nc, es, f"{tag}_osb{i}", [128, TC], F32) for i in range(2)]
    oxt = [sb(nc, es, f"{tag}_ox{i}", [128, TC], F32 if final else BF16)
           for i in range(2)]
    pps = [ps(nc, es, f"{tag}_pps{i}", [128, TC]) for i in range(2)]
    xdtok = [None] * PT
    pps_user = [None, None]
    ob_user = [None, None]
    yuse = None
    for c in range(NCH):
        cs = slice(c * TC, (c + 1) * TC)
        planes.chunk_planes(c)
        for i in range(PT):
            ytok = planes.norm_tile(i, c, xt[i], xtok[i], yc[:], war=yuse)
            xdtok[i] = quant(yc[:], ytok, xd[i][:], TC)
            yuse = xdtok[i]
        for fo in range(PT):
            pb = fo % 2
            mmt = None
            for kt in range(PT):
                mmt = p.mm(pps[pb][:], wdT[kt][:, fo * 128:(fo + 1) * 128],
                           xd[kt][:], start=(kt == 0), stop=(kt == PT - 1),
                           waits=[xdtok[kt], ttok[kt], pps_user[pb]]
                           if kt == 0 else [xdtok[kt], ttok[kt]])
            et = p.act(osb[pb][:], pps[pb][:], AF.Identity,
                       bias=bias[:, fo:fo + 1], waits=[mmt, btok, ob_user[pb]])
            pps_user[pb] = et
            rt = p.tt("vector", oxt[pb][:], osb[pb][:], xt[fo][:, cs], ALU.add,
                      waits=[et, xtok[fo]])
            st = p.dma("sync", dram["xo"].rearrange("(a p) k -> a p k", p=128)[fo][:, cs],
                       oxt[pb][:], waits=[rt])
            ob_user[pb] = st
    return None


def finish(nc, p, sems_holder):
    p.finalize(sems_holder)


def build_l1_ffn_program(final=False, in_dtype="bf16"):
    nc = bass.Bass("TRN2", target_bir_lowering=False)
    es = contextlib.ExitStack()
    xin = nc.dram_tensor("xin", [D, T], BF16 if in_dtype == "bf16" else F32,
                         kind="ExternalInput").ap()
    lw = nc.dram_tensor("lw", [D, D], F32, kind="ExternalInput").ap()
    lb = nc.dram_tensor("lb", [128, D // 128], F32, kind="ExternalInput").ap()
    xo = nc.dram_tensor("xo", [D, T], F32 if final else BF16,
                        kind="ExternalOutput").ap()
    with es:
        p, sems = make_prog(nc, es)
        C = Consts(nc, es, p)
        xt = [sb(nc, es, f"xt{i}", [128, T], BF16) for i in range(PT)]
        xtok = []
        xv = xin.rearrange("(a p) k -> a p k", p=128)
        for i in range(PT):
            q = "gpsimd" if in_dtype == "f32" else "sync"
            xtok.append(p.dma(q, xt[i][:], xv[i]))
        emit_l1_ffn(nc, es, p, C, xt, xtok, {"lw": lw, "lb": lb, "xo": xo},
                    "lf", final=final)
        p.add("gpsimd", lambda e: e.engine_nop(), waits=p.all_tokens(), inc=1)
        p.finalize(sems)
    return nc



def emit_l0_attn_a(nc, es, p, C, dram, tag):
    QC = 256
    NQC = T // QC
    PSUM = ps(nc, es, f"{tag}_psum", [128, 4096])
    B = lambda i, w=TC: PSUM[:, i * TC:i * TC + w]

    xs = XStream(nc, es, p, f"{tag}xs", dram["xin"],
                 q="gpsimd")
    ln = LNChunk(nc, es, p, C, f"{tag}ln", rpl_ps=B(2), mpl_ps=B(3),
                 mps_ap=PSUM[0:1, 0:TC], qps_ap=PSUM[0:1, TC:2 * TC])

    wbuf = [sb(nc, es, f"{tag}_w{i}", [128, D], BF16) for i in range(PT)]
    wbuf2 = [sb(nc, es, f"{tag}_w2_{i}", [128, D], BF16) for i in range(PT)]

    def load_w(dram_w, bufs, waits=()):
        dv = dram_w.rearrange("(a p) k -> a p k", p=128)
        toks = [p.dma("gpsimd", bufs[i][:], dv[i], waits=waits)
                for i in range(PT)]
        return toks

    ybuf = sb(nc, es, f"{tag}_yb", [128, PT * TC], BF16)
    kc = [sb(nc, es, f"{tag}_kc{i}", [128, TC], BF16) for i in range(2)]
    vt = sb(nc, es, f"{tag}_vt", [128, 32 * 1024], BF16)
    sk3 = dram["s_k"].rearrange("(a p) t -> p a t", p=128)
    sy = dram["s_y"].rearrange("(a p) t -> p a t", p=128)
    xo3 = dram["xo"].rearrange("(a p) t -> p a t", p=128)

    wk_t = load_w(dram["wkT"], wbuf)
    ytoks = {}
    ps_user = [None, None]
    kc_user = [None, None]
    b1_war = []
    for c in range(NCH):
        bi = xs.load(c)
        ln.stats_and_planes([xs.tile(bi, i) for i in range(PT)])
        last = None
        for i in range(PT):
            xap, xtk = xs.tile(bi, i)
            last = ln.norm(i, xap, xtk, ybuf[:, i * TC:(i + 1) * TC],
                           war=b1_war)
        xs.use(bi, last)
        ytoks[c] = last
        spt = p.dma("sync", sy[:, :, c * TC:(c + 1) * TC],
                    ybuf[:].rearrange("p (a k) -> p a k", a=PT), waits=[last])
        for fo in range(PT):
            pb = fo % 2
            mmt = None
            for kt in range(PT):
                w = [last, wk_t[-1]] + ([ps_user[pb]] if kt == 0 else [])
                mmt = p.mm(B(4 + pb), wbuf[kt][:, fo * 128:(fo + 1) * 128],
                           ybuf[:, kt * TC:(kt + 1) * TC], start=(kt == 0),
                           stop=(kt == PT - 1), waits=w)
            et = p.act(kc[pb][:], B(4 + pb), AF.Identity,
                       waits=[mmt, kc_user[pb]])
            ps_user[pb] = et
            kc_user[pb] = p.dma("sync",
                                dram["s_k"][fo * 128:(fo + 1) * 128,
                                            c * TC:(c + 1) * TC],
                                kc[pb][:], waits=[et])
        b1_war = [mmt, spt]
    b1_last_mm = mmt

    wv_t = load_w(dram["wvT"], wbuf2)
    vtok = [None] * 32
    yb2 = sb(nc, es, f"{tag}_yb2", [128, PT * TC], BF16)
    yb2_user = None
    vps_user = [None, None]
    for c in range(NCH):
        ylt = p.dma("sync", yb2[:].rearrange("p (a k) -> p a k", a=PT),
                    sy[:, :, c * TC:(c + 1) * TC], waits=[ytoks[c], yb2_user])
        for tsub in range(TC // 128):
            kt_glob = c * (TC // 128) + tsub
            for half in range(2):
                pb = half
                mmt = None
                for kt in range(PT):
                    w = [ylt, wv_t[-1]] + ([vps_user[pb]] if kt == 0 else [])
                    mmt = p.mm(B(6 + pb), yb2[:, kt * TC + tsub * 128:
                                               kt * TC + tsub * 128 + 128],
                               wbuf2[kt][:, half * TC:(half + 1) * TC],
                               start=(kt == 0), stop=(kt == PT - 1), waits=w)
                et = p.copy("vector", vt[:, kt_glob * 1024 + half * TC:
                                         kt_glob * 1024 + (half + 1) * TC],
                            B(6 + pb), waits=[mmt])
                vps_user[pb] = et
                vtok[kt_glob] = et
        yb2_user = mmt

    wq_t = load_w(dram["wqT"], wbuf, waits=[ps_user[0], ps_user[1]])
    wo_t = load_w(dram["woT"], wbuf2, waits=[vtok[31]])
    qbuf = sb(nc, es, f"{tag}_qb", [64, 16 * QC], BF16)
    ptens = sb(nc, es, f"{tag}_pt", [128, 32 * QC], BF16)
    khb = sb(nc, es, f"{tag}_kh0", [64, T], BF16)
    obuf = sb(nc, es, f"{tag}_ob", [128, PT * QC], BF16)
    oev = sb(nc, es, f"{tag}_oev", [64, QC], F32)
    zrow = sb(nc, es, f"{tag}_zr", [1, QC], F32)
    zpl = sb(nc, es, f"{tag}_zpl", [64, QC], F32)
    dmask = sb(nc, es, f"{tag}_dm", [128, 2 * QC], BF16)
    dmk = p.dma("sync", dmask[:], dram["cd_dmask"])
    ybq = yb2
    xbq = xs.bufs[0][:, :PT * QC]
    oxb = [sb(nc, es, f"{tag}_oxb{i}", [128, QC], BF16) for i in range(2)]
    sps = PSUM[:, 0:1024]
    ozps = PSUM[:, 2 * TC:2 * TC + QC]
    zplps = PSUM[0:64, 3 * TC:3 * TC + QC]
    phase_c = ("vector", p.counts["vector"])
    sps_user = phase_c
    oz_user = phase_c
    zpl_wru = phase_c
    zpl_reader = None
    ybq_user = None
    xbq_user = None
    qbuf_user = None
    ob_user = None
    oxb_user = [None, None]
    kh_user = None
    head_war = None
    for qc in range(NQC):
        qs = slice(qc * QC, (qc + 1) * QC)
        ktmax = (qc + 1) * QC // 128
        ylt = p.dma("sync", ybq[:, :PT * QC].rearrange("p (a k) -> p a k",
                                                       a=PT),
                    sy[:, :, qs], waits=[ytoks[NCH - 1], ybq_user or yb2_user])
        xlt = p.dma("gpsimd", xbq.rearrange("p (a k) -> p a k", a=PT),
                    xs.dram[:, :, qs],
                    waits=[xbq_user])
        qtoks = []
        for fo in range(PT):
            pb = fo % 2
            mmt = None
            for kt in range(PT):
                w = [ylt, wq_t[-1]] + ([ps_user[pb]] if kt == 0 else [])
                mmt = p.mm(B(4 + pb, QC), wbuf[kt][:, fo * 128:(fo + 1) * 128],
                           ybq[:, kt * QC:(kt + 1) * QC], start=(kt == 0),
                           stop=(kt == PT - 1), waits=w)
            e_a = p.copy("vector", qbuf[:, (2 * fo) * QC:(2 * fo + 1) * QC],
                         PSUM[0:64, (4 + pb) * TC:(4 + pb) * TC + QC],
                         waits=[mmt, qbuf_user])
            e_b = p.copy("vector", qbuf[:, (2 * fo + 1) * QC:(2 * fo + 2) * QC],
                         PSUM[64:128, (4 + pb) * TC:(4 + pb) * TC + QC],
                         waits=[mmt])
            qbuf_user = None
            ps_user[pb] = e_b
            qtoks.append(e_b)
        ybq_user = mmt
        ob_toks = []
        for h in range(16):
            klen = ktmax * 128
            klt = p.dma("sync", khb[:, :klen],
                        dram["s_k"][h * 64:(h + 1) * 64, 0:klen],
                        waits=[kh_user])
            qrhs = qbuf[:, h * QC:(h + 1) * QC]
            first_oz = True
            t1 = t2 = None
            last_ext = head_war
            for g0 in range(0, ktmax, 4):
                gl = min(4, ktmax - g0)
                smm = []
                for j in range(gl):
                    kt = g0 + j
                    w = [klt, qtoks[h // 2]] + ([sps_user] if j == 0 else [])
                    smm.append(p.mm(sps[:, j * QC:(j + 1) * QC],
                                    khb[:, kt * 128:(kt + 1) * 128], qrhs,
                                    start=True, stop=True, waits=w))
                ext = p.act(ptens[:, g0 * QC:(g0 + gl) * QC],
                            sps[:, 0:gl * QC], AF.Exp, bias=C.sc(-SHIFT),
                            scale=C.sc(0.125), waits=smm + [C.tok_scal, last_ext])
                sps_user = ext
                for j in range(gl):
                    kt = g0 + j
                    r = kt * 128 - qc * QC
                    if 0 <= r < QC:
                        ext = p.tt("vector", ptens[:, kt * QC:(kt + 1) * QC],
                                   ptens[:, kt * QC:(kt + 1) * QC],
                                   dmask[:, (r // 128) * QC:(r // 128 + 1) * QC],
                                   ALU.mult, waits=[ext, dmk])
                for j in range(gl):
                    kt = g0 + j
                    w0 = [ext, vtok[kt]] + ([oz_user] if first_oz else [])
                    t1 = p.mm(ozps[0:64, :] if False else PSUM[0:64, 2 * TC:2 * TC + QC],
                              vt[:, kt * 1024 + h * 64:kt * 1024 + h * 64 + 64],
                              ptens[:, kt * QC:(kt + 1) * QC],
                              start=first_oz, stop=(kt == ktmax - 1), waits=w0)
                    t2 = p.mm(PSUM[64:65, 2 * TC:2 * TC + QC], C.ones_col_bf[:],
                              ptens[:, kt * QC:(kt + 1) * QC],
                              start=first_oz, stop=(kt == ktmax - 1),
                              waits=[ext, C.tok_ones_col_bf])
                    first_oz = False
                last_ext = ext
            e1 = p.act(zrow[:], PSUM[64:65, 2 * TC:2 * TC + QC], AF.Ln,
                       waits=[t2, zpl_reader])
            e2 = p.ts("vector", zrow[:], zrow[:], -1.0, None, ALU.mult, waits=[e1])
            e3 = p.act(zrow[:], zrow[:], AF.Exp, waits=[e2])
            bt = p.mm(zplps, C.ones_row[0:1, 0:64], zrow[:], True, True,
                      waits=[e3, C.tok_ones_row, zpl_wru])
            e4 = p.act(zpl[:], zplps, AF.Identity, waits=[bt, zpl_reader])
            zpl_wru = e4
            e5 = p.act(oev[:], PSUM[0:64, 2 * TC:2 * TC + QC], AF.Identity,
                       waits=[t1] + ([ob_toks[-1]] if ob_toks else []))
            e6 = p.tt("vector", obuf[(h % 2) * 64:(h % 2) * 64 + 64,
                                     (h // 2) * QC:(h // 2) * QC + QC],
                      oev[:], zpl[:], ALU.mult, waits=[e4, e5, ob_user])
            zpl_reader = e6
            oz_user = e6
            kh_user = smm[-1]
            head_war = t2
            ob_toks.append(e6)
        for fo in range(PT):
            pb = fo % 2
            mmt = None
            for kt in range(PT):
                w = [wo_t[-1]] + ob_toks + ([ps_user[pb]] if kt == 0 else [])
                mmt = p.mm(B(4 + pb, QC), wbuf2[kt][:, fo * 128:(fo + 1) * 128],
                           obuf[:, kt * QC:(kt + 1) * QC], start=(kt == 0),
                           stop=(kt == PT - 1), waits=w)
            rt = p.tt("vector", oxb[pb][:], B(4 + pb, QC),
                      xbq[:, fo * QC:(fo + 1) * QC], ALU.add,
                      waits=[mmt, xlt, oxb_user[pb]])
            ps_user[pb] = rt
            oxb_user[pb] = p.dma("sync",
                                 dram["xo"][fo * 128:(fo + 1) * 128, qs],
                                 oxb[pb][:], waits=[rt])
        ob_user = mmt
        xbq_user = rt


def build_l0_attn_a_program(in_dtype="f32"):
    nc = bass.Bass("TRN2", target_bir_lowering=False)
    es = contextlib.ExitStack()
    xin = nc.dram_tensor("xin", [D, T], F32 if in_dtype == "f32" else BF16,
                         kind="ExternalInput").ap()
    names = ["wqT", "wkT", "wvT", "woT"]
    dr = {n: nc.dram_tensor(n, [D, D], F32, kind="ExternalInput").ap()
          for n in names}
    dr["cd_dmask"] = nc.dram_tensor("cd_dmask", [128, 512], BF16,
                                    kind="ExternalInput").ap()
    dr["xo"] = nc.dram_tensor("xo", [D, T], BF16, kind="ExternalOutput").ap()
    dr["s_k"] = nc.dram_tensor("s_k", [D, T], BF16, kind="Internal").ap()
    dr["s_y"] = nc.dram_tensor("s_y", [D, T], BF16, kind="Internal").ap()
    with es:
        p, sems = make_prog(nc, es)
        C = Consts(nc, es, p)
        dr["xin"] = xin
        dr["cast"] = (in_dtype == "f32")
        emit_l0_attn_a(nc, es, p, C, dr, "aa")
        p.add("gpsimd", lambda e: e.engine_nop(), waits=p.all_tokens(), inc=1)
        p.finalize(sems)
    return nc


def emit_l0_attn_b(nc, es, p, C, dram, tag):
    CH = 256
    NC_ = T // CH
    PSUM = ps(nc, es, f"{tag}_psum", [128, 4096])
    B = lambda i, w=TC: PSUM[:, i * TC:i * TC + w]

    xs = XStream(nc, es, p, f"{tag}xs", dram["xin"],
                 q="gpsimd")
    ln = LNChunk(nc, es, p, C, f"{tag}ln", rpl_ps=B(2), mpl_ps=B(3),
                 mps_ap=PSUM[0:1, 0:TC], qps_ap=PSUM[0:1, TC:2 * TC])
    wq = [sb(nc, es, f"{tag}_wq{i}", [128, D], BF16) for i in range(PT)]
    wk = [sb(nc, es, f"{tag}_wk{i}", [128, D], BF16) for i in range(PT)]
    wv = [sb(nc, es, f"{tag}_wv{i}", [128, D], BF16) for i in range(PT)]
    ktt = [sb(nc, es, f"{tag}_kt{i}", [128, T], BF16) for i in range(PT)]
    ybuf = sb(nc, es, f"{tag}_yb", [128, PT * TC], BF16)
    qc_ = [sb(nc, es, f"{tag}_qc{i}", [128, TC], BF16) for i in range(2)]
    vcb = [sb(nc, es, f"{tag}_vc{i}", [128, D], BF16) for i in range(2)]
    bqv, tbq = load_vec_pf(nc, es, p, f"{tag}_bq", dram["bq"], PT)
    bkv, tbk = load_vec_pf(nc, es, p, f"{tag}_bk", dram["bk"], PT)
    bov, tbo = load_vec_pf(nc, es, p, f"{tag}_bo", dram["bo"], PT)
    bvrow = sb(nc, es, f"{tag}_bvr", [1, D], BF16)
    tbv = p.dma("gpsimd", bvrow[:], dram["bv"].rearrange("(o k) -> o k", o=1))

    def load_w(dram_w, bufs, waits=()):
        dv = dram_w.rearrange("(a p) k -> a p k", p=128)
        return [p.dma("gpsimd", bufs[i][:], dv[i], waits=list(waits) or None
                      if False else list(waits)) for i in range(PT)]

    twq = load_w(dram["Wq"], wq)
    twk = load_w(dram["Wk"], wk)
    twv = load_w(dram["Wv"], wv)
    sq_d = dram["s_q"].rearrange("(a p) t -> p a t", p=128)
    sv_d = dram["s_v"]
    xo = dram["xo"]

    ps_user = [None, None]
    qc_user = [None, None]
    vc_user = [None, None, None]
    b1_war = []
    ytoks = {}
    for c in range(NCH):
        bi = xs.load(c)
        ln.stats_and_planes([xs.tile(bi, i) for i in range(PT)])
        last = None
        for i in range(PT):
            xap, xtk = xs.tile(bi, i)
            last = ln.norm(i, xap, xtk, ybuf[:, i * TC:(i + 1) * TC], war=b1_war)
        xs.use(bi, last)
        ytoks[c] = last
        for which, wt, wtk, bias, btok in (("q", wq, twq, bqv, tbq),
                                           ("k", wk, twk, bkv, tbk)):
            for fo in range(PT):
                pb = fo % 2
                mmt = None
                for kt in range(PT):
                    w = [last, wtk[-1]] + ([ps_user[pb]] if kt == 0 else [])
                    mmt = p.mm(B(4 + pb), wt[kt][:, fo * 128:(fo + 1) * 128],
                               ybuf[:, kt * TC:(kt + 1) * TC], start=(kt == 0),
                               stop=(kt == PT - 1), waits=w)
                if which == "q":
                    et = p.act(qc_[pb][:], B(4 + pb), AF.Identity,
                               bias=bias[:, fo:fo + 1],
                               waits=[mmt, btok, qc_user[pb]])
                    qc_user[pb] = p.dma("sync",
                                        dram["s_q"][fo * 128:(fo + 1) * 128,
                                                    c * TC:(c + 1) * TC],
                                        qc_[pb][:], waits=[et])
                    ps_user[pb] = et
                else:
                    et = p.act(ktt[fo][:, c * TC:(c + 1) * TC], B(4 + pb),
                               AF.Identity, bias=bias[:, fo:fo + 1],
                               waits=[mmt, btok])
                    ps_user[pb] = et
        for tsub in range(TC // 128):
            vb = vcb[tsub % 2]
            evts = []
            for half in range(2):
                pb = half
                mmt = None
                for kt in range(PT):
                    w = [last, twv[-1]] + ([vc_user[pb]] if kt == 0 else [])
                    mmt = p.mm(B(6 + pb), ybuf[:, kt * TC + tsub * 128:
                                               kt * TC + tsub * 128 + 128],
                               wv[kt][:, half * TC:(half + 1) * TC],
                               start=(kt == 0), stop=False, waits=w)
                mmt = p.mm(B(6 + pb), C.ones_row_bf[:],
                           bvrow[:, half * TC:(half + 1) * TC],
                           start=False, stop=True,
                           waits=[tbv, C.tok_ones_row_bf])
                et = p.copy("vector", vb[:, half * TC:(half + 1) * TC],
                            B(6 + pb), waits=[mmt, vc_user[2]])
                vc_user[pb] = et
                evts.append(et)
            vc_user[2] = p.dma("sync",
                               sv_d[c * TC + tsub * 128:
                                    c * TC + tsub * 128 + 128, :],
                               vb[:], waits=evts)
        b1_war = [mmt, ps_user[0], ps_user[1]]

    wo = wq
    two = load_w(dram["Wo"], wo, waits=[ps_user[0], ps_user[1]])
    qnb = sb(nc, es, f"{tag}_qnb", [128, PT * CH], BF16)
    vnb = sb(nc, es, f"{tag}_vnb", [128, 2 * D], BF16)
    pacc = sb(nc, es, f"{tag}_pacc", [128, 2 * CH], F32)
    pbf = sb(nc, es, f"{tag}_pbf", [128, 2 * CH], BF16)
    ptb = sb(nc, es, f"{tag}_ptb", [128, 2 * CH], BF16)
    esb = [sb(nc, es, f"{tag}_esb{i}", [128, CH], F32) for i in range(2)]
    zcol = sb(nc, es, f"{tag}_z", [128, 4], F32)
    ostg = sb(nc, es, f"{tag}_ostg", [128, PT * CH], BF16)
    oxb = [sb(nc, es, f"{tag}_oxb{i}", [128, CH], BF16) for i in range(2)]
    xnb = xs.bufs[0][:, :PT * CH]
    phase_v = ("vector", p.counts["vector"])
    phase_s = ("scalar", p.counts["scalar"])
    sps_user = [phase_v, phase_s]
    s_user = {0: [phase_v, phase_s], 1: [phase_v, phase_s]}
    o_user = {0: [phase_v, phase_s], 1: [phase_v, phase_s]}
    t_user = [phase_v]
    esb_user = [None, None]
    pacc_copied = None
    ptb_user = None
    ostg_user = None
    oxb_user = [None, None]
    qn_user = None
    vn_user = None
    xn_user = None
    tpb = PSUM[:, 6 * TC:6 * TC + 128].bitcast(BF16)[:, 0:128]
    for n in range(NC_):
        ns = slice(n * CH, (n + 1) * CH)
        qlt = p.dma("sync", qnb[:].rearrange("p (a k) -> p a k", a=PT),
                    sq_d[:, :, ns], waits=[qn_user, qc_user[0], qc_user[1]])
        vlt = p.dma("sync", vnb[:].rearrange("p (a k) -> p a k", a=2),
                    sv_d.rearrange("(a p) e -> p a e", p=128)[:,
                        2 * n:2 * n + 2, :],
                    waits=[vn_user, vc_user[2]])
        xlt = p.dma("gpsimd", xnb.rearrange("p (a k) -> p a k", a=PT),
                    xs.dram[:, :, ns], waits=[xn_user])
        zt = p.memset("vector", pacc[:], 0.0, waits=[pacc_copied])
        acc_t = {0: zt, 1: zt}
        for mp in range(NC_ // 2):
            for qsub in range(2):
                pb = qsub
                mmt = None
                for et_ in range(PT):
                    w = [qlt] + (s_user[pb] if et_ == 0 else [])
                    mmt = p.mm(B(pb), qnb[:, et_ * CH + qsub * 128:
                                           et_ * CH + qsub * 128 + 128],
                               ktt[et_][:, mp * 512:(mp + 1) * 512],
                               start=(et_ == 0), stop=(et_ == PT - 1), waits=w)
                for half in range(2):
                    m = 2 * mp + half
                    eb = esb[half]
                    ext = p.act(eb[:], PSUM[:, pb * TC + half * CH:
                                            pb * TC + (half + 1) * CH],
                                AF.Exp, bias=C.sc(-SHIFT), scale=C.sc(0.03125),
                                accum_out=zcol[:, half:half + 1],
                                waits=[mmt, C.tok_scal, esb_user[half]])
                    rc = p.recip(zcol[:, 2 + half:3 + half],
                                 zcol[:, half:half + 1], waits=[ext])
                    at = p.stt("vector", pacc[:, qsub * CH:(qsub + 1) * CH],
                               eb[:], zcol[:, 2 + half:3 + half],
                               pacc[:, qsub * CH:(qsub + 1) * CH],
                               ALU.mult, ALU.add, waits=[rc, acc_t[qsub]])
                    if m == n:
                        at = p.stt("vector", pacc[:, qsub * CH:(qsub + 1) * CH],
                                   eb[:], zcol[:, 2 + half:3 + half],
                                   pacc[:, qsub * CH:(qsub + 1) * CH],
                                   ALU.mult, ALU.add, waits=[at])
                    acc_t[qsub] = at
                    esb_user[half] = at
                s_user[pb] = [ext]
        cpt = p.copy("vector", pbf[:], pacc[:], waits=[acc_t[0], acc_t[1],
                                                       ptb_user])
        pacc_copied = cpt
        ttoks = []
        for qsub in range(2):
            for ksub in range(2):
                tt = p.transpose(tpb, pbf[:, qsub * CH + ksub * 128:
                                          qsub * CH + ksub * 128 + 128],
                                 C.ident[:], waits=[cpt, C.tok_ident] + t_user)
                ev = p.copy("vector", ptb[:, ksub * CH + qsub * 128:
                                          ksub * CH + qsub * 128 + 128],
                            tpb, waits=[tt])
                t_user = [ev]
                ttoks.append(ev)
        otoks = []
        for eo in range(PT):
            pb = eo % 2
            mmt = None
            for ksub in range(2):
                w = [vlt] + ttoks + (o_user[pb] if ksub == 0 else [])
                mmt = p.mm(PSUM[:, (2 + pb) * TC:(2 + pb) * TC + CH],
                           vnb[:, ksub * D + eo * 128:ksub * D + eo * 128 + 128],
                           ptb[:, ksub * CH:(ksub + 1) * CH],
                           start=(ksub == 0), stop=(ksub == 1), waits=w)
            ev = p.copy("vector", ostg[:, eo * CH:(eo + 1) * CH],
                        PSUM[:, (2 + pb) * TC:(2 + pb) * TC + CH],
                        waits=[mmt, ostg_user])
            o_user[pb] = [ev]
            otoks.append(ev)
        ptb_user = mmt
        for fo in range(PT):
            pb = fo % 2
            mmt = None
            for kt in range(PT):
                w = [two[-1]] + otoks + ([ps_user[pb]] if kt == 0 else [])
                mmt = p.mm(B(4 + pb, CH), wo[kt][:, fo * 128:(fo + 1) * 128],
                           ostg[:, kt * CH:(kt + 1) * CH], start=(kt == 0),
                           stop=(kt == PT - 1), waits=w)
            et = p.act(esb[pb][:], B(4 + pb, CH), AF.Identity,
                       bias=bov[:, fo:fo + 1], waits=[mmt, tbo, esb_user[pb]])
            ps_user[pb] = et
            rt = p.tt("vector", oxb[pb][:], esb[pb][:],
                      xnb[:, fo * CH:(fo + 1) * CH], ALU.add,
                      waits=[et, xlt, oxb_user[pb]])
            esb_user[pb] = rt
            oxb_user[pb] = p.dma("sync", xo[fo * 128:(fo + 1) * 128, ns],
                                 oxb[pb][:], waits=[rt])
        ostg_user = mmt
        qn_user = mmt
        vn_user = mmt
        xn_user = rt


def build_l0_attn_b_program(in_dtype="bf16"):
    nc = bass.Bass("TRN2", target_bir_lowering=False)
    es = contextlib.ExitStack()
    xin = nc.dram_tensor("xin", [D, T], F32 if in_dtype == "f32" else BF16,
                         kind="ExternalInput").ap()
    dr = {n: nc.dram_tensor(n, [D, D], F32, kind="ExternalInput").ap()
          for n in ["Wq", "Wk", "Wv", "Wo"]}
    for n in ["bq", "bk", "bv"]:
        dr[n] = nc.dram_tensor(n, [D], F32, kind="ExternalInput").ap()
    dr["bq_pf"] = None
    dr["bq"] = dr["bq"]
    dr["bo"] = nc.dram_tensor("bo", [128, PT], F32, kind="ExternalInput").ap()
    dr["xo"] = nc.dram_tensor("xo", [D, T], BF16, kind="ExternalOutput").ap()
    dr["s_q"] = nc.dram_tensor("s_q", [D, T], BF16, kind="Internal").ap()
    dr["s_v"] = nc.dram_tensor("s_v", [T, D], BF16, kind="Internal").ap()
    with es:
        p, sems = make_prog(nc, es)
        C = Consts(nc, es, p)
        dr["xin"] = xin
        dr["cast"] = (in_dtype == "f32")
        emit_l0_attn_b(nc, es, p, C, dr, "ab")
        p.add("gpsimd", lambda e: e.engine_nop(), waits=p.all_tokens(), inc=1)
        p.finalize(sems)
    return nc


LN16 = float(np.log(16.0))


def emit_l1_attn_a(nc, es, p, C, dram, tag):
    PSUM = ps(nc, es, f"{tag}_psum", [128, 4096])
    B = lambda i, w=TC: PSUM[:, i * TC:i * TC + w]
    xs = XStream(nc, es, p, f"{tag}xs", dram["xin"],
                 q="gpsimd")
    ln = LNChunk(nc, es, p, C, f"{tag}ln", rpl_ps=B(2), mpl_ps=B(3),
                 mps_ap=PSUM[0:1, 0:TC], qps_ap=PSUM[0:1, TC:2 * TC])
    wq = [sb(nc, es, f"{tag}_wq{i}", [128, D], BF16) for i in range(PT)]
    wk = [sb(nc, es, f"{tag}_wk{i}", [128, D], BF16) for i in range(PT)]
    wv = [sb(nc, es, f"{tag}_wv{i}", [128, D], BF16) for i in range(PT)]
    omg = sb(nc, es, f"{tag}_om", [64, 256], BF16)
    ybuf = sb(nc, es, f"{tag}_yb", [128, PT * TC], BF16)
    qcb = [sb(nc, es, f"{tag}_qc{i}", [128, TC], BF16) for i in range(2)]
    vcb = [sb(nc, es, f"{tag}_vc{i}", [128, D], BF16) for i in range(2)]

    def load_w(dram_w, bufs, waits=()):
        dv = dram_w.rearrange("(a p) k -> a p k", p=128)
        return [p.dma("gpsimd", bufs[i][:], dv[i], waits=list(waits))
                for i in range(PT)]

    twq = load_w(dram["Wq"], wq)
    twk = load_w(dram["Wk"], wk)
    twv = load_w(dram["Wv"], wv)
    tom = p.dma("gpsimd", omg[:], dram["omegaT"])
    sq_d = dram["s_q"]
    sk_d = dram["s_k"]
    sv_d = dram["s_v"]
    so_d = dram["s_o"]
    xo = dram["xo"]

    ps_user = [None, None]
    eb_user = [None, None]
    vc_user = [None, None, None]
    b1_war = []
    for c in range(NCH):
        bi = xs.load(c)
        ln.stats_and_planes([xs.tile(bi, i) for i in range(PT)])
        last = None
        for i in range(PT):
            xap, xtk = xs.tile(bi, i)
            last = ln.norm(i, xap, xtk, ybuf[:, i * TC:(i + 1) * TC], war=b1_war)
        xs.use(bi, last)
        for which, wt, wtk, sd in (("q", wq, twq, sq_d), ("k", wk, twk, sk_d)):
            for fo in range(PT):
                pb = fo % 2
                mmt = None
                for kt in range(PT):
                    w = [last, wtk[-1]] + ([ps_user[pb]] if kt == 0 else [])
                    mmt = p.mm(B(4 + pb), wt[kt][:, fo * 128:(fo + 1) * 128],
                               ybuf[:, kt * TC:(kt + 1) * TC], start=(kt == 0),
                               stop=(kt == PT - 1), waits=w)
                et = p.act(qcb[pb][:], B(4 + pb), AF.Identity,
                           waits=[mmt, eb_user[pb]])
                ps_user[pb] = et
                eb_user[pb] = p.dma("sync",
                                    sd[fo * 128:(fo + 1) * 128,
                                       c * TC:(c + 1) * TC],
                                    qcb[pb][:], waits=[et])
        for tsub in range(TC // 128):
            vb = vcb[tsub % 2]
            evts = []
            for half in range(2):
                pb = half
                mmt = None
                for kt in range(PT):
                    w = [last, twv[-1]] + ([vc_user[pb]] if kt == 0 else [])
                    mmt = p.mm(B(6 + pb), ybuf[:, kt * TC + tsub * 128:
                                               kt * TC + tsub * 128 + 128],
                               wv[kt][:, half * TC:(half + 1) * TC],
                               start=(kt == 0), stop=(kt == PT - 1), waits=w)
                et = p.copy("vector", vb[:, half * TC:(half + 1) * TC],
                            B(6 + pb), waits=[mmt, vc_user[2]])
                vc_user[pb] = et
                evts.append(et)
            vc_user[2] = p.dma("sync",
                               sv_d[c * TC + tsub * 128:
                                    c * TC + tsub * 128 + 128, :],
                               vb[:], waits=evts)
        b1_war = [mmt, ps_user[0], ps_user[1]]

    qh = sb(nc, es, f"{tag}_qh", [64, T], BF16)
    kh = sb(nc, es, f"{tag}_kh", [64, T], BF16)
    vh = sb(nc, es, f"{tag}_vh", [128, 32 * 64], BF16)
    sqs = sb(nc, es, f"{tag}_sqs", [64, T], F32)
    rowA = sb(nc, es, f"{tag}_rowA", [1, T], F32)
    nkcol = sb(nc, es, f"{tag}_nkc", [128, 32], F32)
    skp = sb(nc, es, f"{tag}_skp", [32, 128], F32)
    pkb = sb(nc, es, f"{tag}_pkb", [128, 32 * 256], BF16)
    pksum = sb(nc, es, f"{tag}_pks", [128, 2], BF16)
    kvb = sb(nc, es, f"{tag}_kvb", [128, 2 * 64], BF16)
    pqb = sb(nc, es, f"{tag}_pqb", [128, 2 * TC], BF16)
    nqr = sb(nc, es, f"{tag}_nqr", [1, TC], F32)
    zr = sb(nc, es, f"{tag}_zr", [1, TC], F32)
    zpl = sb(nc, es, f"{tag}_zpl", [64, TC], F32)
    of32 = sb(nc, es, f"{tag}_of32", [64, TC], F32)
    obf = sb(nc, es, f"{tag}_obf", [64, TC], BF16)
    nhalf = sb(nc, es, f"{tag}_nh", [128, 1], F32)
    t_nh = p.ts("vector", nhalf[:], C.ones_col[:], -0.5, None, ALU.mult,
                waits=[C.tok_ones_col])
    phase_v = ("vector", p.counts["vector"])
    phase_s = ("scalar", p.counts["scalar"])
    guards = [phase_v, phase_s]
    u = {k: guards for k in ("b01", "b2", "b45", "b6", "b7")}
    qh_user = kh_user = vh_user = None
    sqs_user = None
    rowA_user = None
    nk_user = None
    skp_user = None
    pk_user = None
    pq_user = None
    kv_user = None
    pks_user = None
    zr_user = None
    zpl_user = None
    of_user = None
    ob_user = None
    for h in range(16):
        qlt = p.dma("sync", qh[:], sq_d[h * 64:(h + 1) * 64, :],
                    waits=[qh_user, eb_user[0], eb_user[1]])
        klt = p.dma("sync", kh[:], sk_d[h * 64:(h + 1) * 64, :],
                    waits=[kh_user])
        vts = []
        for part in range(8):
            vts.append(p.dma("gpsimd",
                             vh[:, part * 4 * 64:(part + 1) * 4 * 64]
                             .rearrange("p (a k) -> p a k", a=4),
                             sv_d.rearrange("(a p) e -> p a e", p=128)
                             [:, part * 4:(part + 1) * 4,
                              h * 64:(h + 1) * 64],
                             waits=[vh_user, vc_user[2]]))
        sqt = p.act(sqs[:], kh[:], AF.Square, waits=[klt, sqs_user])
        rt = None
        for ch in range(NCH):
            mt = p.mm(PSUM[0:1, 2 * TC:2 * TC + TC], nhalf[0:64, :],
                      sqs[:, ch * TC:(ch + 1) * TC], True, True,
                      waits=[sqt, t_nh, rt] + (u["b2"] if ch == 0 else []))
            rt = p.ts("vector", rowA[:, ch * TC:(ch + 1) * TC],
                      PSUM[0:1, 2 * TC:2 * TC + TC], 1.0, -LN16, ALU.mult,
                      ALU.add, waits=[mt, rowA_user])
        rowA_user = rt
        u["b2"] = [rt]
        bt1 = p.dma("sync", dram["s_row"].rearrange("(o k) -> o k", o=1),
                    rowA[:], waits=[rt])
        bt2 = p.dma("sync", skp[:],
                    dram["s_row"].rearrange("(k p) -> k p", p=128),
                    waits=[bt1, skp_user])
        ttr = p.transpose(PSUM[:, 2 * TC:2 * TC + 32], skp[:],
                          C.identF[0:32, 0:32], waits=[bt2, C.tok_identF] +
                          u["b2"])
        bt2 = p.act(nkcol[:], PSUM[:, 2 * TC:2 * TC + 32], AF.Identity,
                    waits=[ttr, nk_user])
        skp_user = ttr
        u["b2"] = [bt2]
        pktoks = []
        for tt_ in range(32):
            pb = tt_ % 2
            w = [klt, tom] + (u["b01"] if tt_ < 2 else [])
            mt = p.mm(PSUM[:, pb * TC:pb * TC + 256],
                      kh[:, tt_ * 128:(tt_ + 1) * 128], omg[:], True, True,
                      waits=w + ([pktoks[-2]] if len(pktoks) >= 2 else []))
            et = p.act(pkb[:, tt_ * 256:(tt_ + 1) * 256],
                       PSUM[:, pb * TC:pb * TC + 256], AF.Exp,
                       bias=nkcol[:, tt_:tt_ + 1], waits=[mt, bt2, pk_user])
            pktoks.append(et)
        u["b01"] = [pktoks[-1], pktoks[-2]]
        for mh in range(2):
            mmt = None
            for tt_ in range(32):
                w = [pktoks[tt_], C.tok_ones_col_bf] + \
                    (u["b2"] if tt_ == 0 else [])
                mmt = p.mm(PSUM[:, 2 * TC:2 * TC + 1],
                           pkb[:, tt_ * 256 + mh * 128:tt_ * 256 + mh * 128 + 128],
                           C.ones_col_bf[:], start=(tt_ == 0), stop=(tt_ == 31),
                           waits=w)
            e1 = p.copy("vector", pksum[:, mh:mh + 1], PSUM[:, 2 * TC:2 * TC + 1],
                        waits=[mmt, pks_user])
            u["b2"] = [e1]
            pks_user = e1
            mmt = None
            for tt_ in range(32):
                w = [pktoks[tt_], vts[tt_ // 4]] + \
                    (u["b6"] if tt_ == 0 else [])
                mmt = p.mm(PSUM[:, 6 * TC:6 * TC + 64],
                           pkb[:, tt_ * 256 + mh * 128:tt_ * 256 + mh * 128 + 128],
                           vh[:, tt_ * 64:(tt_ + 1) * 64], start=(tt_ == 0),
                           stop=(tt_ == 31), waits=w)
            e2 = p.copy("vector", kvb[:, mh * 64:(mh + 1) * 64],
                        PSUM[:, 6 * TC:6 * TC + 64], waits=[mmt, kv_user])
            u["b6"] = [e2]
            kv_user = e2
        kv_done = kv_user
        pks_done = pks_user
        for ch in range(NCH):
            cs = slice(ch * TC, (ch + 1) * TC)
            sqt2 = p.act(sqs[:, 0:TC], qh[:, cs], AF.Square,
                         waits=[qlt, sqt if ch == 0 else None, sqs_user,
                                rowA_user])
            sqs_user = sqt2
            mt = p.mm(PSUM[0:1, 2 * TC:2 * TC + TC], nhalf[0:64, :],
                      sqs[:, 0:TC], True, True, waits=[sqt2, t_nh] + u["b2"])
            nt = p.ts("vector", nqr[:], PSUM[0:1, 2 * TC:2 * TC + TC], 1.0,
                      -LN16, ALU.mult, ALU.add, waits=[mt, zr_user])
            u["b2"] = [nt]
            pqtoks = []
            for mt_ in range(2):
                pb = 4 + mt_
                w = [qlt, tom] + u["b45"]
                m1 = p.mm(B(pb), omg[:, mt_ * 128:(mt_ + 1) * 128],
                          qh[:, cs], start=True, stop=False, waits=w)
                m2 = p.mm(B(pb), C.ones_row[:], nqr[:], start=False, stop=True,
                          waits=[nt, C.tok_ones_row])
                et = p.act(pqb[:, mt_ * TC:(mt_ + 1) * TC], B(pb), AF.Exp,
                           waits=[m2, pq_user])
                pqtoks.append(et)
            pq_user = None
            u["b45"] = pqtoks
            mmt = None
            for mt_ in range(2):
                w = [pqtoks[mt_], pks_done] + (u["b7"] if mt_ == 0 else [])
                mmt = p.mm(PSUM[0:1, 7 * TC:7 * TC + TC],
                           pksum[:, mt_:mt_ + 1], pqb[:, mt_ * TC:(mt_ + 1) * TC],
                           start=(mt_ == 0), stop=(mt_ == 1), waits=w)
            z1 = p.ts("vector", zr[:], PSUM[0:1, 7 * TC:7 * TC + TC], 1.0,
                      1e-6, ALU.mult, ALU.add, waits=[mmt, zr_user])
            u["b7"] = [z1]
            z2 = p.act(zr[:], zr[:], AF.Ln, waits=[z1])
            z3 = p.ts("vector", zr[:], zr[:], -1.0, None, ALU.mult, waits=[z2])
            z4 = p.act(zr[:], zr[:], AF.Exp, waits=[z3])
            zr_user = z4
            mmt = None
            for mt_ in range(2):
                w = [pqtoks[mt_], kv_done] + (u["b7"] if mt_ == 0 else [])
                mmt = p.mm(PSUM[0:64, 7 * TC:7 * TC + TC],
                           kvb[:, mt_ * 64:(mt_ + 1) * 64],
                           pqb[:, mt_ * TC:(mt_ + 1) * TC],
                           start=(mt_ == 0), stop=(mt_ == 1), waits=w)
            u["b7"] = [mmt]
            bt = p.mm(PSUM[0:64, 6 * TC:6 * TC + TC], C.ones_row[0:1, 0:64],
                      zr[:], True, True, waits=[z4, C.tok_ones_row] + u["b6"])
            e1 = p.act(zpl[:], PSUM[0:64, 6 * TC:6 * TC + TC], AF.Identity,
                       waits=[bt, zpl_user])
            u["b6"] = [e1]
            e2 = p.act(of32[:], PSUM[0:64, 7 * TC:7 * TC + TC], AF.Identity,
                       waits=[mmt, of_user])
            u["b7"] = [e2]
            e3 = p.tt("vector", obf[:], of32[:], zpl[:], ALU.mult,
                      waits=[e1, e2, ob_user])
            zpl_user = e3
            of_user = e3
            ob_user = p.dma("sync", so_d[h * 64:(h + 1) * 64, cs], obf[:],
                            waits=[e3])
        qh_user = pqtoks[-1]
        kh_user = pktoks[-1]
        vh_user = kv_done
        nk_user = pktoks[-1]
        pk_user = kv_done
    out_toks = []
    oxb = [sb(nc, es, f"{tag}_oxbf{i}", [128, TC], BF16) for i in range(2)]
    sob = ybuf
    oxb_user = [None, None]
    sob_user = [ob_user]
    for c in range(NCH):
        cs = slice(c * TC, (c + 1) * TC)
        bi = xs.load(c)
        slt = p.dma("sync", sob[:].rearrange("p (a k) -> p a k", a=PT),
                    so_d.rearrange("(a p) t -> p a t", p=128)[:, :, cs],
                    waits=sob_user)
        sob_user = []
        for i in range(PT):
            xap, xtk = xs.tile(bi, i)
            pb = i % 2
            rt = p.tt("vector", oxb[pb][:], sob[:, i * TC:(i + 1) * TC], xap,
                      ALU.add, waits=[slt, xtk, oxb_user[pb]])
            oxb_user[pb] = p.dma("sync", xo[i * 128:(i + 1) * 128, cs],
                                 oxb[pb][:], waits=[rt])
            sob_user.append(rt)
        xs.use(bi, rt)


def build_l1_attn_a_program(in_dtype="bf16"):
    nc = bass.Bass("TRN2", target_bir_lowering=False)
    es = contextlib.ExitStack()
    xin = nc.dram_tensor("xin", [D, T], F32 if in_dtype == "f32" else BF16,
                         kind="ExternalInput").ap()
    dr = {n: nc.dram_tensor(n, [D, D], F32, kind="ExternalInput").ap()
          for n in ["Wq", "Wk", "Wv"]}
    dr["omegaT"] = nc.dram_tensor("omegaT", [64, 256], F32,
                                  kind="ExternalInput").ap()
    dr["xo"] = nc.dram_tensor("xo", [D, T], BF16, kind="ExternalOutput").ap()
    for n in ["s_q", "s_k", "s_o"]:
        dr[n] = nc.dram_tensor(n, [D, T], BF16, kind="Internal").ap()
    dr["s_v"] = nc.dram_tensor("s_v", [T, D], BF16, kind="Internal").ap()
    dr["s_row"] = nc.dram_tensor("s_row", [T], F32, kind="Internal").ap()
    with es:
        p, sems = make_prog(nc, es)
        C = Consts(nc, es, p)
        dr["xin"] = xin
        dr["cast"] = (in_dtype == "f32")
        emit_l1_attn_a(nc, es, p, C, dr, "pa")
        p.add("gpsimd", lambda e: e.engine_nop(), waits=p.all_tokens(), inc=1)
        p.finalize(sems)
    return nc




def emit_l1_attn_b(nc, es, p, C, dram, tag):
    IT = 16
    PSUM = ps(nc, es, f"{tag}_psum", [128, 4096])
    B = lambda i, w=TC: PSUM[:, i * TC:i * TC + w]
    xs = XStream(nc, es, p, f"{tag}xs", dram["xin"],
                 q="gpsimd")
    wi = [sb(nc, es, f"{tag}_wi{i}", [128, 2 * D], BF16) for i in range(PT)]
    wi2 = [sb(nc, es, f"{tag}_wi2{i}", [128, 2 * D], BF16) for i in range(PT)]
    hbuf = sb(nc, es, f"{tag}_h", [128, IT * TC], BF16)
    sgb = sb(nc, es, f"{tag}_sg", [128, IT * TC], BF16)
    twi = [p.dma("gpsimd", wi[i][:],
                 dram["W_in"].rearrange("(a p) k -> a p k", p=128)[i][:, 0:2 * D])
           for i in range(PT)]
    twi2 = [p.dma("gpsimd", wi2[i][:],
                  dram["W_in"].rearrange("(a p) k -> a p k", p=128)[i][:, 2 * D:])
            for i in range(PT)]
    sh_d = dram["s_h"].rearrange("(a p) t -> p a t", p=128)
    sg_d = dram["s_g"].rearrange("(a p) t -> p a t", p=128)

    es1 = contextlib.ExitStack()
    ln = LNChunk(nc, es1, p, C, f"{tag}ln", rpl_ps=B(2), mpl_ps=B(3),
                 mps_ap=PSUM[0:1, 0:TC], qps_ap=PSUM[0:1, TC:2 * TC])
    ybuf = sb(nc, es1, f"{tag}_yb", [128, PT * TC], BF16)
    cw, tcw = load_vec_pf(nc, es1, p, f"{tag}_cw", dram["conv_w_pf"], IT * 3)
    cb, tcb = load_vec_pf(nc, es1, p, f"{tag}_cb", dram["conv_b"], IT)
    inb = [sb(nc, es1, f"{tag}_in{i}", [128, IT * TC], BF16) for i in range(2)]
    cscr = [sb(nc, es1, f"{tag}_cs{i}", [128, TC], F32) for i in range(2)]
    halo_p = sb(nc, es1, f"{tag}_hp", [128, IT], F32)
    halo_n = sb(nc, es1, f"{tag}_hn", [128, IT], F32)
    hp_t = p.memset("vector", halo_p[:], 0.0)
    hn_t0 = p.memset("vector", halo_n[:], 0.0)
    st = {"sh": None, "sg": None, "hp": hp_t, "hw": None, "hts": [None] * IT}

    def conv_emit(c, buf, etoks, hn_tok):
        hts = []
        for it in range(IT):
            s = cscr[it % 2]
            iap = buf[:, it * TC:(it + 1) * TC]
            w0c = cw[:, it * 3 + 0:it * 3 + 1]
            w1c = cw[:, it * 3 + 1:it * 3 + 2]
            w2c = cw[:, it * 3 + 2:it * 3 + 3]
            t1 = p.ts("vector", s[:], iap, w1c, None, ALU.mult,
                      waits=[etoks[it], tcw, hn_tok,
                             hts[-2] if len(hts) >= 2 else None])
            t2 = p.stt("vector", s[:, 1:TC], iap[:, 0:TC - 1], w0c, s[:, 1:TC],
                       ALU.mult, ALU.add, waits=[t1])
            t3 = p.stt("vector", s[:, 0:1], halo_p[:, it:it + 1], w0c,
                       s[:, 0:1], ALU.mult, ALU.add, waits=[t2, st["hp"]])
            t4 = p.stt("vector", s[:, 0:TC - 1], iap[:, 1:TC], w2c,
                       s[:, 0:TC - 1], ALU.mult, ALU.add, waits=[t3])
            t5 = p.stt("vector", s[:, TC - 1:TC], halo_n[:, it:it + 1], w2c,
                       s[:, TC - 1:TC], ALU.mult, ALU.add, waits=[t4])
            ht = p.act(hbuf[:, it * TC:(it + 1) * TC], s[:], AF.Silu,
                       bias=cb[:, it:it + 1], waits=[t5, tcb, st["sh"]])
            hts.append(ht)
        st["hp"] = p.copy("vector", halo_p[:],
                          buf[:].rearrange("p (a k) -> p a k", a=IT)[:, :, TC - 1],
                          waits=etoks + [hts[-1]])
        st["hw"] = st["hp"]
        st["sh"] = p.dma("sync", sh_d[:, :, c * TC:(c + 1) * TC],
                         hbuf[:].rearrange("p (a k) -> p a k", a=IT), waits=hts)
        st["hts"] = hts

    ps_user = [None, None]
    buf_prev = etoks_prev = None
    for c in range(NCH):
        buf = inb[c % 2]
        bi = xs.load(c)
        ln.stats_and_planes([xs.tile(bi, i) for i in range(PT)])
        last = None
        for i in range(PT):
            xap, xtk = xs.tile(bi, i)
            last = ln.norm(i, xap, xtk, ybuf[:, i * TC:(i + 1) * TC],
                           war=[st["hw"], ps_user[0], ps_user[1]])
        xs.use(bi, last)
        etoks = []
        for fo in range(IT):
            pb = fo % 2
            mmt = None
            for kt in range(PT):
                w = [last, twi[-1]] + ([ps_user[pb]] if kt == 0 else [])
                mmt = p.mm(B(4 + pb), wi[kt][:, fo * 128:(fo + 1) * 128],
                           ybuf[:, kt * TC:(kt + 1) * TC], start=(kt == 0),
                           stop=(kt == PT - 1), waits=w)
            et = p.copy("vector", buf[:, fo * TC:(fo + 1) * TC], B(4 + pb),
                        waits=[mmt, st["hts"][fo] if c >= 2 else None])
            ps_user[pb] = et
            etoks.append(et)
        gts = []
        for fo in range(IT):
            pb = fo % 2
            mmt = None
            for kt in range(PT):
                w = [last, twi2[-1]] + ([ps_user[pb]] if kt == 0 else [])
                mmt = p.mm(B(4 + pb), wi2[kt][:, fo * 128:(fo + 1) * 128],
                           ybuf[:, kt * TC:(kt + 1) * TC], start=(kt == 0),
                           stop=(kt == PT - 1), waits=w)
            et = p.act(sgb[:, fo * TC:(fo + 1) * TC], B(4 + pb), AF.Silu,
                       waits=[mmt, st["sg"]])
            ps_user[pb] = et
            gts.append(et)
        st["sg"] = p.dma("sync", sg_d[:, :, c * TC:(c + 1) * TC],
                         sgb[:].rearrange("p (a k) -> p a k", a=IT), waits=gts)
        if c > 0:
            hn = p.copy("vector", halo_n[:], buf[:].rearrange(
                "p (a k) -> p a k", a=IT)[:, :, 0], waits=etoks + [st["hw"]])
            conv_emit(c - 1, buf_prev, etoks_prev, hn)
        buf_prev, etoks_prev = buf, etoks
    hn2 = p.memset("vector", halo_n[:], 0.0, waits=etoks_prev + [st["hw"]])
    conv_emit(NCH - 1, buf_prev, etoks_prev, hn2)
    es1.close()
    p.barrier()

    wt = wi + wi2
    twt = [p.dma("gpsimd", wt[i][:],
                 dram["W_time"].rearrange("(a p) k -> a p k", p=128)[i])
           for i in range(IT)]
    wst = sb(nc, es, f"{tag}_wst", [128, 16 * 32], BF16)
    twst = p.dma("gpsimd", wst[:], dram["W_state_pf"])
    wob = [sb(nc, es, f"{tag}_wob{i}", [128, D], BF16) for i in range(2)]
    wo_d = dram["W_out"].rearrange("(a p) k -> a p k", p=128)
    bsel = sb(nc, es, f"{tag}_bsel", [32, 16 * 128], BF16)
    tbsel = p.dma("sync", bsel[:], dram["cd_bsel"])
    alog, talog = load_vec_pf(nc, es, p, f"{tag}_al", dram["A_log_pf"], 256)
    btv, tbt = load_vec_pf(nc, es, p, f"{tag}_bt", dram["b_time"], IT)
    skv, tskv = load_vec_pf(nc, es, p, f"{tag}_sk", dram["skip"], IT)
    A_sb = sb(nc, es, f"{tag}_A", [128, 256], F32)
    ta1 = p.act(A_sb[:], alog[:], AF.Exp, waits=[talog])
    ta2 = p.ts("vector", A_sb[:], A_sb[:], -1.0, None, ALU.mult, waits=[ta1])
    carry = sb(nc, es, f"{tag}_carry", [128, 256], F32)
    tc0 = p.memset("vector", carry[:], 0.0)
    hb2 = hbuf
    gb2 = sgb
    HT = 8
    dtb = sb(nc, es, f"{tag}_dt", [128, HT * TC], F32)
    dhb = sb(nc, es, f"{tag}_dh", [128, HT * TC], BF16)
    yac = sb(nc, es, f"{tag}_yac", [128, HT * TC], F32)
    ygb = sb(nc, es, f"{tag}_yg", [128, IT * TC], BF16)
    spsb = sb(nc, es, f"{tag}_spsb", [32, TC], BF16)
    bpl = [sb(nc, es, f"{tag}_bpl{i}", [128, TC], F32) for i in range(2)]
    dab = [sb(nc, es, f"{tag}_da{i}", [128, TC], F32) for i in range(2)]
    dbb = [sb(nc, es, f"{tag}_db{i}", [128, TC], F32) for i in range(2)]
    mem = [sb(nc, es, f"{tag}_mem{i}", [128, TC], F32) for i in range(2)]
    expd = sb(nc, es, f"{tag}_expd", [128, TC], F32)
    oxb = [sb(nc, es, f"{tag}_oxb{i}", [128, TC], BF16) for i in range(2)]
    xo = dram["xo"]
    u = {i: [] for i in range(8)}
    h_user = None
    g_user = None
    sp_user = None
    bpl_user = [None, None]
    da_user = [None, None]
    db_user = [None, None]
    mem_user = [None, None]
    yg_user = [None] * IT
    exp_user = None
    oxb_user = [None, None]
    wob_user = [None, None]
    carry_rd = tc0
    for c in range(NCH):
        cs = slice(c * TC, (c + 1) * TC)
        hlt = p.dma("sync", hb2[:].rearrange("p (a k) -> p a k", a=IT),
                    sh_d[:, :, cs], waits=[h_user])
        glt = p.dma("sync", gb2[:].rearrange("p (a k) -> p a k", a=IT),
                    sg_d[:, :, cs], waits=[g_user])
        xbi = xs.load(c)
        mmt = None
        for kt in range(IT):
            w = [hlt, twst] + (u[6] if kt == 0 else [])
            mmt = p.mm(PSUM[0:32, 6 * TC:6 * TC + TC],
                       wst[:, kt * 32:(kt + 1) * 32],
                       hb2[:, kt * TC:(kt + 1) * TC], start=(kt == 0),
                       stop=(kt == IT - 1), waits=w)
        spt = p.copy("vector", spsb[:], PSUM[0:32, 6 * TC:6 * TC + TC],
                     waits=[mmt, sp_user])
        u[6] = [spt]
        for half in range(2):
            dt_user = [None] * HT
            dh_user = [None] * HT
            yac_user = [None] * HT
            for f_ in range(HT):
                fo = half * HT + f_
                pb = fo % 2
                mmt = None
                for kt in range(IT):
                    w = [hlt, twt[-1]] + (u[4 + pb] if kt == 0 else [])
                    mmt = p.mm(B(4 + pb), wt[kt][:, fo * 128:(fo + 1) * 128],
                               hb2[:, kt * TC:(kt + 1) * TC], start=(kt == 0),
                               stop=(kt == IT - 1), waits=w)
                e1 = p.act(expd[:], B(4 + pb), AF.Exp, bias=btv[:, fo:fo + 1],
                           waits=[mmt, tbt, exp_user])
                u[4 + pb] = [e1]
                e2 = p.act(dtb[:, f_ * TC:(f_ + 1) * TC], expd[:], AF.Ln,
                           bias=C.sc(1.0),
                           waits=[e1, C.tok_scal, yg_user[fo]])
                exp_user = e2
                dt_user[f_] = e2
                e3 = p.tt("vector", dhb[:, f_ * TC:(f_ + 1) * TC],
                          dtb[:, f_ * TC:(f_ + 1) * TC],
                          hb2[:, f_ * TC:(f_ + 1) * TC][:, :] if False else
                          hb2[:, fo * TC:(fo + 1) * TC], ALU.mult,
                          waits=[e2, yg_user[fo]])
                dh_user[f_] = e3
            for n in range(16):
                nb = n % 2
                bt_ = p.mm(PSUM[:, (6 + nb) * TC:(7 + nb) * TC],
                           bsel[:, n * 128:(n + 1) * 128], spsb[:], True, True,
                           waits=[spt, tbsel] + u[6 + nb])
                bev = p.act(bpl[nb][:], PSUM[:, (6 + nb) * TC:(7 + nb) * TC],
                            AF.Identity, waits=[bt_, bpl_user[nb]])
                u[6 + nb] = [bev]
                bpl_user[nb] = None
                for f_ in range(HT):
                    fo = half * HT + f_
                    ib = f_ % 2
                    da_t = p.act(dab[ib][:], dtb[:, f_ * TC:(f_ + 1) * TC],
                                 AF.Exp,
                                 scale=A_sb[:, fo * 16 + n:fo * 16 + n + 1],
                                 waits=[dt_user[f_], ta2, da_user[ib]])
                    db_t = p.tt("vector", dbb[ib][:],
                                dhb[:, f_ * TC:(f_ + 1) * TC], bpl[nb][:],
                                ALU.mult, waits=[dh_user[f_], bev, db_user[ib]])
                    sc_t = p.scan("vector", mem[ib][:], dab[ib][:], dbb[ib][:],
                                  carry[:, fo * 16 + n:fo * 16 + n + 1],
                                  waits=[da_t, db_t, mem_user[ib], carry_rd])
                    da_user[ib] = sc_t
                    db_user[ib] = sc_t
                    cu = p.copy("vector",
                                carry[:, fo * 16 + n:fo * 16 + n + 1],
                                mem[ib][:, TC - 1:TC], waits=[sc_t])
                    carry_rd = cu
                    if n == 0:
                        ya = p.stt("vector", yac[:, f_ * TC:(f_ + 1) * TC],
                                   hb2[:, fo * TC:(fo + 1) * TC],
                                   skv[:, fo:fo + 1], mem[ib][:], ALU.mult,
                                   ALU.add, waits=[sc_t, tskv, hlt,
                                                   yac_user[f_], yg_user[fo]])
                    else:
                        ya = p.tt("vector", yac[:, f_ * TC:(f_ + 1) * TC],
                                  yac[:, f_ * TC:(f_ + 1) * TC], mem[ib][:],
                                  ALU.add, waits=[sc_t, yac_user[f_]])
                    yac_user[f_] = ya
                    mem_user[ib] = ya
                    bpl_user[nb] = db_t
            for f_ in range(HT):
                fo = half * HT + f_
                t = p.tt("vector", ygb[:, fo * TC:(fo + 1) * TC],
                         yac[:, f_ * TC:(f_ + 1) * TC],
                         gb2[:, fo * TC:(fo + 1) * TC], ALU.mult,
                         waits=[yac_user[f_], glt, yg_user[fo]])
                yg_user[fo] = t
        out_mm = [None] * PT
        for kt in range(IT):
            wb = wob[kt % 2]
            wlt = p.dma("gpsimd", wb[:], wo_d[kt], waits=[wob_user[kt % 2]])
            for fo in range(PT):
                w = [wlt, yg_user[kt]]
                if kt == 0:
                    w += u[fo] + [("vector", p.counts["vector"])] if False \
                        else u[fo]
                out_mm[fo] = p.mm(B(fo), wb[:, fo * 128:(fo + 1) * 128],
                                  ygb[:, kt * TC:(kt + 1) * TC],
                                  start=(kt == 0), stop=(kt == IT - 1),
                                  waits=w)
            wob_user[kt % 2] = out_mm[-1]
        for fo in range(PT):
            pb = fo % 2
            xap, xtk = xs.tile(xbi, fo)
            rt = p.tt("vector", oxb[pb][:], B(fo), xap, ALU.add,
                      waits=[out_mm[fo], xtk, oxb_user[pb]])
            u[fo] = [rt]
            oxb_user[pb] = p.dma("sync", xo[fo * 128:(fo + 1) * 128, cs],
                                 oxb[pb][:], waits=[rt])
        xs.use(xbi, rt)
        h_user = out_mm[-1]
        g_user = out_mm[-1]
        for fo in range(IT):
            yg_user[fo] = out_mm[-1]
        sp_user = bt_


def build_l1_attn_b_program(in_dtype="bf16"):
    nc = bass.Bass("TRN2", target_bir_lowering=False)
    es = contextlib.ExitStack()
    xin = nc.dram_tensor("xin", [D, T], F32 if in_dtype == "f32" else BF16,
                         kind="ExternalInput").ap()
    dr = {}
    dr["W_in"] = nc.dram_tensor("W_in", [D, 4 * D], F32, kind="ExternalInput").ap()
    dr["W_time"] = nc.dram_tensor("W_time", [2 * D, 2 * D], F32,
                                  kind="ExternalInput").ap()
    dr["W_state_pf"] = nc.dram_tensor("W_state_pf", [128, 512], F32,
                                      kind="ExternalInput").ap()
    dr["W_out"] = nc.dram_tensor("W_out", [2 * D, D], F32,
                                 kind="ExternalInput").ap()
    for n, sz in [("conv_w_pf", 2 * D * 3), ("conv_b", 2 * D),
                  ("b_time", 2 * D), ("skip", 2 * D), ("A_log_pf", 2 * D * 16)]:
        dr[n] = nc.dram_tensor(n, [128, sz // 128], F32,
                               kind="ExternalInput").ap()
    dr["cd_bsel"] = nc.dram_tensor("cd_bsel", [32, 2048], BF16,
                                   kind="ExternalInput").ap()
    dr["xo"] = nc.dram_tensor("xo", [D, T], BF16, kind="ExternalOutput").ap()
    dr["s_h"] = nc.dram_tensor("s_h", [2 * D, T], BF16, kind="Internal").ap()
    dr["s_g"] = nc.dram_tensor("s_g", [2 * D, T], BF16, kind="Internal").ap()
    with es:
        p, sems = make_prog(nc, es)
        C = Consts(nc, es, p)
        dr["xin"] = xin
        dr["cast"] = (in_dtype == "f32")
        emit_l1_attn_b(nc, es, p, C, dr, "sb")
        p.add("gpsimd", lambda e: e.engine_nop(), waits=p.all_tokens(), inc=1)
        p.finalize(sems)
    return nc


FFN_PHASES = 3


def emit_l0_ffn(nc, es, p, C, dram, tag):
    PSUM = ps(nc, es, f"{tag}_psum", [128, 4096])
    B = lambda i, w=TC: PSUM[:, i * TC:i * TC + w]
    xs = XStream(nc, es, p, f"{tag}xs", dram["xin"],
                 q="gpsimd")
    iwv, t_iwv = load_vec_pf(nc, es, p, f"{tag}_iw",
                             dram["integration_weight"], PT)
    es_w = contextlib.ExitStack()

    def ldw(name, nt, width, es=es_w):
        t = [sb(nc, es, f"{tag}_{name}{i}", [128, width], BF16)
             for i in range(nt)]
        dv = dram[name].rearrange("(a p) k -> a p k", p=128)
        tk = None
        for i in range(nt):
            tk = p.dma("gpsimd", t[i][:], dv[i])
        return t, tk

    rw1, t_rw1 = ldw("router_w1", PT, 256)
    hw1, t_hw1 = ldw("halt_w1", PT, 256)
    sw1, t_sw1 = ldw("shared_w1", 16, 64)
    w1a, t_w1a = ldw("W1_all", 16, 512)
    w2a, t_w2a = ldw("W2_all", 4, 3072)
    rw2, t_rw2 = ldw("router_w2", 2, 8)
    hw2, t_hw2 = ldw("halt_w2", 2, 1)
    sw2 = sb(nc, es_w, f"{tag}_sw2", [64, 3072], BF16)
    t_sw2 = p.dma("gpsimd", sw2[:], dram["shared_w2"])
    b2a = sb(nc, es_w, f"{tag}_b2a", [8, 3072], BF16)
    t_b2a = p.dma("gpsimd", b2a[:], dram["expert_b2"])
    epat = sb(nc, es_w, f"{tag}_epat", [8, 4 * 128], BF16)
    t_epat = p.dma("sync", epat[:], dram["cd_epat"])
    rb2r = sb(nc, es_w, f"{tag}_rb2r", [1, 8], BF16)
    t_rb2r = p.dma("gpsimd", rb2r[:],
                   dram["router_b2"].rearrange("(o k) -> o k", o=1))
    rb1, t_rb1 = load_vec_pf(nc, es_w, p, f"{tag}_rb1", dram["router_b1"], 2)
    hb1, t_hb1 = load_vec_pf(nc, es_w, p, f"{tag}_hb1", dram["halt_b1"], 2)
    b1a, t_b1a = load_vec_pf(nc, es_w, p, f"{tag}_b1a", dram["expert_b1"], 4)
    sb2v, t_sb2v = load_vec_pf(nc, es_w, p, f"{tag}_sb2", dram["shared_b2"], 24)
    muv, t_muv = load_vec_pf(nc, es_w, p, f"{tag}_mu", dram["mu"], PT)
    sb1c = sb(nc, es_w, f"{tag}_sb1c", [128, 1], F32)
    t_sb1c = p.dma("sync", sb1c[0:64, :],
                   dram["shared_b1"].rearrange("(k o) -> k o", o=1))
    hb2c = sb(nc, es_w, f"{tag}_hb2c", [1, 1], F32)
    t_hb2c = p.dma("sync", hb2c[:],
                   dram["halt_b2"].rearrange("(o k) -> o k", o=1))
    swsc = sb(nc, es_w, f"{tag}_swsc", [1, 1], F32)
    t_swsc = p.dma("sync", swsc[:],
                   dram["shared_weight"].rearrange("(o k) -> o k", o=1))
    swrow = sb(nc, es_w, f"{tag}_swrow", [1, 1], F32)
    t_sig = p.act(swrow[:], swsc[:], AF.Sigmoid, waits=[t_swsc])
    swcol = sb(nc, es_w, f"{tag}_swcol", [128, 1], F32)
    t_swp = p.mm(PSUM[:, 7 * TC:7 * TC + 1], C.ones_row[:], swrow[:], True,
                 True, waits=[t_sig, C.tok_ones_row])
    t_swc = p.act(swcol[:], PSUM[:, 7 * TC:7 * TC + 1], AF.Identity,
                  waits=[t_swp])
    maskT = [sb(nc, es_w, f"{tag}_mT{j}", [8, T], BF16) for j in range(2)]
    wrow = [sb(nc, es_w, f"{tag}_wr{j}", [1, T], BF16) for j in range(2)]
    sy = dram["s_y"].rearrange("(a p) t -> p a t", p=128)
    si = dram["s_i"].rearrange("(a p) t -> p a t", p=128)

    es_r = contextlib.ExitStack()
    ln = LNChunk(nc, es_r, p, C, f"{tag}ln", rpl_ps=B(2), mpl_ps=B(3),
                 mps_ap=PSUM[0:1, 0:TC], qps_ap=PSUM[0:1, TC:2 * TC])
    ybuf = sb(nc, es_r, f"{tag}_yb", [128, PT * TC], BF16)
    r1b = sb(nc, es_r, f"{tag}_r1b", [128, 2 * TC], BF16)
    pr8 = sb(nc, es_r, f"{tag}_pr8", [128, 8], F32)
    sc8 = sb(nc, es_r, f"{tag}_sc8", [128, 10], F32)
    mk8 = [sb(nc, es_r, f"{tag}_mk8{j}", [128, 8], F32) for j in range(2)]
    mtp = PSUM[0:8, 6 * TC + 256:6 * TC + 384]
    wtp = PSUM[0:1, 6 * TC + 384:6 * TC + 512]
    ps_user = [None, None]
    r1_user = [None, None]
    l8_user = None
    pr_user = None
    mt_user = None
    b1_war = []
    for c in range(NCH):
        bi = xs.load(c)
        ln.stats_and_planes([xs.tile(bi, i) for i in range(PT)])
        last = None
        for i in range(PT):
            xap, xtk = xs.tile(bi, i)
            last = ln.norm(i, xap, xtk, ybuf[:, i * TC:(i + 1) * TC],
                           war=b1_war)
        xs.use(bi, last)
        spt = p.dma("sync", sy[:, :, c * TC:(c + 1) * TC],
                    ybuf[:].rearrange("p (a k) -> p a k", a=PT), waits=[last])
        for fo in range(2):
            pb = fo % 2
            mmt = None
            for kt in range(PT):
                w = [last, t_rw1] + ([ps_user[pb]] if kt == 0 else [])
                mmt = p.mm(B(4 + pb), rw1[kt][:, fo * 128:(fo + 1) * 128],
                           ybuf[:, kt * TC:(kt + 1) * TC], start=(kt == 0),
                           stop=(kt == PT - 1), waits=w)
            et = p.act(r1b[:, fo * TC:(fo + 1) * TC], B(4 + pb), AF.Gelu,
                       bias=rb1[:, fo:fo + 1], waits=[mmt, t_rb1, r1_user[pb]])
            ps_user[pb] = et
            r1_user[pb] = et
        b1_war = [mmt, spt]
        for tsub in range(4):
            mm1 = None
            for kt in range(2):
                w = [r1_user[kt], t_rw2] +                     ([l8_user, mt_user] if kt == 0 else [])
                mm1 = p.mm(PSUM[:, 6 * TC:6 * TC + 8],
                           r1b[:, kt * TC + tsub * 128:
                               kt * TC + tsub * 128 + 128],
                           rw2[kt][:], start=(kt == 0), stop=False, waits=w)
            mm1 = p.mm(PSUM[:, 6 * TC:6 * TC + 8], C.ones_row_bf[:], rb2r[:],
                       start=False, stop=True,
                       waits=[t_rb2r, C.tok_ones_row_bf])
            s1 = p.reduce("vector", sc8[:, 0:1], PSUM[:, 6 * TC:6 * TC + 8],
                          ALU.max, waits=[mm1, pr_user], negate=True)
            s2 = p.act(pr8[:], PSUM[:, 6 * TC:6 * TC + 8], AF.Exp,
                       bias=sc8[:, 0:1], accum_out=sc8[:, 1:2], waits=[s1])
            l8_user = s2
            s3 = p.recip(sc8[:, 2:3], sc8[:, 1:2], waits=[s2])
            s4 = p.ts("vector", pr8[:], pr8[:], sc8[:, 2:3], None, ALU.mult,
                      waits=[s3])
            m1 = p.reduce("vector", sc8[:, 3:4], pr8[:], ALU.max, waits=[s4])
            k1 = p.ts("vector", mk8[0][:], pr8[:], sc8[:, 3:4], None,
                      ALU.is_equal, waits=[m1, mt_user])
            q2 = p.stt("vector", pr8[:], mk8[0][:], -1e9, pr8[:], ALU.mult,
                       ALU.add, waits=[k1])
            m2 = p.reduce("vector", sc8[:, 4:5], pr8[:], ALU.max, waits=[q2])
            k2 = p.ts("vector", mk8[1][:], pr8[:], sc8[:, 4:5], None,
                      ALU.is_equal, waits=[m2])
            ws_ = p.tt("vector", sc8[:, 5:6], sc8[:, 3:4], sc8[:, 4:5],
                       ALU.add, waits=[m2])
            wi_ = p.recip(sc8[:, 6:7], sc8[:, 5:6], waits=[ws_])
            w0_ = p.tt("vector", sc8[:, 7:8], sc8[:, 3:4], sc8[:, 6:7],
                       ALU.mult, waits=[wi_])
            w1_ = p.tt("vector", sc8[:, 8:9], sc8[:, 4:5], sc8[:, 6:7],
                       ALU.mult, waits=[w0_])
            pr_user = w1_
            off = c * TC + tsub * 128
            toks = []
            for j in range(2):
                tt1 = p.transpose(mtp, mk8[j][:], C.identF[:],
                                  waits=[k1 if j == 0 else k2, C.tok_identF,
                                         mt_user, toks[-1] if toks else None])
                ev = p.copy("vector", maskT[j][:, off:off + 128], mtp,
                            waits=[tt1])
                toks.append(ev)
                tt2 = p.transpose(wtp, sc8[:, 7 + j:8 + j], C.identF[:],
                                  waits=[w0_ if j == 0 else w1_, C.tok_identF,
                                         ev])
                ev2 = p.copy("vector", wrow[j][:, off:off + 128], wtp,
                             waits=[tt2])
                toks.append(ev2)
            mt_user = toks[-1]
    es_r.close()
    p.barrier()
    if FFN_PHASES < 2:
        es_w.close()
        return

    cswr = sb(nc, es_w, f"{tag}_cswr", [1, 1], F32)
    tcsw = p.ts("vector", cswr[:], swrow[:], -1.0, 1.0, ALU.mult, ALU.add)
    tw0 = p.ts("vector", wrow[0][:], wrow[0][:], cswr[0:1, 0:1], None,
               ALU.mult, waits=[tcsw])
    tw1 = p.ts("vector", wrow[1][:], wrow[1][:], cswr[0:1, 0:1], None,
               ALU.mult, waits=[tcsw, tw0])

    es_i = contextlib.ExitStack()
    integ = sb(nc, es_i, f"{tag}_ig", [128, PT * TC], BF16)
    vst = sb(nc, es_i, f"{tag}_vs", [128, PT * TC], BF16)
    xm = sb(nc, es_i, f"{tag}_xm", [128, PT * TC], BF16)
    sa = sb(nc, es_i, f"{tag}_sa", [128, PT * TC], BF16)
    sp_ = sb(nc, es_i, f"{tag}_sp", [128, PT * TC], BF16)
    sg = sb(nc, es_i, f"{tag}_sg", [128, PT * TC], BF16)
    vn = sb(nc, es_i, f"{tag}_vn", [128, PT * TC], BF16)
    Gb = sb(nc, es_i, f"{tag}_G", [128, PT * TC], BF16)
    vmix = sb(nc, es_i, f"{tag}_vm", [128, PT * TC], BF16)
    he = sb(nc, es_i, f"{tag}_he", [128, 4 * TC], BF16)
    hj = sb(nc, es_i, f"{tag}_hj", [128, 4 * TC], BF16)
    hsh = sb(nc, es_i, f"{tag}_hsh", [64, TC], BF16)
    h1b = sb(nc, es_i, f"{tag}_h1b", [128, 2 * TC], BF16)
    hrow = sb(nc, es_i, f"{tag}_hrow", [1, TC], F32)
    row01 = sb(nc, es_i, f"{tag}_r01", [1, 128], F32)
    t_r01 = p.ts("vector", row01[:], C.ones_row[:], 0.1, None, ALU.mult,
                 waits=[C.tok_ones_row])
    _xpd = sb(nc, es_i, f"{tag}_xpd", [128, TC], F32)
    expd = [_xpd, _xpd]
    tmp = [sb(nc, es_i, f"{tag}_tmp{i}", [128, TC], BF16) for i in range(3)]
    hpl = sb(nc, es_i, f"{tag}_hpl", [128, TC], BF16)
    h01 = sb(nc, es_i, f"{tag}_h01", [128, TC], BF16)
    wpl = sb(nc, es_i, f"{tag}_wpl", [128, TC], BF16)
    u = {i: [] for i in range(8)}
    yc_user = None
    st_user = None
    gate_user = {"sa": None, "sp": None, "sg": None}
    vn_user = None
    G_user = None
    vm_user = None
    xm_user = None
    he_user = None
    hj_user = None
    hsh_user = None
    h1_user = [None, None]
    hrow_user = None
    expd_user = [None, None]
    tmp_user = [None, None, None]
    hpl_user = None
    h01_user = None
    wpl_user = None
    spill_t = None
    last_ctx_mm = None
    for c in range(NCH):
        cs = slice(c * TC, (c + 1) * TC)
        it0 = p.dma("sync", integ[:].rearrange("p (a k) -> p a k", a=PT),
                    sy[:, :, cs], waits=[st_user, spill_t, last_ctx_mm])
        vz = p.memset("vector", vst[:], 0.0, waits=[st_user, last_ctx_mm])
        st_user = vz
        ctx = [integ[:, i * TC:(i + 1) * TC] for i in range(PT)] + \
              [vst[:, i * TC:(i + 1) * TC] for i in range(PT)]
        for it_ in range(2):
            xmt = []
            for i in range(PT):
                t = p.ts("vector", xm[:, i * TC:(i + 1) * TC],
                         integ[:, i * TC:(i + 1) * TC], muv[:, i:i + 1], None,
                         ALU.subtract, waits=[t_muv, it0 if it_ == 0 else None,
                                              st_user, xm_user])
                xmt.append(t)
            xm_user = xmt[-1]

            def gates_from_ctrl(mm_cb, bias_ap_fn, tagj):
                toks = {"sa": [], "sp": [], "sg": []}
                for fot in range(24):
                    pb = fot % 2
                    mmt = mm_cb(fot, B(4 + pb), u[4 + pb])
                    g = fot // 8
                    i = fot % 8
                    dst = (sa, sp_, sg)[g]
                    dsl = dst[:, i * TC:(i + 1) * TC]
                    bias = bias_ap_fn(fot)
                    if g == 1:
                        eb = expd[pb]
                        e1 = p.act(eb[:], B(4 + pb), AF.Exp, bias=bias,
                                   waits=[mmt, expd_user[0], expd_user[1],
                                          gate_user["sp"]])
                        u[4 + pb] = [e1]
                        e2 = p.act(dsl, eb[:], AF.Ln, bias=C.sc(1.0),
                                   waits=[e1, C.tok_scal])
                        expd_user[0] = e2
                        expd_user[1] = e2
                        toks["sp"].append(e2)
                    else:
                        nm = "sa" if g == 0 else "sg"
                        e1 = p.act(dsl, B(4 + pb), AF.Sigmoid, bias=bias,
                                   waits=[mmt, gate_user[nm]])
                        u[4 + pb] = [e1]
                        toks[nm].append(e1)
                return toks

            def inl_and_combine(toks, coef_kind):
                nonlocal G_user, vm_user, vn_user, wpl_user
                vnt = []
                for i in range(PT):
                    isl = slice(i * TC, (i + 1) * TC)
                    t1 = p.tt("vector", tmp[0][:], sa[:, isl], vst[:, isl],
                              ALU.mult, waits=[toks["sa"][i], st_user,
                                               tmp_user[0]])
                    t2 = p.tt("vector", tmp[1][:], sp_[:, isl], xm[:, isl],
                              ALU.mult, waits=[toks["sp"][i], xmt[i],
                                               tmp_user[1]])
                    t3 = p.tt("vector", vn[:, isl], tmp[0][:], tmp[1][:],
                              ALU.subtract, waits=[t1, t2, vn_user])
                    tmp_user[0] = t3
                    tmp_user[1] = t3
                    vnt.append(t3)
                    t4 = p.tt("vector", tmp[2][:], sg[:, isl], vn[:, isl],
                              ALU.mult, waits=[toks["sg"][i], t3, tmp_user[2]])
                    if coef_kind == "shared":
                        t5 = p.ts("vector", Gb[:, isl], tmp[2][:],
                                  swcol[:, 0:1], None, ALU.mult,
                                  waits=[t4, t_swc, G_user])
                        t6 = p.ts("vector", vmix[:, isl], vn[:, isl],
                                  swcol[:, 0:1], None, ALU.mult,
                                  waits=[t3, t_swc, vm_user])
                    else:
                        j = coef_kind
                        t5a = p.tt("vector", tmp[2][:], tmp[2][:], wpl[:],
                                   ALU.mult, waits=[t4, wpl_user])
                        t5 = p.tt("vector", Gb[:, isl], Gb[:, isl], tmp[2][:],
                                  ALU.add, waits=[t5a, G_user])
                        t6a = p.tt("vector", tmp[2][:], vn[:, isl], wpl[:],
                                   ALU.mult, waits=[t3, t5])
                        t6 = p.tt("vector", vmix[:, isl], vmix[:, isl],
                                  tmp[2][:], ALU.add, waits=[t6a, vm_user])
                        wpl_user = t6a
                    tmp_user[2] = t6
                    G_user = None
                    vm_user = None
                return vnt

            mmt = None
            for kt in range(16):
                w = [t_sw1] + ([st_user, it0, vz] if kt == 0 else []) + \
                    (u[6] if kt == 0 else [])
                mmt = p.mm(PSUM[0:64, 6 * TC:7 * TC], sw1[kt][:, 0:64],
                           ctx[kt], start=(kt == 0), stop=(kt == 15), waits=w)
            hsh_t = p.act(hsh[:], PSUM[0:64, 6 * TC:7 * TC], AF.Gelu,
                          bias=sb1c[0:64, 0:1], waits=[mmt, t_sb1c, hsh_user])
            u[6] = [hsh_t]
            hsh_user = None

            def mm_shared(fot, bank, uguard):
                w = [hsh_t, t_sw2] + uguard
                m1 = p.mm(bank, sw2[:, fot * 128:(fot + 1) * 128], hsh[:],
                          start=True, stop=True, waits=w)
                return m1
            tk_sh = gates_from_ctrl(mm_shared,
                                    lambda fot: sb2v[:, fot:fot + 1], "sh")
            vn_user = None
            vns = inl_and_combine(tk_sh, "shared")
            vn_user = vns[-1]
            hsh_user = mmt
            het = []
            for fot in range(4):
                pb = fot % 2
                mmt = None
                for kt in range(16):
                    w = [t_w1a] + ([it0, vz, vn_user] if kt == 0 else []) + \
                        (u[4 + pb] if kt == 0 else [])
                    mmt = p.mm(B(4 + pb), w1a[kt][:, fot * 128:(fot + 1) * 128],
                               ctx[kt], start=(kt == 0), stop=(kt == 15),
                               waits=w)
                e = p.act(he[:, fot * TC:(fot + 1) * TC], B(4 + pb), AF.Gelu,
                          bias=b1a[:, fot:fot + 1],
                          waits=[mmt, t_b1a, he_user])
                u[4 + pb] = [e]
                het.append(e)
            he_user = None
            for j in range(2):
                wm = p.mm(B(2), C.ones_row_bf[:], wrow[j][0:1, cs], True, True,
                          waits=[tw1, C.tok_ones_row_bf] + u[2])
                wev = p.act(wpl[:], B(2), AF.Identity, waits=[wm, wpl_user])
                u[2] = [wev]
                hjt = []
                for fot in range(4):
                    mm_ = p.mm(B(3), epat[:, fot * 128:(fot + 1) * 128],
                               maskT[j][:, cs], True, True,
                               waits=[t_epat, mt_user] + u[3])
                    t = p.tt("vector", hj[:, fot * TC:(fot + 1) * TC],
                             he[:, fot * TC:(fot + 1) * TC], B(3, w=TC),
                             ALU.mult, waits=[mm_, het[fot], hj_user])
                    u[3] = [t]
                    hjt.append(t)
                hj_user = None

                def mm_exp(fot, bank, uguard, j=j, hjt=hjt):
                    m1 = None
                    for kt in range(4):
                        w = [hjt[kt], t_w2a] + (uguard if kt == 0 else [])
                        m1 = p.mm(bank, w2a[kt][:, fot * 128:(fot + 1) * 128],
                                  hj[:, kt * TC:(kt + 1) * TC],
                                  start=(kt == 0), stop=False, waits=w)
                    m1 = p.mm(bank, b2a[:, fot * 128:(fot + 1) * 128],
                              maskT[j][:, cs], start=False, stop=True,
                              waits=[t_b2a, mt_user])
                    return m1
                tk_j = gates_from_ctrl(mm_exp, lambda fot: 0.0, f"e{j}")
                vns = inl_and_combine(tk_j, j)
                vn_user = vns[-1]
                hj_user = m1_last = None
            for fo in range(2):
                pb = fo % 2
                mmt = None
                for kt in range(PT):
                    w = [t_hw1, st_user, it0] + (u[4 + pb] if kt == 0 else [])
                    mmt = p.mm(B(4 + pb), hw1[kt][:, fo * 128:(fo + 1) * 128],
                               integ[:, kt * TC:(kt + 1) * TC],
                               start=(kt == 0), stop=(kt == PT - 1), waits=w)
                e = p.act(h1b[:, fo * TC:(fo + 1) * TC], B(4 + pb), AF.Gelu,
                          bias=hb1[:, fo:fo + 1], waits=[mmt, t_hb1,
                                                         h1_user[pb]])
                u[4 + pb] = [e]
                h1_user[pb] = e
            mmt = None
            for kt in range(2):
                w = [h1_user[kt], t_hw2] + (u[6] if kt == 0 else [])
                mmt = p.mm(PSUM[0:1, 6 * TC:7 * TC], hw2[kt][:],
                           h1b[:, kt * TC:(kt + 1) * TC], start=(kt == 0),
                           stop=(kt == 1), waits=w)
            hr1 = p.act(hrow[:, 0:TC], PSUM[0:1, 6 * TC:7 * TC], AF.Sigmoid,
                        bias=hb2c[0:1, 0:1], waits=[mmt, t_hb2c, hrow_user])
            u[6] = [hr1]
            hrow_user = hr1
            hm1 = p.mm(B(2), C.ones_row[:], hrow[:, 0:TC], True, True,
                       waits=[hr1, C.tok_ones_row] + u[2])
            hpe = p.act(hpl[:], B(2), AF.Identity, waits=[hm1, hpl_user])
            u[2] = [hpe]
            hm2 = p.mm(B(3), row01[:], hrow[:, 0:TC], True, True,
                       waits=[hr1, t_r01] + u[3])
            h0e = p.act(h01[:], B(3), AF.Identity, waits=[hm2, h01_user])
            u[3] = [h0e]
            hpl_user = None
            h01_user = None
            upd = []
            for i in range(PT):
                isl = slice(i * TC, (i + 1) * TC)
                t1 = p.tt("vector", tmp[0][:], Gb[:, isl], h01[:],
                          ALU.mult, waits=[h0e, G_user, tmp_user[0]])
                t2 = p.tt("vector", integ[:, isl], integ[:, isl], tmp[0][:],
                          ALU.add, waits=[t1])
                tmp_user[0] = t2
                t3 = p.tt("vector", tmp[1][:], vmix[:, isl], vst[:, isl],
                          ALU.subtract, waits=[vm_user, tmp_user[1], t2])
                t4 = p.tt("vector", tmp[2][:], tmp[1][:], hpl[:], ALU.mult,
                          waits=[t3, hpe, tmp_user[2]])
                tmp_user[1] = t4
                t5 = p.tt("vector", vst[:, isl], vst[:, isl], tmp[2][:],
                          ALU.add, waits=[t4])
                tmp_user[2] = t5
                upd.append(t5)
            st_user = upd[-1]
            G_user = upd[-1]
            vm_user = upd[-1]
            hpl_user = upd[-1]
            h01_user = upd[-1]
            vn_user = upd[-1]
            xm_user = upd[-1]
            he_user = upd[-1]
            hj_user = upd[-1]
            hsh_user = upd[-1]
            hrow_user = upd[-1]
            gate_user = {"sa": upd[-1], "sp": upd[-1], "sg": upd[-1]}
            expd_user = [upd[-1], upd[-1]]
            tmp_user = [None, None, None]
        last_ctx_mm = mmt
        spill_t = p.dma("sync", si[:, :, cs],
                        integ[:].rearrange("p (a k) -> p a k", a=PT),
                        waits=[st_user] + upd)

    es_i.close()
    es_w.close()
    p.barrier()
    if FFN_PHASES < 3:
        return

    rf1, t_rf1 = ldw("refine_w1", PT, 2 * D, es=es)
    rf2, t_rf2 = ldw("refine_w2", 16, D, es=es)
    rfb1, t_rfb1 = load_vec_pf(nc, es, p, f"{tag}_rfb1", dram["refine_b1"], 16)
    rfb2, t_rfb2 = load_vec_pf(nc, es, p, f"{tag}_rfb2", dram["refine_b2"], PT)
    igf = sb(nc, es, f"{tag}_igf", [128, PT * TC], BF16)
    ycf = sb(nc, es, f"{tag}_ycf", [128, PT * TC], BF16)
    midb = sb(nc, es, f"{tag}_mid", [128, 16 * TC], BF16)
    oxb = [sb(nc, es, f"{tag}_oxf{i}", [128, TC], BF16) for i in range(2)]
    rscr = [sb(nc, es, f"{tag}_rs{i}", [128, TC], F32) for i in range(2)]
    xo = dram["xo"]
    ig_user = None
    ycf_user = None
    mid_user = [None] * 16
    oxb_user = [None, None]
    rs_user = [None, None]
    uF = {i: [] for i in range(8)}
    for c in range(NCH):
        cs = slice(c * TC, (c + 1) * TC)
        ilt = p.dma("sync", igf[:].rearrange("p (a k) -> p a k", a=PT),
                    si[:, :, cs], waits=[ig_user, spill_t if c == 0 else None])
        ylt = p.dma("sync", ycf[:].rearrange("p (a k) -> p a k", a=PT),
                    sy[:, :, cs], waits=[ycf_user])
        xbi = xs.load(c)
        mts = []
        for fot in range(16):
            pb = fot % 2
            mmt = None
            for kt in range(PT):
                w = [ilt, t_rf1] + (uF[4 + pb] if kt == 0 else [])
                mmt = p.mm(B(4 + pb), rf1[kt][:, fot * 128:(fot + 1) * 128],
                           igf[:, kt * TC:(kt + 1) * TC], start=(kt == 0),
                           stop=(kt == PT - 1), waits=w)
            e = p.act(midb[:, fot * TC:(fot + 1) * TC], B(4 + pb), AF.Gelu,
                      bias=rfb1[:, fot:fot + 1],
                      waits=[mmt, t_rfb1, mid_user[fot]])
            uF[4 + pb] = [e]
            mts.append(e)
        for fo in range(PT):
            pb = fo % 2
            mmt = None
            for kt in range(16):
                w = [mts[kt], t_rf2] + (uF[6 + pb] if kt == 0 else [])
                mmt = p.mm(B(6 + pb), rf2[kt][:, fo * 128:(fo + 1) * 128],
                           midb[:, kt * TC:(kt + 1) * TC], start=(kt == 0),
                           stop=(kt == 15), waits=w)
            e = p.act(rscr[pb][:], B(6 + pb), AF.Identity,
                      bias=rfb2[:, fo:fo + 1], waits=[mmt, t_rfb2,
                                                      rs_user[pb]])
            uF[6 + pb] = [e]
            xap, xtk = xs.tile(xbi, fo)
            t1 = p.ts("vector", rscr[pb][:], rscr[pb][:], iwv[:, fo:fo + 1],
                      None, ALU.mult, waits=[e, t_iwv])
            t2 = p.tt("vector", rscr[pb][:], rscr[pb][:],
                      ycf[:, fo * TC:(fo + 1) * TC], ALU.add,
                      waits=[t1, ylt])
            rt = p.tt("vector", oxb[pb][:], rscr[pb][:], xap, ALU.add,
                      waits=[t2, xtk, oxb_user[pb]])
            rs_user[pb] = rt
            oxb_user[pb] = p.dma("sync", xo[fo * 128:(fo + 1) * 128, cs],
                                 oxb[pb][:], waits=[rt])
        xs.use(xbi, rt)
        ig_user = mts[-1]
        for fot in range(16):
            mid_user[fot] = mmt
        ycf_user = rt


def build_l0_ffn_program(in_dtype="bf16"):
    nc = bass.Bass("TRN2", target_bir_lowering=False)
    es = contextlib.ExitStack()
    xin = nc.dram_tensor("xin", [D, T], F32 if in_dtype == "f32" else BF16,
                         kind="ExternalInput").ap()
    dr = {}
    shapes = {
        "router_w1": [D, 256], "router_w2": [256, 8], "halt_w1": [D, 256],
        "halt_w2": [256, 1], "shared_w1": [2 * D, 64], "shared_w2": [64, 3 * D],
        "W1_all": [2 * D, 512], "W2_all": [512, 3 * D], "expert_b2": [8, 3 * D],
        "refine_w1": [D, 2 * D], "refine_w2": [2 * D, D],
        "router_b2": [8], "shared_b1": [64], "halt_b2": [1],
        "shared_weight": [1], "cd_epat": [8, 4 * 128],
    }
    for n, s in shapes.items():
        dt = BF16 if n == "cd_epat" else F32
        dr[n] = nc.dram_tensor(n, s, dt, kind="ExternalInput").ap()
    for n, npad in [("router_b1", 2), ("halt_b1", 2), ("expert_b1", 4),
                    ("shared_b2", 24), ("mu", PT), ("integration_weight", PT),
                    ("refine_b1", 16), ("refine_b2", PT)]:
        dr[n] = nc.dram_tensor(n, [128, npad], F32, kind="ExternalInput").ap()
    dr["xo"] = nc.dram_tensor("xo", [D, T], BF16, kind="ExternalOutput").ap()
    dr["s_y"] = nc.dram_tensor("s_y", [D, T], BF16, kind="Internal").ap()
    dr["s_i"] = nc.dram_tensor("s_i", [D, T], BF16, kind="Internal").ap()
    with es:
        p, sems = make_prog(nc, es)
        C = Consts(nc, es, p)
        dr["xin"] = xin
        dr["cast"] = (in_dtype == "f32")
        emit_l0_ffn(nc, es, p, C, dr, "ff")
        p.add("gpsimd", lambda e: e.engine_nop(), waits=p.all_tokens(), inc=1)
        p.finalize(sems)
    return nc


QC_AA = 256


def _host_dmask():
    k = np.arange(128)[:, None]
    q = np.arange(QC_AA)[None, :]
    m0 = (k <= q).astype(np.float32)
    m1 = (k <= q - 128).astype(np.float32)
    return np.concatenate([m0, m1], axis=1).astype(BF)


def _host_epat():
    epat = np.zeros((8, 512), np.float32)
    for t in range(4):
        for a in range(2):
            epat[2 * t + a, t * 128 + a * 64:t * 128 + (a + 1) * 64] = 1.0
    return epat.astype(BF)


def _host_bsel():
    bsel = np.zeros((32, 16 * 128), np.float32)
    for n in range(16):
        bsel[16 + n, n * 128:(n + 1) * 128] = 1.0
    return bsel.astype(BF)


_PROGRAMS = {}


def _prog(name, builder):
    if name not in _PROGRAMS:
        _PROGRAMS[name] = builder()
    return _PROGRAMS[name]


def kernel(**g):
    CO = host_consts()
    ncores = 8
    xts = [np.ascontiguousarray(np.asarray(g["L0_attn_a_X"][b]).T)
           for b in range(ncores)]

    def run(nc, maps):
        res = run_bass_kernel_spmd(nc, maps, core_ids=list(range(ncores)))
        return [r["xo"] for r in res.results]

    nc = _prog("aa", lambda: build_l0_attn_a_program(in_dtype="f32"))
    wm = {
        "wqT": _c(g["L0_attn_a_q_proj"].T), "wkT": _c(g["L0_attn_a_k_proj"].T),
        "wvT": _c(g["L0_attn_a_v_proj"].T), "woT": _c(g["L0_attn_a_o_proj"].T),
        "cd_dmask": _host_dmask(), **CO,
    }
    xs_ = run(nc, [{"xin": xts[b], **wm} for b in range(ncores)])

    def ffn_maps():
        return {
            "router_w1": g['L0_ffn_router_w1'], "router_w2": g['L0_ffn_router_w2'],
            "halt_w1": g['L0_ffn_halt_w1'], "halt_w2": g['L0_ffn_halt_w2'],
            "shared_w1": g['L0_ffn_shared_w1'], "shared_w2": g['L0_ffn_shared_w2'],
            "W1_all": _c(np.asarray(g['L0_ffn_expert_w1']).transpose(1, 0, 2)
                         .reshape(2048, 512)),
            "W2_all": _c(np.asarray(g['L0_ffn_expert_w2']).reshape(512, 3072)),
            "expert_b2": g['L0_ffn_expert_b2'],
            "refine_w1": g['L0_ffn_refine_w1'], "refine_w2": g['L0_ffn_refine_w2'],
            "router_b2": g['L0_ffn_router_b2'], "shared_b1": g['L0_ffn_shared_b1'],
            "halt_b2": np.asarray(g['L0_ffn_halt_b2']).reshape(1),
            "shared_weight": np.asarray(g['L0_ffn_shared_weight']).reshape(1),
            "cd_epat": _host_epat(),
            "router_b1": _pf(g['L0_ffn_router_b1']),
            "halt_b1": _pf(g['L0_ffn_halt_b1']),
            "expert_b1": _pf(np.asarray(g['L0_ffn_expert_b1']).reshape(-1)),
            "shared_b2": _pf(g['L0_ffn_shared_b2']),
            "mu": _pf(g['L0_ffn_mu']),
            "integration_weight": _pf(g['L0_ffn_integration_weight']),
            "refine_b1": _pf(g['L0_ffn_refine_b1']),
            "refine_b2": _pf(g['L0_ffn_refine_b2']), **CO,
        }

    nc = _prog("ffn", lambda: build_l0_ffn_program(in_dtype="bf16"))
    fm = ffn_maps()
    xs_ = run(nc, [{"xin": xs_[b], **fm} for b in range(ncores)])

    nc = _prog("ab", lambda: build_l0_attn_b_program(in_dtype="bf16"))
    wm = {
        "Wq": g['L0_attn_b_Wq'], "Wk": g['L0_attn_b_Wk'],
        "Wv": g['L0_attn_b_Wv'], "Wo": g['L0_attn_b_Wo'],
        "bq": g['L0_attn_b_bq'], "bk": g['L0_attn_b_bk'],
        "bv": g['L0_attn_b_bv'], "bo": _pf(g['L0_attn_b_bo']), **CO,
    }
    xs_ = run(nc, [{"xin": xs_[b], **wm} for b in range(ncores)])

    nc = _prog("ffn", lambda: build_l0_ffn_program(in_dtype="bf16"))
    xs_ = run(nc, [{"xin": xs_[b], **fm} for b in range(ncores)])

    nc = _prog("l1aa", lambda: build_l1_attn_a_program(in_dtype="bf16"))
    wm = {
        "Wq": g['L1_attn_a_Wq'], "Wk": g['L1_attn_a_Wk'],
        "Wv": g['L1_attn_a_Wv'],
        "omegaT": _c(np.asarray(g['L1_attn_a_omega']).T), **CO,
    }
    xs_ = run(nc, [{"xin": xs_[b], **wm} for b in range(ncores)])

    nc = _prog("l1f", lambda: build_l1_ffn_program(final=False, in_dtype="bf16"))
    wm = {"lw": g['L1_ffn_weight'], "lb": _pf(g['L1_ffn_bias']), **CO}
    xs_ = run(nc, [{"xin": xs_[b], **wm} for b in range(ncores)])

    nc = _prog("ssm", lambda: build_l1_attn_b_program(in_dtype="bf16"))
    wm = {
        "W_in": g['L1_attn_b_W_in'], "W_time": g['L1_attn_b_W_time'],
        "W_out": g['L1_attn_b_W_out'],
        "W_state_pf": _c(np.asarray(g['L1_attn_b_W_state'])
                         .reshape(16, 128, 32).transpose(1, 0, 2)
                         .reshape(128, 512)),
        "conv_w_pf": _pf2(g['L1_attn_b_conv_w'], 3),
        "conv_b": _pf(g['L1_attn_b_conv_b']),
        "b_time": _pf(g['L1_attn_b_b_time']),
        "skip": _pf(g['L1_attn_b_skip']),
        "A_log_pf": _pf2(g['L1_attn_b_A_log'], 16),
        "cd_bsel": _host_bsel(), **CO,
    }
    xs_ = run(nc, [{"xin": xs_[b], **wm} for b in range(ncores)])

    nc = _prog("l1ff", lambda: build_l1_ffn_program(final=True, in_dtype="bf16"))
    wm = {"lw": g['L1_ffn_weight'], "lb": _pf(g['L1_ffn_bias']), **CO}
    xs_ = run(nc, [{"xin": xs_[b], **wm} for b in range(ncores)])

    out = np.stack([np.ascontiguousarray(np.asarray(x, np.float32).T)
                    for x in xs_], axis=0)
    return out


# version 05
# speedup vs baseline: 1.1920x; 1.1920x over previous
import contextlib
import numpy as np
import ml_dtypes
import concourse.bass as bass
import concourse.mybir as mybir
from concourse.bass_utils import run_bass_kernel_spmd

F32 = mybir.dt.float32
BF16 = mybir.dt.bfloat16
AF = mybir.ActivationFunctionType
ALU = mybir.AluOpType
AX = mybir.AxisListType
BF = ml_dtypes.bfloat16
INT32 = mybir.dt.int32

T = 4096
D = 1024
PT = D // 128
TC = 512
NCH = T // TC
SHIFT = 12.0
ENGINES = ("sync", "scalar", "vector", "tensor", "gpsimd")


class Prog:
    SEMS = ("sync", "sdma", "gdma", "scalar", "vector", "tensor", "gpsimd")
    DMA_SEM = {"sync": "sync", "scalar": "sdma", "gpsimd": "gdma"}

    def __init__(self, nc):
        self.nc = nc
        self.ops = {e: [] for e in ENGINES}
        self.counts = {s: 0 for s in self.SEMS}
        self.queue_extra = {}

    def add(self, eng, fn, waits=(), inc=1, sem=None):
        sem = sem or eng
        waits = [w for w in waits if w is not None]
        for w in waits:
            assert isinstance(w, tuple) and len(w) == 2 and w[0] in self.SEMS,                 f"bad wait token {w!r}"
        self.counts[sem] += inc
        self.ops[eng].append((fn, tuple(waits), inc, sem))
        return (sem, self.counts[sem])

    def dma(self, q, out, in_, waits=()):
        s = self.DMA_SEM[q]
        waits = list(waits) + ([(s, self.counts[s])] if self.counts[s] else [])
        if s in self.queue_extra:
            waits.append(self.queue_extra.pop(s))
        return self.add(q, lambda e, o=out, i=in_: e.dma_start(out=o, in_=i),
                        waits, inc=16, sem=s)

    def mm(self, out, lhsT, rhs, start, stop, waits=()):
        return self.add("tensor",
                        lambda e, o=out, l=lhsT, r=rhs, st=start, sp=stop:
                        e.matmul(o, l, r, start=st, stop=sp), waits)

    def act(self, out, in_, func, bias=0.0, scale=1.0, accum_out=None, waits=()):
        return self.add("scalar",
                        lambda e, o=out, i=in_, f=func, b=bias, s=scale, a=accum_out:
                        e.activation(out=o, in_=i, func=f, bias=b, scale=s,
                                     accum_out=a), waits)

    def tt(self, eng, out, in0, in1, op, waits=()):
        return self.add(eng,
                        lambda e, o=out, a=in0, b=in1, p=op:
                        e.tensor_tensor(out=o, in0=a, in1=b, op=p), waits)

    def ts(self, eng, out, in0, s1, s2=None, op0=ALU.mult, op1=ALU.bypass,
           accum_out=None, waits=()):
        return self.add(eng,
                        lambda e, o=out, a=in0, x=s1, y=s2, p0=op0, p1=op1, ac=accum_out:
                        e.tensor_scalar(out=o, in0=a, scalar1=x, scalar2=y,
                                        op0=p0, op1=p1, accum_out=ac), waits)

    def stt(self, eng, out, in0, s, in1, op0, op1, accum_out=None, waits=()):
        return self.add(eng,
                        lambda e, o=out, a=in0, x=s, b=in1, p0=op0, p1=op1, ac=accum_out:
                        e.scalar_tensor_tensor(out=o, in0=a, scalar=x, in1=b,
                                               op0=p0, op1=p1, accum_out=ac), waits)

    def copy(self, eng, out, in_, waits=()):
        return self.add(eng, lambda e, o=out, i=in_: e.tensor_copy(out=o, in_=i),
                        waits)

    def reduce(self, eng, out, in_, op, waits=(), negate=None, absval=None):
        return self.add(eng,
                        lambda e, o=out, i=in_, p=op, n=negate, a=absval:
                        e.tensor_reduce(o, i, AX.X, p, apply_absolute_value=a,
                                        negate=n), waits)

    def scan(self, eng, out, d0, d1, init, waits=()):
        return self.add(eng,
                        lambda e, o=out, a=d0, b=d1, i=init:
                        e.tensor_tensor_scan(out=o, data0=a, data1=b, initial=i,
                                             op0=ALU.mult, op1=ALU.add), waits)

    def transpose(self, out_psum, in_sbuf, ident, waits=()):
        return self.add("tensor",
                        lambda e, o=out_psum, i=in_sbuf, d=ident:
                        e.transpose(o, i, d), waits)

    def vtrans(self, out, in_, waits=()):
        return self.add("vector",
                        lambda e, o=out, i=in_: e.transpose(out=o, in_=i), waits)

    def memset(self, eng, ap, val, waits=()):
        return self.add(eng, lambda e, a=ap, v=val: e.memset(a, v), waits)

    def recip(self, out, in_, waits=()):
        return self.add("vector",
                        lambda e, o=out, i=in_: e.reciprocal(out=o, in_=i), waits)

    def barrier(self):
        toks = [(s, c) for s, c in self.counts.items() if c]
        for e in ENGINES:
            w = [t for t in toks if not (t[0] == e and e != "sync")]
            self.add(e, lambda eng: eng.nop(), w, inc=0)

    def all_tokens(self):
        return [(s, c) for s, c in self.counts.items() if c]

    def finalize(self, sems):
        nc = self.nc
        with nc.Block() as blk:
            for ename in ENGINES:
                ops = self.ops[ename]

                def make(ops=ops):
                    def body(eng):
                        for fn, waits, inc, sem in ops:
                            for (we, wv) in waits:
                                eng.wait_ge(sems[we], wv)
                            inst = fn(eng)
                            if inst is not None and inc:
                                inst.then_inc(sems[sem], inc)
                    return body
                getattr(blk, ename)(make())


def make_prog(nc, es):
    sems = {s: es.enter_context(nc.semaphore(f"sem_{s}")) for s in Prog.SEMS}
    return Prog(nc), sems


def sb(nc, es, name, shape, dtype):
    return es.enter_context(nc.sbuf_tensor(name, shape, dtype))


def ps(nc, es, name, shape, dtype=F32):
    return es.enter_context(nc.psum_tensor(name, shape, dtype))


class Consts:
    def __init__(self, nc, es, p):
        self.ident = sb(nc, es, "c_ident", [128, 128], BF16)
        self.ones_col = sb(nc, es, "c_ones_col", [128, 1], F32)
        self.ones_col_bf = sb(nc, es, "c_ones_col_bf", [128, 1], BF16)
        self.ones_row = sb(nc, es, "c_ones_row", [1, 128], F32)
        self.ones_row_bf = sb(nc, es, "c_ones_row_bf", [1, 128], BF16)
        d_id = nc.dram_tensor("cd_ident", [128, 128], F32, kind="ExternalInput").ap()
        d_oc = nc.dram_tensor("cd_ones", [128], F32, kind="ExternalInput").ap()
        self.SCALS = {0.125: 0, -SHIFT: 1, 0.03125: 2, 1e-6: 3, -1.0: 4,
                      0.5: 5, 0.1: 6, 1.0: 7}
        d_sc = nc.dram_tensor("cd_scal", [128, 8], F32, kind="ExternalInput").ap()
        self.scal = sb(nc, es, "c_scal", [128, 8], F32)
        self.tok_scal = p.dma("sync", self.scal[:], d_sc)
        self.tok_ones_col = p.dma("sync", self.ones_col[:, 0:1],
                                  d_oc.rearrange("(k o) -> k o", o=1))
        self.tok_ones_row = p.dma("sync", self.ones_row[:, :],
                                  d_oc.rearrange("(o k) -> o k", o=1))
        self.tok_ident = p.dma("gpsimd", self.ident[:], d_id)
        self.identF = sb(nc, es, "c_identF", [128, 128], F32)
        self.tok_identF = p.dma("sync", self.identF[:], d_id)
        self.tok_ones_col_bf = p.dma("gpsimd", self.ones_col_bf[:, 0:1],
                                     d_oc.rearrange("(k o) -> k o", o=1))
        self.tok_ones_row_bf = p.dma("gpsimd", self.ones_row_bf[:, :],
                                     d_oc.rearrange("(o k) -> o k", o=1))

    def sc(self, v):
        return self.scal[:, self.SCALS[v]:self.SCALS[v] + 1]


def load_vec_pf(nc, es, p, name, dram_ap, n, dtype=F32, q="sync"):
    t = sb(nc, es, name, [128, n], dtype)
    tok = p.dma(q, t[:], dram_ap)
    return t, tok


def emit_ln(nc, es, p, C, xt, xtok, tag, get_tile=None, mps_ap=None,
            qps_ap=None):
    if get_tile is None:
        def get_tile(c, i):
            return xt[i][:, c * TC:(c + 1) * TC], xtok[i]
    sq = [sb(nc, es, f"{tag}_sq{i}", [128, TC], F32) for i in range(2)]
    rstd = sb(nc, es, f"{tag}_rstd", [1, T], F32)
    negmr = sb(nc, es, f"{tag}_negmr", [1, T], F32)
    m_r = sb(nc, es, f"{tag}_mr", [1, TC], F32)
    q_r = sb(nc, es, f"{tag}_qr", [1, TC], F32)
    s_r = sb(nc, es, f"{tag}_sr", [1, TC], F32)
    mps = (mps_ap if mps_ap is not None
           else ps(nc, es, f"{tag}_mps", [1, TC])[:])
    qps = (qps_ap if qps_ap is not None
           else ps(nc, es, f"{tag}_qps", [1, TC])[:])
    ev_m = None
    ev_q = None
    sq_toks = []
    last = None
    for c in range(NCH):
        cs = slice(c * TC, (c + 1) * TC)
        tm = tq = None
        for i in range(PT):
            b = sq[i % 2]
            xap, xtk = get_tile(c, i)
            w = [xtk, sq_toks[-2] if len(sq_toks) >= 2 else None]
            tsq = p.act(b[:], xap, AF.Square, waits=w)
            tm = p.mm(mps, C.ones_col_bf[:], xap, start=(i == 0),
                      stop=(i == PT - 1), waits=[xtk, ev_m, C.tok_ones_col_bf])
            tq = p.mm(qps, C.ones_col[:], b[:], start=(i == 0),
                      stop=(i == PT - 1), waits=[tsq, ev_q, C.tok_ones_col])
            sq_toks.append(tq)
        t1 = p.ts("vector", m_r[:], mps, 1.0 / D, waits=[tm, ev_m])
        ev_m = t1
        t2 = p.act(s_r[:], m_r[:], AF.Square, waits=[t1, ev_q])
        t3 = p.ts("vector", q_r[:], qps, 1.0 / D, 1e-5, ALU.mult, ALU.add,
                  waits=[tq])
        ev_q = t3
        t4 = p.tt("vector", q_r[:], q_r[:], s_r[:], ALU.subtract, waits=[t2, t3])
        t5 = p.act(s_r[:], q_r[:], AF.Ln, waits=[t4])
        t5b = p.ts("vector", s_r[:], s_r[:], -0.5, waits=[t5])
        t6 = p.act(rstd[:, cs], s_r[:], AF.Exp, waits=[t5b])
        last = p.stt("vector", negmr[:, cs], m_r[:], -1.0, rstd[:, cs],
                     ALU.mult, ALU.mult, waits=[t6])
        ev_m = last
        ev_q = last
    return rstd, negmr, last


def emit_y_chunk(nc, es, p, C, xt, xtok, rstd, negmr, stok, c, ytiles, tag,
                 planes=None, ydtype_note=None):
    raise NotImplementedError


class LNPlanes:
    def __init__(self, nc, es, p, C, rstd, negmr, stok, tag, rpl_ps=None,
                 mpl_ps=None):
        self.nc, self.p, self.C = nc, p, C
        self.rstd, self.negmr, self.stok = rstd, negmr, stok
        self.rpl_ps = rpl_ps if rpl_ps is not None \
            else ps(nc, es, f"{tag}_rplps", [128, TC])[:]
        self.mpl_ps = mpl_ps if mpl_ps is not None \
            else ps(nc, es, f"{tag}_mplps", [128, TC])[:]
        self.rpl = sb(nc, es, f"{tag}_rpl", [128, TC], F32)
        self.mpl = sb(nc, es, f"{tag}_mpl", [128, TC], F32)
        self.scr = [sb(nc, es, f"{tag}_yscr{i}", [128, TC], F32) for i in range(2)]
        self.last_use = []
        self.scr_use = {}

    def chunk_planes(self, c):
        p, C = self.p, self.C
        cs = slice(c * TC, (c + 1) * TC)
        w = [self.stok, C.tok_ones_row] + self.last_use
        self.last_use = []
        t1 = p.mm(self.rpl_ps, C.ones_row[:], self.rstd[:, cs], True, True, waits=w)
        t2 = p.mm(self.mpl_ps, C.ones_row[:], self.negmr[:, cs], True, True, waits=w)
        e1 = p.act(self.rpl[:], self.rpl_ps, AF.Identity, waits=[t1])
        e2 = p.act(self.mpl[:], self.mpl_ps, AF.Identity, waits=[t2])
        self.ptok = [e1, e2]
        return [e1, e2]

    def norm_tile(self, i, c, xt, xtok, out_ap, eng="vector", war=None):
        p = self.p
        cs = slice(c * TC, (c + 1) * TC)
        s = self.scr[i % 2]
        war = [war] if (war is None or isinstance(war, tuple)) else list(war)
        w = [xtok, self.scr_use.get(i % 2)] + list(war) + self.ptok
        xap = xt[:, cs] if tuple(xt.shape)[1] > TC else xt
        t1 = p.tt(eng, s[:], xap, self.rpl[:], ALU.mult, waits=w)
        t2 = p.tt(eng, out_ap, s[:], self.mpl[:], ALU.add, waits=[t1])
        self.scr_use[i % 2] = t2
        self.last_use.append(t2)
        return t2


class LNChunk:
    def __init__(self, nc, es, p, C, tag, rpl_ps=None, mpl_ps=None,
                 mps_ap=None, qps_ap=None):
        self.p, self.C = p, C
        self.rstd = sb(nc, es, f"{tag}_rstd", [1, TC], F32)
        self.negmr = sb(nc, es, f"{tag}_negmr", [1, TC], F32)
        self.m_r = sb(nc, es, f"{tag}_mr", [1, TC], F32)
        self.q_r = sb(nc, es, f"{tag}_qr", [1, TC], F32)
        self.s_r = sb(nc, es, f"{tag}_sr", [1, TC], F32)
        self.sq = [sb(nc, es, f"{tag}_sq{i}", [128, TC], F32) for i in range(2)]
        self.rpl = sb(nc, es, f"{tag}_rpl", [128, TC], F32)
        self.mpl = sb(nc, es, f"{tag}_mpl", [128, TC], F32)
        self.scr = [sb(nc, es, f"{tag}_ys{i}", [128, TC], F32) for i in range(2)]
        self.mps = mps_ap if mps_ap is not None else ps(nc, es, f"{tag}_mps", [1, TC])[:]
        self.qps = qps_ap if qps_ap is not None else ps(nc, es, f"{tag}_qps", [1, TC])[:]
        self.rpl_ps = rpl_ps if rpl_ps is not None else ps(nc, es, f"{tag}_rpp", [128, TC])[:]
        self.mpl_ps = mpl_ps if mpl_ps is not None else ps(nc, es, f"{tag}_mpp", [128, TC])[:]
        self.sq_toks = []
        self.row_user = None
        self.ev_m = None
        self.ev_q = None
        self.ptok = []
        self.scr_use = {}
        self.pl_user = []

    def stats_and_planes(self, tiles):
        p, C = self.p, self.C
        tm = tq = None
        for i, (xap, xtk) in enumerate(tiles):
            b = self.sq[i % 2]
            w = [xtk, self.sq_toks[-2] if len(self.sq_toks) >= 2 else None]
            tsq = p.act(b[:], xap, AF.Square, waits=w)
            tm = p.mm(self.mps, C.ones_col_bf[:], xap, start=(i == 0),
                      stop=(i == PT - 1), waits=[xtk, self.ev_m,
                                                C.tok_ones_col_bf])
            tq = p.mm(self.qps, C.ones_col[:], b[:], start=(i == 0),
                      stop=(i == PT - 1), waits=[tsq, self.ev_q,
                                                C.tok_ones_col])
            self.sq_toks.append(tq)
        t1 = p.ts("vector", self.m_r[:], self.mps, 1.0 / D,
                  waits=[tm, self.ev_m, self.row_user])
        self.ev_m = t1
        t2 = p.act(self.s_r[:], self.m_r[:], AF.Square, waits=[t1, self.ev_q])
        t3 = p.ts("vector", self.q_r[:], self.qps, 1.0 / D, 1e-5, ALU.mult,
                  ALU.add, waits=[tq, self.row_user])
        self.ev_q = t3
        t4 = p.tt("vector", self.q_r[:], self.q_r[:], self.s_r[:],
                  ALU.subtract, waits=[t2, t3])
        t5 = p.act(self.s_r[:], self.q_r[:], AF.Ln, waits=[t4])
        t5b = p.ts("vector", self.s_r[:], self.s_r[:], -0.5, waits=[t5])
        t6 = p.act(self.rstd[:], self.s_r[:], AF.Exp, waits=[t5b])
        t7 = p.stt("vector", self.negmr[:], self.m_r[:], -1.0, self.rstd[:],
                   ALU.mult, ALU.mult, waits=[t6])
        w = [t7, C.tok_ones_row] + self.pl_user
        self.pl_user = []
        b1 = p.mm(self.rpl_ps, C.ones_row[:], self.rstd[:], True, True, waits=w)
        b2 = p.mm(self.mpl_ps, C.ones_row[:], self.negmr[:], True, True, waits=w)
        self.row_user = b2
        e1 = p.act(self.rpl[:], self.rpl_ps, AF.Identity, waits=[b1])
        e2 = p.act(self.mpl[:], self.mpl_ps, AF.Identity, waits=[b2])
        self.ptok = [e1, e2]

    def norm(self, i, xap, xtok, out_ap, eng="vector", war=None):
        p = self.p
        s = self.scr[i % 2]
        war = [war] if (war is None or isinstance(war, tuple)) else list(war)
        w = [xtok, self.scr_use.get(i % 2)] + list(war) + self.ptok
        t1 = p.tt(eng, s[:], xap, self.rpl[:], ALU.mult, waits=w)
        t2 = p.tt(eng, out_ap, s[:], self.mpl[:], ALU.add, waits=[t1])
        self.scr_use[i % 2] = t2
        self.pl_user.append(t2)
        return t2


class XStream:
    def __init__(self, nc, es, p, name, dram_ap, W=TC, q="sync", nbuf=2):
        self.p, self.W, self.q = p, W, q
        self.dram = dram_ap.rearrange("(a p) t -> p a t", p=128)
        self.bufs = [sb(nc, es, f"{name}{i}", [128, PT * W], BF16)
                     for i in range(nbuf)]
        self.nbuf = nbuf
        self.users = [None] * nbuf
        self.toks = [None] * nbuf

    def load(self, c):
        bi = c % self.nbuf
        b = self.bufs[bi]
        w = self.W
        tok = self.p.dma(self.q, b[:].rearrange("p (a k) -> p a k", a=PT),
                         self.dram[:, :, c * w:(c + 1) * w],
                         waits=[self.users[bi]])
        self.toks[bi] = tok
        self.users[bi] = None
        return bi

    def tile(self, bi, i):
        w = self.W
        return self.bufs[bi][:, i * w:(i + 1) * w], self.toks[bi]

    def use(self, bi, tok):
        self.users[bi] = tok


def _c(a):
    return np.ascontiguousarray(a)


def _pf2(v, inner):
    v = np.asarray(v, np.float32)
    nt = v.shape[0] // 128
    return _c(v.reshape(nt, 128, inner).transpose(1, 0, 2).reshape(128, nt * inner))


def _pf(v):
    v = np.asarray(v, np.float32)
    n = v.size // 128
    return _c(v.reshape(n, 128).T)


SCALS_HOST = [0.125, -SHIFT, 0.03125, 1e-6, -1.0, 0.5, 0.1, 1.0]


def host_consts():
    return {
        "cd_ident": np.eye(128, dtype=np.float32),
        "cd_ones": np.ones((128,), np.float32),
        "cd_scal": _c(np.tile(np.asarray(SCALS_HOST, np.float32)[None, :],
                              (128, 1))),
    }


MAGIC = float(3 * 2 ** 22)


def emit_quant(nc, es, p, tag, WMAX):
    ytr = sb(nc, es, f"{tag}_ytr", [128, WMAX], F32)
    yn = sb(nc, es, f"{tag}_yn", [128, WMAX], F32)
    st = sb(nc, es, f"{tag}_st", [128, WMAX // 2], F32)
    sc = sb(nc, es, f"{tag}_sc", [128, WMAX // 32], F32)
    si = sb(nc, es, f"{tag}_si", [128, WMAX // 32], F32)
    aa = sb(nc, es, f"{tag}_aa", [128, WMAX], F32)
    g2 = sb(nc, es, f"{tag}_g2", [128, WMAX], BF16)
    g4 = sb(nc, es, f"{tag}_g4", [128, WMAX], BF16)
    stp = sb(nc, es, f"{tag}_stp", [128, WMAX], BF16)
    inv = sb(nc, es, f"{tag}_inv", [128, WMAX], BF16)
    r1 = sb(nc, es, f"{tag}_r1", [128, WMAX], F32)
    xdt = sb(nc, es, f"{tag}_xdt", [128, WMAX], BF16)
    state = {"prev": None}

    def quant(in_ap, in_tok, out_ap, W, transpose_layout=True):
        nb = W // 32
        w0 = [in_tok, state["prev"]]
        t = p.vtrans(ytr[:, :W], in_ap, waits=w0) if transpose_layout             else p.copy("vector", ytr[:, :W], in_ap, waits=w0)
        v = ytr[:, :W].rearrange("p (b e) -> p b e", e=32)
        s3 = st[:, :W // 2].rearrange("p (b e) -> p b e", e=16)
        t = p.ts("vector", aa[:, :W].bitcast(INT32), ytr[:, :W].bitcast(INT32),
                 0x7FFFFFFF, None, ALU.bitwise_and, waits=[t])
        va = aa[:, :W].rearrange("p (b e) -> p b e", e=32)
        t = p.tt("vector", s3[:, :, 0:16], va[:, :, 0:16], va[:, :, 16:32],
                 ALU.max, waits=[t])
        for hw in (8, 4, 2, 1):
            t = p.tt("vector", s3[:, :, 0:hw], s3[:, :, 0:hw],
                     s3[:, :, hw:2 * hw], ALU.max, waits=[t])
        t = p.ts("vector", sc[:, :nb], s3[:, :, 0:1].squeeze(2), 1.0 / 7.5,
                 1e-30, ALU.mult, ALU.max, waits=[t])
        t = p.add("vector", lambda e: e.reciprocal(out=si[:, :nb], in_=sc[:, :nb]), [t])
        scb = sc[:, :nb].unsqueeze(2).broadcast_to([128, nb, 32])
        sib = si[:, :nb].unsqueeze(2).broadcast_to([128, nb, 32])
        t = p.tt("vector", yn[:, :W].rearrange("p (b e) -> p b e", e=32), v, sib,
                 ALU.mult, waits=[t])
        ta = p.ts("vector", aa[:, :W].bitcast(INT32), yn[:, :W].bitcast(INT32),
                  0x7FFFFFFF, None, ALU.bitwise_and, waits=[t])
        t2 = p.ts("vector", g2[:, :W], aa[:, :W], 2.0, None, ALU.is_ge, waits=[ta])
        t4 = p.ts("vector", g4[:, :W], aa[:, :W], 4.0, None, ALU.is_ge, waits=[ta])
        ts1 = p.ts("vector", stp[:, :W], g4[:, :W], 0.25, 0.125, ALU.mult,
                   ALU.add, waits=[t4])
        ts2 = p.stt("vector", stp[:, :W], g2[:, :W], 0.125, stp[:, :W],
                    ALU.mult, ALU.add, waits=[ts1, t2])
        ti1 = p.ts("vector", inv[:, :W], g4[:, :W], -2.0, 8.0, ALU.mult,
                   ALU.add, waits=[t4])
        ti2 = p.stt("vector", inv[:, :W], g2[:, :W], -4.0, inv[:, :W],
                    ALU.mult, ALU.add, waits=[ti1, t2])
        t = p.tt("vector", r1[:, :W], yn[:, :W], inv[:, :W], ALU.mult,
                 waits=[ti2, t])
        t = p.ts("vector", r1[:, :W], r1[:, :W], MAGIC, None, ALU.add, waits=[t])
        t = p.ts("vector", r1[:, :W], r1[:, :W], -MAGIC, None, ALU.add, waits=[t])
        t = p.tt("vector", r1[:, :W], r1[:, :W], stp[:, :W], ALU.mult,
                 waits=[t, ts2])
        t = p.tt("vector", xdt[:, :W].rearrange("p (b e) -> p b e", e=32),
                 r1[:, :W].rearrange("p (b e) -> p b e", e=32), scb, ALU.mult,
                 waits=[t])
        t = p.vtrans(out_ap, xdt[:, :W], waits=[t]) if transpose_layout             else p.copy("vector", out_ap, xdt[:, :W], waits=[t])
        state["prev"] = t
        return t

    return quant


def emit_l1_ffn(nc, es, p, C, xt, xtok, dram, tag, final=False, psum=None):
    PSUM = psum if psum is not None else None
    if PSUM is not None:
        mps_ap, qps_ap = PSUM[0:1, 0:TC], PSUM[0:1, TC:2 * TC]
        rpl_ps, mpl_ps = PSUM[:, 2 * TC:3 * TC], PSUM[:, 3 * TC:4 * TC]
    else:
        mps_ap = qps_ap = rpl_ps = mpl_ps = None
    rstd, negmr, stok = emit_ln(nc, es, p, C, xt, xtok, f"{tag}ln",
                               mps_ap=mps_ap, qps_ap=qps_ap)
    planes = LNPlanes(nc, es, p, C, rstd, negmr, stok, f"{tag}pl",
                      rpl_ps=rpl_ps, mpl_ps=mpl_ps)
    quant = emit_quant(nc, es, p, f"{tag}q", D)

    w32 = [sb(nc, es, f"{tag}_w32_{i}", [128, D], F32) for i in range(2)]
    wdbuf = sb(nc, es, f"{tag}_wd", [128, D], BF16)
    wdT = [sb(nc, es, f"{tag}_wdT{i}", [128, D], BF16) for i in range(PT)]
    tps = (PSUM[:, 6 * TC:6 * TC + 64].bitcast(BF16) if PSUM is not None
           else ps(nc, es, f"{tag}_tps", [128, 128], BF16)[:])
    ttok = [None] * PT
    w32tok = {}
    ev_prev = None
    for i in range(PT):
        b = w32[i % 2]
        lt = p.dma("sync", b[:], dram["lw"].rearrange("(a p) k -> a p k", p=128)[i],
                   waits=[w32tok.get(i % 2)])
        qt = quant(b[:], lt, wdbuf[:], D, transpose_layout=False)
        w32tok[i % 2] = qt
        for j in range(PT):
            tt = p.transpose(tps, wdbuf[:, j * 128:(j + 1) * 128], C.ident[:],
                             waits=[qt, C.tok_ident, ev_prev])
            ev_prev = p.copy("vector", wdT[j][:, i * 128:(i + 1) * 128], tps,
                             waits=[tt])
            ttok[j] = ev_prev
        w32tok[i % 2] = ev_prev
    bias, btok = load_vec_pf(nc, es, p, f"{tag}_b", dram["lb"], PT)

    yc = sb(nc, es, f"{tag}_yc", [128, TC], F32)
    xd = [sb(nc, es, f"{tag}_xd{i}", [128, TC], BF16) for i in range(PT)]
    osb = [sb(nc, es, f"{tag}_osb{i}", [128, TC], F32) for i in range(2)]
    oxt = [sb(nc, es, f"{tag}_ox{i}", [128, TC], F32 if final else BF16)
           for i in range(2)]
    pps = ([PSUM[:, 4 * TC:5 * TC], PSUM[:, 5 * TC:6 * TC]]
           if PSUM is not None else
           [ps(nc, es, f"{tag}_pps{i}", [128, TC])[:] for i in range(2)])
    xdtok = [None] * PT
    pps_user = [None, None]
    ob_user = [None, None]
    yuse = None
    for c in range(NCH):
        cs = slice(c * TC, (c + 1) * TC)
        planes.chunk_planes(c)
        for i in range(PT):
            ytok = planes.norm_tile(i, c, xt[i], xtok[i], yc[:], war=yuse)
            xdtok[i] = quant(yc[:], ytok, xd[i][:], TC)
            yuse = xdtok[i]
        for fo in range(PT):
            pb = fo % 2
            mmt = None
            for kt in range(PT):
                mmt = p.mm(pps[pb], wdT[kt][:, fo * 128:(fo + 1) * 128],
                           xd[kt][:], start=(kt == 0), stop=(kt == PT - 1),
                           waits=[xdtok[kt], ttok[kt], pps_user[pb]]
                           if kt == 0 else [xdtok[kt], ttok[kt]])
            et = p.act(osb[pb][:], pps[pb], AF.Identity,
                       bias=bias[:, fo:fo + 1], waits=[mmt, btok, ob_user[pb]])
            pps_user[pb] = et
            rt = p.tt("vector", oxt[pb][:], osb[pb][:], xt[fo][:, cs], ALU.add,
                      waits=[et, xtok[fo]])
            st = p.dma("sync", dram["xo"].rearrange("(a p) k -> a p k", p=128)[fo][:, cs],
                       oxt[pb][:], waits=[rt])
            ob_user[pb] = st
    return None


def finish(nc, p, sems_holder):
    p.finalize(sems_holder)


def build_l1_ffn_program(final=False, in_dtype="bf16"):
    nc = bass.Bass("TRN2", target_bir_lowering=False)
    es = contextlib.ExitStack()
    xin = nc.dram_tensor("xin", [D, T], BF16 if in_dtype == "bf16" else F32,
                         kind="ExternalInput").ap()
    lw = nc.dram_tensor("lw", [D, D], F32, kind="ExternalInput").ap()
    lb = nc.dram_tensor("lb", [128, D // 128], F32, kind="ExternalInput").ap()
    xo = nc.dram_tensor("xo", [D, T], F32 if final else BF16,
                        kind="ExternalOutput").ap()
    with es:
        p, sems = make_prog(nc, es)
        C = Consts(nc, es, p)
        xt = [sb(nc, es, f"xt{i}", [128, T], BF16) for i in range(PT)]
        xtok = []
        xv = xin.rearrange("(a p) k -> a p k", p=128)
        for i in range(PT):
            q = "gpsimd" if in_dtype == "f32" else "sync"
            xtok.append(p.dma(q, xt[i][:], xv[i]))
        emit_l1_ffn(nc, es, p, C, xt, xtok, {"lw": lw, "lb": lb, "xo": xo},
                    "lf", final=final)
        p.add("gpsimd", lambda e: e.engine_nop(), waits=p.all_tokens(), inc=1)
        p.finalize(sems)
    return nc



def emit_l0_attn_a(nc, es, p, C, dram, tag, psum=None):
    QC = 256
    NQC = T // QC
    PSUM = psum if psum is not None else ps(nc, es, f"{tag}_psum", [128, 4096])
    B = lambda i, w=TC: PSUM[:, i * TC:i * TC + w]

    xs = XStream(nc, es, p, f"{tag}xs", dram["xin"],
                 q="gpsimd")
    ln = LNChunk(nc, es, p, C, f"{tag}ln", rpl_ps=B(2), mpl_ps=B(3),
                 mps_ap=PSUM[0:1, 0:TC], qps_ap=PSUM[0:1, TC:2 * TC])

    wbuf = [sb(nc, es, f"{tag}_w{i}", [128, D], BF16) for i in range(PT)]
    wbuf2 = [sb(nc, es, f"{tag}_w2_{i}", [128, D], BF16) for i in range(PT)]

    def load_w(dram_w, bufs, waits=()):
        dv = dram_w.rearrange("(a p) k -> a p k", p=128)
        toks = [p.dma("gpsimd", bufs[i][:], dv[i], waits=waits)
                for i in range(PT)]
        return toks

    ybuf = sb(nc, es, f"{tag}_yb", [128, PT * TC], BF16)
    kc = [sb(nc, es, f"{tag}_kc{i}", [128, TC], BF16) for i in range(2)]
    vt = sb(nc, es, f"{tag}_vt", [128, 32 * 1024], BF16)
    sk3 = dram["s_k"].rearrange("(a p) t -> p a t", p=128)
    sy = dram["s_y"].rearrange("(a p) t -> p a t", p=128)
    xo3 = dram["xo"].rearrange("(a p) t -> p a t", p=128)

    wk_t = load_w(dram["wkT"], wbuf)
    ytoks = {}
    ps_user = [None, None]
    kc_user = [None, None]
    b1_war = []
    for c in range(NCH):
        bi = xs.load(c)
        ln.stats_and_planes([xs.tile(bi, i) for i in range(PT)])
        last = None
        for i in range(PT):
            xap, xtk = xs.tile(bi, i)
            last = ln.norm(i, xap, xtk, ybuf[:, i * TC:(i + 1) * TC],
                           war=b1_war)
        xs.use(bi, last)
        ytoks[c] = last
        spt = p.dma("sync", sy[:, :, c * TC:(c + 1) * TC],
                    ybuf[:].rearrange("p (a k) -> p a k", a=PT), waits=[last])
        for fo in range(PT):
            pb = fo % 2
            mmt = None
            for kt in range(PT):
                w = [last, wk_t[-1]] + ([ps_user[pb]] if kt == 0 else [])
                mmt = p.mm(B(4 + pb), wbuf[kt][:, fo * 128:(fo + 1) * 128],
                           ybuf[:, kt * TC:(kt + 1) * TC], start=(kt == 0),
                           stop=(kt == PT - 1), waits=w)
            et = p.act(kc[pb][:], B(4 + pb), AF.Identity,
                       waits=[mmt, kc_user[pb]])
            ps_user[pb] = et
            kc_user[pb] = p.dma("sync",
                                dram["s_k"][fo * 128:(fo + 1) * 128,
                                            c * TC:(c + 1) * TC],
                                kc[pb][:], waits=[et])
        b1_war = [mmt, spt]
    b1_last_mm = mmt

    wv_t = load_w(dram["wvT"], wbuf2)
    vtok = [None] * 32
    yb2 = sb(nc, es, f"{tag}_yb2", [128, PT * TC], BF16)
    yb2_user = None
    vps_user = [None, None]
    for c in range(NCH):
        ylt = p.dma("sync", yb2[:].rearrange("p (a k) -> p a k", a=PT),
                    sy[:, :, c * TC:(c + 1) * TC], waits=[ytoks[c], yb2_user])
        for tsub in range(TC // 128):
            kt_glob = c * (TC // 128) + tsub
            for half in range(2):
                pb = half
                mmt = None
                for kt in range(PT):
                    w = [ylt, wv_t[-1]] + ([vps_user[pb]] if kt == 0 else [])
                    mmt = p.mm(B(6 + pb), yb2[:, kt * TC + tsub * 128:
                                               kt * TC + tsub * 128 + 128],
                               wbuf2[kt][:, half * TC:(half + 1) * TC],
                               start=(kt == 0), stop=(kt == PT - 1), waits=w)
                et = p.copy("vector", vt[:, kt_glob * 1024 + half * TC:
                                         kt_glob * 1024 + (half + 1) * TC],
                            B(6 + pb), waits=[mmt])
                vps_user[pb] = et
                vtok[kt_glob] = et
        yb2_user = mmt

    wq_t = load_w(dram["wqT"], wbuf, waits=[ps_user[0], ps_user[1]])
    wo_t = load_w(dram["woT"], wbuf2, waits=[vtok[31]])
    qbuf = sb(nc, es, f"{tag}_qb", [64, 16 * QC], BF16)
    ptens = sb(nc, es, f"{tag}_pt", [128, 32 * QC], BF16)
    khb = sb(nc, es, f"{tag}_kh0", [64, T], BF16)
    obuf = sb(nc, es, f"{tag}_ob", [128, PT * QC], BF16)
    oev = sb(nc, es, f"{tag}_oev", [64, QC], F32)
    zrow = sb(nc, es, f"{tag}_zr", [1, QC], F32)
    zpl = sb(nc, es, f"{tag}_zpl", [64, QC], F32)
    dmask = sb(nc, es, f"{tag}_dm", [128, 2 * QC], BF16)
    dmk = p.dma("sync", dmask[:], dram["cd_dmask"])
    ybq = yb2
    xbq = xs.bufs[0][:, :PT * QC]
    oxb = [sb(nc, es, f"{tag}_oxb{i}", [128, QC], BF16) for i in range(2)]
    sps = PSUM[:, 0:1024]
    ozps = PSUM[:, 2 * TC:2 * TC + QC]
    zplps = PSUM[0:64, 3 * TC:3 * TC + QC]
    phase_c = ("vector", p.counts["vector"])
    sps_user = phase_c
    oz_user = phase_c
    zpl_wru = phase_c
    zpl_reader = None
    ybq_user = None
    xbq_user = None
    qbuf_user = None
    ob_user = None
    oxb_user = [None, None]
    kh_user = None
    head_war = None
    for qc in range(NQC):
        qs = slice(qc * QC, (qc + 1) * QC)
        ktmax = (qc + 1) * QC // 128
        ylt = p.dma("sync", ybq[:, :PT * QC].rearrange("p (a k) -> p a k",
                                                       a=PT),
                    sy[:, :, qs], waits=[ytoks[NCH - 1], ybq_user or yb2_user])
        xlt = p.dma("gpsimd", xbq.rearrange("p (a k) -> p a k", a=PT),
                    xs.dram[:, :, qs],
                    waits=[xbq_user])
        qtoks = []
        for fo in range(PT):
            pb = fo % 2
            mmt = None
            for kt in range(PT):
                w = [ylt, wq_t[-1]] + ([ps_user[pb]] if kt == 0 else [])
                mmt = p.mm(B(4 + pb, QC), wbuf[kt][:, fo * 128:(fo + 1) * 128],
                           ybq[:, kt * QC:(kt + 1) * QC], start=(kt == 0),
                           stop=(kt == PT - 1), waits=w)
            e_a = p.copy("vector", qbuf[:, (2 * fo) * QC:(2 * fo + 1) * QC],
                         PSUM[0:64, (4 + pb) * TC:(4 + pb) * TC + QC],
                         waits=[mmt, qbuf_user])
            e_b = p.copy("vector", qbuf[:, (2 * fo + 1) * QC:(2 * fo + 2) * QC],
                         PSUM[64:128, (4 + pb) * TC:(4 + pb) * TC + QC],
                         waits=[mmt])
            qbuf_user = None
            ps_user[pb] = e_b
            qtoks.append(e_b)
        ybq_user = mmt
        ob_toks = []
        for h in range(16):
            klen = ktmax * 128
            klt = p.dma("sync", khb[:, :klen],
                        dram["s_k"][h * 64:(h + 1) * 64, 0:klen],
                        waits=[kh_user])
            qrhs = qbuf[:, h * QC:(h + 1) * QC]
            first_oz = True
            t1 = t2 = None
            last_ext = head_war
            for g0 in range(0, ktmax, 4):
                gl = min(4, ktmax - g0)
                smm = []
                for j in range(gl):
                    kt = g0 + j
                    w = [klt, qtoks[h // 2]] + ([sps_user] if j == 0 else [])
                    smm.append(p.mm(sps[:, j * QC:(j + 1) * QC],
                                    khb[:, kt * 128:(kt + 1) * 128], qrhs,
                                    start=True, stop=True, waits=w))
                ext = p.act(ptens[:, g0 * QC:(g0 + gl) * QC],
                            sps[:, 0:gl * QC], AF.Exp, bias=C.sc(-SHIFT),
                            scale=C.sc(0.125), waits=smm + [C.tok_scal, last_ext])
                sps_user = ext
                for j in range(gl):
                    kt = g0 + j
                    r = kt * 128 - qc * QC
                    if 0 <= r < QC:
                        ext = p.tt("vector", ptens[:, kt * QC:(kt + 1) * QC],
                                   ptens[:, kt * QC:(kt + 1) * QC],
                                   dmask[:, (r // 128) * QC:(r // 128 + 1) * QC],
                                   ALU.mult, waits=[ext, dmk])
                for j in range(gl):
                    kt = g0 + j
                    w0 = [ext, vtok[kt]] + ([oz_user] if first_oz else [])
                    t1 = p.mm(ozps[0:64, :] if False else PSUM[0:64, 2 * TC:2 * TC + QC],
                              vt[:, kt * 1024 + h * 64:kt * 1024 + h * 64 + 64],
                              ptens[:, kt * QC:(kt + 1) * QC],
                              start=first_oz, stop=(kt == ktmax - 1), waits=w0)
                    t2 = p.mm(PSUM[64:65, 2 * TC:2 * TC + QC], C.ones_col_bf[:],
                              ptens[:, kt * QC:(kt + 1) * QC],
                              start=first_oz, stop=(kt == ktmax - 1),
                              waits=[ext, C.tok_ones_col_bf])
                    first_oz = False
                last_ext = ext
            e1 = p.act(zrow[:], PSUM[64:65, 2 * TC:2 * TC + QC], AF.Ln,
                       waits=[t2, zpl_reader])
            e2 = p.ts("vector", zrow[:], zrow[:], -1.0, None, ALU.mult, waits=[e1])
            e3 = p.act(zrow[:], zrow[:], AF.Exp, waits=[e2])
            bt = p.mm(zplps, C.ones_row[0:1, 0:64], zrow[:], True, True,
                      waits=[e3, C.tok_ones_row, zpl_wru])
            e4 = p.act(zpl[:], zplps, AF.Identity, waits=[bt, zpl_reader])
            zpl_wru = e4
            e5 = p.act(oev[:], PSUM[0:64, 2 * TC:2 * TC + QC], AF.Identity,
                       waits=[t1] + ([ob_toks[-1]] if ob_toks else []))
            e6 = p.tt("vector", obuf[(h % 2) * 64:(h % 2) * 64 + 64,
                                     (h // 2) * QC:(h // 2) * QC + QC],
                      oev[:], zpl[:], ALU.mult, waits=[e4, e5, ob_user])
            zpl_reader = e6
            oz_user = e6
            kh_user = smm[-1]
            head_war = t2
            ob_toks.append(e6)
        for fo in range(PT):
            pb = fo % 2
            mmt = None
            for kt in range(PT):
                w = [wo_t[-1]] + ob_toks + ([ps_user[pb]] if kt == 0 else [])
                mmt = p.mm(B(4 + pb, QC), wbuf2[kt][:, fo * 128:(fo + 1) * 128],
                           obuf[:, kt * QC:(kt + 1) * QC], start=(kt == 0),
                           stop=(kt == PT - 1), waits=w)
            rt = p.tt("vector", oxb[pb][:], B(4 + pb, QC),
                      xbq[:, fo * QC:(fo + 1) * QC], ALU.add,
                      waits=[mmt, xlt, oxb_user[pb]])
            ps_user[pb] = rt
            oxb_user[pb] = p.dma("sync",
                                 dram["xo"][fo * 128:(fo + 1) * 128, qs],
                                 oxb[pb][:], waits=[rt])
        ob_user = mmt
        xbq_user = rt


def build_l0_attn_a_program(in_dtype="f32"):
    nc = bass.Bass("TRN2", target_bir_lowering=False)
    es = contextlib.ExitStack()
    xin = nc.dram_tensor("xin", [D, T], F32 if in_dtype == "f32" else BF16,
                         kind="ExternalInput").ap()
    names = ["wqT", "wkT", "wvT", "woT"]
    dr = {n: nc.dram_tensor(n, [D, D], F32, kind="ExternalInput").ap()
          for n in names}
    dr["cd_dmask"] = nc.dram_tensor("cd_dmask", [128, 512], BF16,
                                    kind="ExternalInput").ap()
    dr["xo"] = nc.dram_tensor("xo", [D, T], BF16, kind="ExternalOutput").ap()
    dr["s_k"] = nc.dram_tensor("s_k", [D, T], BF16, kind="Internal").ap()
    dr["s_y"] = nc.dram_tensor("s_y", [D, T], BF16, kind="Internal").ap()
    with es:
        p, sems = make_prog(nc, es)
        C = Consts(nc, es, p)
        dr["xin"] = xin
        dr["cast"] = (in_dtype == "f32")
        emit_l0_attn_a(nc, es, p, C, dr, "aa")
        p.add("gpsimd", lambda e: e.engine_nop(), waits=p.all_tokens(), inc=1)
        p.finalize(sems)
    return nc


def emit_l0_attn_b(nc, es, p, C, dram, tag, psum=None):
    CH = 256
    NC_ = T // CH
    PSUM = psum if psum is not None else ps(nc, es, f"{tag}_psum", [128, 4096])
    B = lambda i, w=TC: PSUM[:, i * TC:i * TC + w]

    xs = XStream(nc, es, p, f"{tag}xs", dram["xin"],
                 q="gpsimd")
    ln = LNChunk(nc, es, p, C, f"{tag}ln", rpl_ps=B(2), mpl_ps=B(3),
                 mps_ap=PSUM[0:1, 0:TC], qps_ap=PSUM[0:1, TC:2 * TC])
    wq = [sb(nc, es, f"{tag}_wq{i}", [128, D], BF16) for i in range(PT)]
    wk = [sb(nc, es, f"{tag}_wk{i}", [128, D], BF16) for i in range(PT)]
    wv = [sb(nc, es, f"{tag}_wv{i}", [128, D], BF16) for i in range(PT)]
    ktt = [sb(nc, es, f"{tag}_kt{i}", [128, T], BF16) for i in range(PT)]
    ybuf = sb(nc, es, f"{tag}_yb", [128, PT * TC], BF16)
    qc_ = [sb(nc, es, f"{tag}_qc{i}", [128, TC], BF16) for i in range(2)]
    vcb = [sb(nc, es, f"{tag}_vc{i}", [128, D], BF16) for i in range(2)]
    bqv, tbq = load_vec_pf(nc, es, p, f"{tag}_bq", dram["bq"], PT)
    bkv, tbk = load_vec_pf(nc, es, p, f"{tag}_bk", dram["bk"], PT)
    bov, tbo = load_vec_pf(nc, es, p, f"{tag}_bo", dram["bo"], PT)
    bvrow = sb(nc, es, f"{tag}_bvr", [1, D], BF16)
    tbv = p.dma("gpsimd", bvrow[:], dram["bv"].rearrange("(o k) -> o k", o=1))

    def load_w(dram_w, bufs, waits=()):
        dv = dram_w.rearrange("(a p) k -> a p k", p=128)
        return [p.dma("gpsimd", bufs[i][:], dv[i], waits=list(waits) or None
                      if False else list(waits)) for i in range(PT)]

    twq = load_w(dram["Wq"], wq)
    twk = load_w(dram["Wk"], wk)
    twv = load_w(dram["Wv"], wv)
    sq_d = dram["s_q"].rearrange("(a p) t -> p a t", p=128)
    sv_d = dram["s_v"]
    xo = dram["xo"]

    ps_user = [None, None]
    qc_user = [None, None]
    vc_user = [None, None, None]
    b1_war = []
    ytoks = {}
    for c in range(NCH):
        bi = xs.load(c)
        ln.stats_and_planes([xs.tile(bi, i) for i in range(PT)])
        last = None
        for i in range(PT):
            xap, xtk = xs.tile(bi, i)
            last = ln.norm(i, xap, xtk, ybuf[:, i * TC:(i + 1) * TC], war=b1_war)
        xs.use(bi, last)
        ytoks[c] = last
        for which, wt, wtk, bias, btok in (("q", wq, twq, bqv, tbq),
                                           ("k", wk, twk, bkv, tbk)):
            for fo in range(PT):
                pb = fo % 2
                mmt = None
                for kt in range(PT):
                    w = [last, wtk[-1]] + ([ps_user[pb]] if kt == 0 else [])
                    mmt = p.mm(B(4 + pb), wt[kt][:, fo * 128:(fo + 1) * 128],
                               ybuf[:, kt * TC:(kt + 1) * TC], start=(kt == 0),
                               stop=(kt == PT - 1), waits=w)
                if which == "q":
                    et = p.act(qc_[pb][:], B(4 + pb), AF.Identity,
                               bias=bias[:, fo:fo + 1],
                               waits=[mmt, btok, qc_user[pb]])
                    qc_user[pb] = p.dma("sync",
                                        dram["s_q"][fo * 128:(fo + 1) * 128,
                                                    c * TC:(c + 1) * TC],
                                        qc_[pb][:], waits=[et])
                    ps_user[pb] = et
                else:
                    et = p.act(ktt[fo][:, c * TC:(c + 1) * TC], B(4 + pb),
                               AF.Identity, bias=bias[:, fo:fo + 1],
                               waits=[mmt, btok])
                    ps_user[pb] = et
        for tsub in range(TC // 128):
            vb = vcb[tsub % 2]
            evts = []
            for half in range(2):
                pb = half
                mmt = None
                for kt in range(PT):
                    w = [last, twv[-1]] + ([vc_user[pb]] if kt == 0 else [])
                    mmt = p.mm(B(6 + pb), ybuf[:, kt * TC + tsub * 128:
                                               kt * TC + tsub * 128 + 128],
                               wv[kt][:, half * TC:(half + 1) * TC],
                               start=(kt == 0), stop=False, waits=w)
                mmt = p.mm(B(6 + pb), C.ones_row_bf[:],
                           bvrow[:, half * TC:(half + 1) * TC],
                           start=False, stop=True,
                           waits=[tbv, C.tok_ones_row_bf])
                et = p.copy("vector", vb[:, half * TC:(half + 1) * TC],
                            B(6 + pb), waits=[mmt, vc_user[2]])
                vc_user[pb] = et
                evts.append(et)
            vc_user[2] = p.dma("sync",
                               sv_d[c * TC + tsub * 128:
                                    c * TC + tsub * 128 + 128, :],
                               vb[:], waits=evts)
        b1_war = [mmt, ps_user[0], ps_user[1]]

    wo = wq
    two = load_w(dram["Wo"], wo, waits=[ps_user[0], ps_user[1]])
    qnb = sb(nc, es, f"{tag}_qnb", [128, PT * CH], BF16)
    vnb = sb(nc, es, f"{tag}_vnb", [128, 2 * D], BF16)
    pacc = sb(nc, es, f"{tag}_pacc", [128, 2 * CH], F32)
    pbf = sb(nc, es, f"{tag}_pbf", [128, 2 * CH], BF16)
    ptb = sb(nc, es, f"{tag}_ptb", [128, 2 * CH], BF16)
    esb = [sb(nc, es, f"{tag}_esb{i}", [128, CH], F32) for i in range(2)]
    zcol = sb(nc, es, f"{tag}_z", [128, 4], F32)
    ostg = sb(nc, es, f"{tag}_ostg", [128, PT * CH], BF16)
    oxb = [sb(nc, es, f"{tag}_oxb{i}", [128, CH], BF16) for i in range(2)]
    xnb = xs.bufs[0][:, :PT * CH]
    phase_v = ("vector", p.counts["vector"])
    phase_s = ("scalar", p.counts["scalar"])
    sps_user = [phase_v, phase_s]
    s_user = {0: [phase_v, phase_s], 1: [phase_v, phase_s]}
    o_user = {0: [phase_v, phase_s], 1: [phase_v, phase_s]}
    t_user = [phase_v]
    esb_user = [None, None]
    pacc_copied = None
    ptb_user = None
    ostg_user = None
    oxb_user = [None, None]
    qn_user = None
    vn_user = None
    xn_user = None
    tpb = PSUM[:, 6 * TC:6 * TC + 128].bitcast(BF16)[:, 0:128]
    for n in range(NC_):
        ns = slice(n * CH, (n + 1) * CH)
        qlt = p.dma("sync", qnb[:].rearrange("p (a k) -> p a k", a=PT),
                    sq_d[:, :, ns], waits=[qn_user, qc_user[0], qc_user[1]])
        vlt = p.dma("sync", vnb[:].rearrange("p (a k) -> p a k", a=2),
                    sv_d.rearrange("(a p) e -> p a e", p=128)[:,
                        2 * n:2 * n + 2, :],
                    waits=[vn_user, vc_user[2]])
        xlt = p.dma("gpsimd", xnb.rearrange("p (a k) -> p a k", a=PT),
                    xs.dram[:, :, ns], waits=[xn_user])
        zt = p.memset("vector", pacc[:], 0.0, waits=[pacc_copied])
        acc_t = {0: zt, 1: zt}
        for mp in range(NC_ // 2):
            for qsub in range(2):
                pb = qsub
                mmt = None
                for et_ in range(PT):
                    w = [qlt] + (s_user[pb] if et_ == 0 else [])
                    mmt = p.mm(B(pb), qnb[:, et_ * CH + qsub * 128:
                                           et_ * CH + qsub * 128 + 128],
                               ktt[et_][:, mp * 512:(mp + 1) * 512],
                               start=(et_ == 0), stop=(et_ == PT - 1), waits=w)
                for half in range(2):
                    m = 2 * mp + half
                    eb = esb[half]
                    ext = p.act(eb[:], PSUM[:, pb * TC + half * CH:
                                            pb * TC + (half + 1) * CH],
                                AF.Exp, bias=C.sc(-SHIFT), scale=C.sc(0.03125),
                                accum_out=zcol[:, half:half + 1],
                                waits=[mmt, C.tok_scal, esb_user[half]])
                    rc = p.recip(zcol[:, 2 + half:3 + half],
                                 zcol[:, half:half + 1], waits=[ext])
                    at = p.stt("vector", pacc[:, qsub * CH:(qsub + 1) * CH],
                               eb[:], zcol[:, 2 + half:3 + half],
                               pacc[:, qsub * CH:(qsub + 1) * CH],
                               ALU.mult, ALU.add, waits=[rc, acc_t[qsub]])
                    if m == n:
                        at = p.stt("vector", pacc[:, qsub * CH:(qsub + 1) * CH],
                                   eb[:], zcol[:, 2 + half:3 + half],
                                   pacc[:, qsub * CH:(qsub + 1) * CH],
                                   ALU.mult, ALU.add, waits=[at])
                    acc_t[qsub] = at
                    esb_user[half] = at
                s_user[pb] = [ext]
        cpt = p.copy("vector", pbf[:], pacc[:], waits=[acc_t[0], acc_t[1],
                                                       ptb_user])
        pacc_copied = cpt
        ttoks = []
        for qsub in range(2):
            for ksub in range(2):
                tt = p.transpose(tpb, pbf[:, qsub * CH + ksub * 128:
                                          qsub * CH + ksub * 128 + 128],
                                 C.ident[:], waits=[cpt, C.tok_ident] + t_user)
                ev = p.copy("vector", ptb[:, ksub * CH + qsub * 128:
                                          ksub * CH + qsub * 128 + 128],
                            tpb, waits=[tt])
                t_user = [ev]
                ttoks.append(ev)
        otoks = []
        for eo in range(PT):
            pb = eo % 2
            mmt = None
            for ksub in range(2):
                w = [vlt] + ttoks + (o_user[pb] if ksub == 0 else [])
                mmt = p.mm(PSUM[:, (2 + pb) * TC:(2 + pb) * TC + CH],
                           vnb[:, ksub * D + eo * 128:ksub * D + eo * 128 + 128],
                           ptb[:, ksub * CH:(ksub + 1) * CH],
                           start=(ksub == 0), stop=(ksub == 1), waits=w)
            ev = p.copy("vector", ostg[:, eo * CH:(eo + 1) * CH],
                        PSUM[:, (2 + pb) * TC:(2 + pb) * TC + CH],
                        waits=[mmt, ostg_user])
            o_user[pb] = [ev]
            otoks.append(ev)
        ptb_user = mmt
        for fo in range(PT):
            pb = fo % 2
            mmt = None
            for kt in range(PT):
                w = [two[-1]] + otoks + ([ps_user[pb]] if kt == 0 else [])
                mmt = p.mm(B(4 + pb, CH), wo[kt][:, fo * 128:(fo + 1) * 128],
                           ostg[:, kt * CH:(kt + 1) * CH], start=(kt == 0),
                           stop=(kt == PT - 1), waits=w)
            et = p.act(esb[pb][:], B(4 + pb, CH), AF.Identity,
                       bias=bov[:, fo:fo + 1], waits=[mmt, tbo, esb_user[pb]])
            ps_user[pb] = et
            rt = p.tt("vector", oxb[pb][:], esb[pb][:],
                      xnb[:, fo * CH:(fo + 1) * CH], ALU.add,
                      waits=[et, xlt, oxb_user[pb]])
            esb_user[pb] = rt
            oxb_user[pb] = p.dma("sync", xo[fo * 128:(fo + 1) * 128, ns],
                                 oxb[pb][:], waits=[rt])
        ostg_user = mmt
        qn_user = mmt
        vn_user = mmt
        xn_user = rt


def build_l0_attn_b_program(in_dtype="bf16"):
    nc = bass.Bass("TRN2", target_bir_lowering=False)
    es = contextlib.ExitStack()
    xin = nc.dram_tensor("xin", [D, T], F32 if in_dtype == "f32" else BF16,
                         kind="ExternalInput").ap()
    dr = {n: nc.dram_tensor(n, [D, D], F32, kind="ExternalInput").ap()
          for n in ["Wq", "Wk", "Wv", "Wo"]}
    for n in ["bq", "bk", "bv"]:
        dr[n] = nc.dram_tensor(n, [D], F32, kind="ExternalInput").ap()
    dr["bq_pf"] = None
    dr["bq"] = dr["bq"]
    dr["bo"] = nc.dram_tensor("bo", [128, PT], F32, kind="ExternalInput").ap()
    dr["xo"] = nc.dram_tensor("xo", [D, T], BF16, kind="ExternalOutput").ap()
    dr["s_q"] = nc.dram_tensor("s_q", [D, T], BF16, kind="Internal").ap()
    dr["s_v"] = nc.dram_tensor("s_v", [T, D], BF16, kind="Internal").ap()
    with es:
        p, sems = make_prog(nc, es)
        C = Consts(nc, es, p)
        dr["xin"] = xin
        dr["cast"] = (in_dtype == "f32")
        emit_l0_attn_b(nc, es, p, C, dr, "ab")
        p.add("gpsimd", lambda e: e.engine_nop(), waits=p.all_tokens(), inc=1)
        p.finalize(sems)
    return nc


LN16 = float(np.log(16.0))


def emit_l1_attn_a(nc, es, p, C, dram, tag, psum=None):
    PSUM = psum if psum is not None else ps(nc, es, f"{tag}_psum", [128, 4096])
    B = lambda i, w=TC: PSUM[:, i * TC:i * TC + w]
    xs = XStream(nc, es, p, f"{tag}xs", dram["xin"],
                 q="gpsimd")
    ln = LNChunk(nc, es, p, C, f"{tag}ln", rpl_ps=B(2), mpl_ps=B(3),
                 mps_ap=PSUM[0:1, 0:TC], qps_ap=PSUM[0:1, TC:2 * TC])
    wq = [sb(nc, es, f"{tag}_wq{i}", [128, D], BF16) for i in range(PT)]
    wk = [sb(nc, es, f"{tag}_wk{i}", [128, D], BF16) for i in range(PT)]
    wv = [sb(nc, es, f"{tag}_wv{i}", [128, D], BF16) for i in range(PT)]
    omg = sb(nc, es, f"{tag}_om", [64, 256], BF16)
    ybuf = sb(nc, es, f"{tag}_yb", [128, PT * TC], BF16)
    qcb = [sb(nc, es, f"{tag}_qc{i}", [128, TC], BF16) for i in range(2)]
    vcb = [sb(nc, es, f"{tag}_vc{i}", [128, D], BF16) for i in range(2)]

    def load_w(dram_w, bufs, waits=()):
        dv = dram_w.rearrange("(a p) k -> a p k", p=128)
        return [p.dma("gpsimd", bufs[i][:], dv[i], waits=list(waits))
                for i in range(PT)]

    twq = load_w(dram["Wq"], wq)
    twk = load_w(dram["Wk"], wk)
    twv = load_w(dram["Wv"], wv)
    tom = p.dma("gpsimd", omg[:], dram["omegaT"])
    sq_d = dram["s_q"]
    sk_d = dram["s_k"]
    sv_d = dram["s_v"]
    so_d = dram["s_o"]
    xo = dram["xo"]

    ps_user = [None, None]
    eb_user = [None, None]
    vc_user = [None, None, None]
    b1_war = []
    for c in range(NCH):
        bi = xs.load(c)
        ln.stats_and_planes([xs.tile(bi, i) for i in range(PT)])
        last = None
        for i in range(PT):
            xap, xtk = xs.tile(bi, i)
            last = ln.norm(i, xap, xtk, ybuf[:, i * TC:(i + 1) * TC], war=b1_war)
        xs.use(bi, last)
        for which, wt, wtk, sd in (("q", wq, twq, sq_d), ("k", wk, twk, sk_d)):
            for fo in range(PT):
                pb = fo % 2
                mmt = None
                for kt in range(PT):
                    w = [last, wtk[-1]] + ([ps_user[pb]] if kt == 0 else [])
                    mmt = p.mm(B(4 + pb), wt[kt][:, fo * 128:(fo + 1) * 128],
                               ybuf[:, kt * TC:(kt + 1) * TC], start=(kt == 0),
                               stop=(kt == PT - 1), waits=w)
                et = p.act(qcb[pb][:], B(4 + pb), AF.Identity,
                           waits=[mmt, eb_user[pb]])
                ps_user[pb] = et
                eb_user[pb] = p.dma("sync",
                                    sd[fo * 128:(fo + 1) * 128,
                                       c * TC:(c + 1) * TC],
                                    qcb[pb][:], waits=[et])
        for tsub in range(TC // 128):
            vb = vcb[tsub % 2]
            evts = []
            for half in range(2):
                pb = half
                mmt = None
                for kt in range(PT):
                    w = [last, twv[-1]] + ([vc_user[pb]] if kt == 0 else [])
                    mmt = p.mm(B(6 + pb), ybuf[:, kt * TC + tsub * 128:
                                               kt * TC + tsub * 128 + 128],
                               wv[kt][:, half * TC:(half + 1) * TC],
                               start=(kt == 0), stop=(kt == PT - 1), waits=w)
                et = p.copy("vector", vb[:, half * TC:(half + 1) * TC],
                            B(6 + pb), waits=[mmt, vc_user[2]])
                vc_user[pb] = et
                evts.append(et)
            vc_user[2] = p.dma("sync",
                               sv_d[c * TC + tsub * 128:
                                    c * TC + tsub * 128 + 128, :],
                               vb[:], waits=evts)
        b1_war = [mmt, ps_user[0], ps_user[1]]

    qh = sb(nc, es, f"{tag}_qh", [64, T], BF16)
    kh = sb(nc, es, f"{tag}_kh", [64, T], BF16)
    vh = sb(nc, es, f"{tag}_vh", [128, 32 * 64], BF16)
    sqs = sb(nc, es, f"{tag}_sqs", [64, T], F32)
    rowA = sb(nc, es, f"{tag}_rowA", [1, T], F32)
    nkcol = sb(nc, es, f"{tag}_nkc", [128, 32], F32)
    skp = sb(nc, es, f"{tag}_skp", [32, 128], F32)
    pkb = sb(nc, es, f"{tag}_pkb", [128, 32 * 256], BF16)
    pksum = sb(nc, es, f"{tag}_pks", [128, 2], BF16)
    kvb = sb(nc, es, f"{tag}_kvb", [128, 2 * 64], BF16)
    pqb = sb(nc, es, f"{tag}_pqb", [128, 2 * TC], BF16)
    nqr = sb(nc, es, f"{tag}_nqr", [1, TC], F32)
    zr = sb(nc, es, f"{tag}_zr", [1, TC], F32)
    zpl = sb(nc, es, f"{tag}_zpl", [64, TC], F32)
    of32 = sb(nc, es, f"{tag}_of32", [64, TC], F32)
    obf = sb(nc, es, f"{tag}_obf", [64, TC], BF16)
    nhalf = sb(nc, es, f"{tag}_nh", [128, 1], F32)
    t_nh = p.ts("vector", nhalf[:], C.ones_col[:], -0.5, None, ALU.mult,
                waits=[C.tok_ones_col])
    phase_v = ("vector", p.counts["vector"])
    phase_s = ("scalar", p.counts["scalar"])
    guards = [phase_v, phase_s]
    u = {k: guards for k in ("b01", "b2", "b45", "b6", "b7")}
    qh_user = kh_user = vh_user = None
    sqs_user = None
    rowA_user = None
    nk_user = None
    skp_user = None
    pk_user = None
    pq_user = None
    kv_user = None
    pks_user = None
    zr_user = None
    zpl_user = None
    of_user = None
    ob_user = None
    for h in range(16):
        qlt = p.dma("sync", qh[:], sq_d[h * 64:(h + 1) * 64, :],
                    waits=[qh_user, eb_user[0], eb_user[1]])
        klt = p.dma("sync", kh[:], sk_d[h * 64:(h + 1) * 64, :],
                    waits=[kh_user])
        vts = []
        for part in range(8):
            vts.append(p.dma("gpsimd",
                             vh[:, part * 4 * 64:(part + 1) * 4 * 64]
                             .rearrange("p (a k) -> p a k", a=4),
                             sv_d.rearrange("(a p) e -> p a e", p=128)
                             [:, part * 4:(part + 1) * 4,
                              h * 64:(h + 1) * 64],
                             waits=[vh_user, vc_user[2]]))
        sqt = p.act(sqs[:], kh[:], AF.Square, waits=[klt, sqs_user])
        rt = None
        for ch in range(NCH):
            mt = p.mm(PSUM[0:1, 2 * TC:2 * TC + TC], nhalf[0:64, :],
                      sqs[:, ch * TC:(ch + 1) * TC], True, True,
                      waits=[sqt, t_nh, rt] + (u["b2"] if ch == 0 else []))
            rt = p.ts("vector", rowA[:, ch * TC:(ch + 1) * TC],
                      PSUM[0:1, 2 * TC:2 * TC + TC], 1.0, -LN16, ALU.mult,
                      ALU.add, waits=[mt, rowA_user])
        rowA_user = rt
        u["b2"] = [rt]
        bt1 = p.dma("sync", dram["s_row"].rearrange("(o k) -> o k", o=1),
                    rowA[:], waits=[rt])
        bt2 = p.dma("sync", skp[:],
                    dram["s_row"].rearrange("(k p) -> k p", p=128),
                    waits=[bt1, skp_user])
        ttr = p.transpose(PSUM[:, 2 * TC:2 * TC + 32], skp[:],
                          C.identF[0:32, 0:32], waits=[bt2, C.tok_identF] +
                          u["b2"])
        bt2 = p.act(nkcol[:], PSUM[:, 2 * TC:2 * TC + 32], AF.Identity,
                    waits=[ttr, nk_user])
        skp_user = ttr
        u["b2"] = [bt2]
        pktoks = []
        for tt_ in range(32):
            pb = tt_ % 2
            w = [klt, tom] + (u["b01"] if tt_ < 2 else [])
            mt = p.mm(PSUM[:, pb * TC:pb * TC + 256],
                      kh[:, tt_ * 128:(tt_ + 1) * 128], omg[:], True, True,
                      waits=w + ([pktoks[-2]] if len(pktoks) >= 2 else []))
            et = p.act(pkb[:, tt_ * 256:(tt_ + 1) * 256],
                       PSUM[:, pb * TC:pb * TC + 256], AF.Exp,
                       bias=nkcol[:, tt_:tt_ + 1], waits=[mt, bt2, pk_user])
            pktoks.append(et)
        u["b01"] = [pktoks[-1], pktoks[-2]]
        for mh in range(2):
            mmt = None
            for tt_ in range(32):
                w = [pktoks[tt_], C.tok_ones_col_bf] + \
                    (u["b2"] if tt_ == 0 else [])
                mmt = p.mm(PSUM[:, 2 * TC:2 * TC + 1],
                           pkb[:, tt_ * 256 + mh * 128:tt_ * 256 + mh * 128 + 128],
                           C.ones_col_bf[:], start=(tt_ == 0), stop=(tt_ == 31),
                           waits=w)
            e1 = p.copy("vector", pksum[:, mh:mh + 1], PSUM[:, 2 * TC:2 * TC + 1],
                        waits=[mmt, pks_user])
            u["b2"] = [e1]
            pks_user = e1
            mmt = None
            for tt_ in range(32):
                w = [pktoks[tt_], vts[tt_ // 4]] + \
                    (u["b6"] if tt_ == 0 else [])
                mmt = p.mm(PSUM[:, 6 * TC:6 * TC + 64],
                           pkb[:, tt_ * 256 + mh * 128:tt_ * 256 + mh * 128 + 128],
                           vh[:, tt_ * 64:(tt_ + 1) * 64], start=(tt_ == 0),
                           stop=(tt_ == 31), waits=w)
            e2 = p.copy("vector", kvb[:, mh * 64:(mh + 1) * 64],
                        PSUM[:, 6 * TC:6 * TC + 64], waits=[mmt, kv_user])
            u["b6"] = [e2]
            kv_user = e2
        kv_done = kv_user
        pks_done = pks_user
        for ch in range(NCH):
            cs = slice(ch * TC, (ch + 1) * TC)
            sqt2 = p.act(sqs[:, 0:TC], qh[:, cs], AF.Square,
                         waits=[qlt, sqt if ch == 0 else None, sqs_user,
                                rowA_user])
            sqs_user = sqt2
            mt = p.mm(PSUM[0:1, 2 * TC:2 * TC + TC], nhalf[0:64, :],
                      sqs[:, 0:TC], True, True, waits=[sqt2, t_nh] + u["b2"])
            nt = p.ts("vector", nqr[:], PSUM[0:1, 2 * TC:2 * TC + TC], 1.0,
                      -LN16, ALU.mult, ALU.add, waits=[mt, zr_user])
            u["b2"] = [nt]
            pqtoks = []
            for mt_ in range(2):
                pb = 4 + mt_
                w = [qlt, tom] + u["b45"]
                m1 = p.mm(B(pb), omg[:, mt_ * 128:(mt_ + 1) * 128],
                          qh[:, cs], start=True, stop=False, waits=w)
                m2 = p.mm(B(pb), C.ones_row[:], nqr[:], start=False, stop=True,
                          waits=[nt, C.tok_ones_row])
                et = p.act(pqb[:, mt_ * TC:(mt_ + 1) * TC], B(pb), AF.Exp,
                           waits=[m2, pq_user])
                pqtoks.append(et)
            pq_user = None
            u["b45"] = pqtoks
            mmt = None
            for mt_ in range(2):
                w = [pqtoks[mt_], pks_done] + (u["b7"] if mt_ == 0 else [])
                mmt = p.mm(PSUM[0:1, 7 * TC:7 * TC + TC],
                           pksum[:, mt_:mt_ + 1], pqb[:, mt_ * TC:(mt_ + 1) * TC],
                           start=(mt_ == 0), stop=(mt_ == 1), waits=w)
            z1 = p.ts("vector", zr[:], PSUM[0:1, 7 * TC:7 * TC + TC], 1.0,
                      1e-6, ALU.mult, ALU.add, waits=[mmt, zr_user])
            u["b7"] = [z1]
            z2 = p.act(zr[:], zr[:], AF.Ln, waits=[z1])
            z3 = p.ts("vector", zr[:], zr[:], -1.0, None, ALU.mult, waits=[z2])
            z4 = p.act(zr[:], zr[:], AF.Exp, waits=[z3])
            zr_user = z4
            mmt = None
            for mt_ in range(2):
                w = [pqtoks[mt_], kv_done] + (u["b7"] if mt_ == 0 else [])
                mmt = p.mm(PSUM[0:64, 7 * TC:7 * TC + TC],
                           kvb[:, mt_ * 64:(mt_ + 1) * 64],
                           pqb[:, mt_ * TC:(mt_ + 1) * TC],
                           start=(mt_ == 0), stop=(mt_ == 1), waits=w)
            u["b7"] = [mmt]
            bt = p.mm(PSUM[0:64, 6 * TC:6 * TC + TC], C.ones_row[0:1, 0:64],
                      zr[:], True, True, waits=[z4, C.tok_ones_row] + u["b6"])
            e1 = p.act(zpl[:], PSUM[0:64, 6 * TC:6 * TC + TC], AF.Identity,
                       waits=[bt, zpl_user])
            u["b6"] = [e1]
            e2 = p.act(of32[:], PSUM[0:64, 7 * TC:7 * TC + TC], AF.Identity,
                       waits=[mmt, of_user])
            u["b7"] = [e2]
            e3 = p.tt("vector", obf[:], of32[:], zpl[:], ALU.mult,
                      waits=[e1, e2, ob_user])
            zpl_user = e3
            of_user = e3
            ob_user = p.dma("sync", so_d[h * 64:(h + 1) * 64, cs], obf[:],
                            waits=[e3])
        qh_user = pqtoks[-1]
        kh_user = pktoks[-1]
        vh_user = kv_done
        nk_user = pktoks[-1]
        pk_user = kv_done
    out_toks = []
    oxb = [sb(nc, es, f"{tag}_oxbf{i}", [128, TC], BF16) for i in range(2)]
    sob = ybuf
    oxb_user = [None, None]
    sob_user = [ob_user]
    for c in range(NCH):
        cs = slice(c * TC, (c + 1) * TC)
        bi = xs.load(c)
        slt = p.dma("sync", sob[:].rearrange("p (a k) -> p a k", a=PT),
                    so_d.rearrange("(a p) t -> p a t", p=128)[:, :, cs],
                    waits=sob_user)
        sob_user = []
        for i in range(PT):
            xap, xtk = xs.tile(bi, i)
            pb = i % 2
            rt = p.tt("vector", oxb[pb][:], sob[:, i * TC:(i + 1) * TC], xap,
                      ALU.add, waits=[slt, xtk, oxb_user[pb]])
            oxb_user[pb] = p.dma("sync", xo[i * 128:(i + 1) * 128, cs],
                                 oxb[pb][:], waits=[rt])
            sob_user.append(rt)
        xs.use(bi, rt)


def build_l1_attn_a_program(in_dtype="bf16"):
    nc = bass.Bass("TRN2", target_bir_lowering=False)
    es = contextlib.ExitStack()
    xin = nc.dram_tensor("xin", [D, T], F32 if in_dtype == "f32" else BF16,
                         kind="ExternalInput").ap()
    dr = {n: nc.dram_tensor(n, [D, D], F32, kind="ExternalInput").ap()
          for n in ["Wq", "Wk", "Wv"]}
    dr["omegaT"] = nc.dram_tensor("omegaT", [64, 256], F32,
                                  kind="ExternalInput").ap()
    dr["xo"] = nc.dram_tensor("xo", [D, T], BF16, kind="ExternalOutput").ap()
    for n in ["s_q", "s_k", "s_o"]:
        dr[n] = nc.dram_tensor(n, [D, T], BF16, kind="Internal").ap()
    dr["s_v"] = nc.dram_tensor("s_v", [T, D], BF16, kind="Internal").ap()
    dr["s_row"] = nc.dram_tensor("s_row", [T], F32, kind="Internal").ap()
    with es:
        p, sems = make_prog(nc, es)
        C = Consts(nc, es, p)
        dr["xin"] = xin
        dr["cast"] = (in_dtype == "f32")
        emit_l1_attn_a(nc, es, p, C, dr, "pa")
        p.add("gpsimd", lambda e: e.engine_nop(), waits=p.all_tokens(), inc=1)
        p.finalize(sems)
    return nc




def emit_l1_attn_b(nc, es, p, C, dram, tag, psum=None):
    IT = 16
    PSUM = psum if psum is not None else ps(nc, es, f"{tag}_psum", [128, 4096])
    B = lambda i, w=TC: PSUM[:, i * TC:i * TC + w]
    xs = XStream(nc, es, p, f"{tag}xs", dram["xin"],
                 q="gpsimd")
    wi = [sb(nc, es, f"{tag}_wi{i}", [128, 2 * D], BF16) for i in range(PT)]
    wi2 = [sb(nc, es, f"{tag}_wi2{i}", [128, 2 * D], BF16) for i in range(PT)]
    hbuf = sb(nc, es, f"{tag}_h", [128, IT * TC], BF16)
    sgb = sb(nc, es, f"{tag}_sg", [128, IT * TC], BF16)
    twi = [p.dma("gpsimd", wi[i][:],
                 dram["W_in"].rearrange("(a p) k -> a p k", p=128)[i][:, 0:2 * D])
           for i in range(PT)]
    twi2 = [p.dma("gpsimd", wi2[i][:],
                  dram["W_in"].rearrange("(a p) k -> a p k", p=128)[i][:, 2 * D:])
            for i in range(PT)]
    sh_d = dram["s_h"].rearrange("(a p) t -> p a t", p=128)
    sg_d = dram["s_g"].rearrange("(a p) t -> p a t", p=128)

    es1 = contextlib.ExitStack()
    ln = LNChunk(nc, es1, p, C, f"{tag}ln", rpl_ps=B(2), mpl_ps=B(3),
                 mps_ap=PSUM[0:1, 0:TC], qps_ap=PSUM[0:1, TC:2 * TC])
    ybuf = sb(nc, es1, f"{tag}_yb", [128, PT * TC], BF16)
    cw, tcw = load_vec_pf(nc, es1, p, f"{tag}_cw", dram["conv_w_pf"], IT * 3)
    cb, tcb = load_vec_pf(nc, es1, p, f"{tag}_cb", dram["conv_b"], IT)
    inb = [sb(nc, es1, f"{tag}_in{i}", [128, IT * TC], BF16) for i in range(2)]
    cscr = [sb(nc, es1, f"{tag}_cs{i}", [128, TC], F32) for i in range(2)]
    halo_p = sb(nc, es1, f"{tag}_hp", [128, IT], F32)
    halo_n = sb(nc, es1, f"{tag}_hn", [128, IT], F32)
    hp_t = p.memset("vector", halo_p[:], 0.0)
    hn_t0 = p.memset("vector", halo_n[:], 0.0)
    st = {"sh": None, "sg": None, "hp": hp_t, "hw": None, "hts": [None] * IT}

    def conv_emit(c, buf, etoks, hn_tok):
        hts = []
        for it in range(IT):
            s = cscr[it % 2]
            iap = buf[:, it * TC:(it + 1) * TC]
            w0c = cw[:, it * 3 + 0:it * 3 + 1]
            w1c = cw[:, it * 3 + 1:it * 3 + 2]
            w2c = cw[:, it * 3 + 2:it * 3 + 3]
            t1 = p.ts("vector", s[:], iap, w1c, None, ALU.mult,
                      waits=[etoks[it], tcw, hn_tok,
                             hts[-2] if len(hts) >= 2 else None])
            t2 = p.stt("vector", s[:, 1:TC], iap[:, 0:TC - 1], w0c, s[:, 1:TC],
                       ALU.mult, ALU.add, waits=[t1])
            t3 = p.stt("vector", s[:, 0:1], halo_p[:, it:it + 1], w0c,
                       s[:, 0:1], ALU.mult, ALU.add, waits=[t2, st["hp"]])
            t4 = p.stt("vector", s[:, 0:TC - 1], iap[:, 1:TC], w2c,
                       s[:, 0:TC - 1], ALU.mult, ALU.add, waits=[t3])
            t5 = p.stt("vector", s[:, TC - 1:TC], halo_n[:, it:it + 1], w2c,
                       s[:, TC - 1:TC], ALU.mult, ALU.add, waits=[t4])
            ht = p.act(hbuf[:, it * TC:(it + 1) * TC], s[:], AF.Silu,
                       bias=cb[:, it:it + 1], waits=[t5, tcb, st["sh"]])
            hts.append(ht)
        st["hp"] = p.copy("vector", halo_p[:],
                          buf[:].rearrange("p (a k) -> p a k", a=IT)[:, :, TC - 1],
                          waits=etoks + [hts[-1]])
        st["hw"] = st["hp"]
        st["sh"] = p.dma("sync", sh_d[:, :, c * TC:(c + 1) * TC],
                         hbuf[:].rearrange("p (a k) -> p a k", a=IT), waits=hts)
        st["hts"] = hts

    ps_user = [None, None]
    buf_prev = etoks_prev = None
    for c in range(NCH):
        buf = inb[c % 2]
        bi = xs.load(c)
        ln.stats_and_planes([xs.tile(bi, i) for i in range(PT)])
        last = None
        for i in range(PT):
            xap, xtk = xs.tile(bi, i)
            last = ln.norm(i, xap, xtk, ybuf[:, i * TC:(i + 1) * TC],
                           war=[st["hw"], ps_user[0], ps_user[1]])
        xs.use(bi, last)
        etoks = []
        for fo in range(IT):
            pb = fo % 2
            mmt = None
            for kt in range(PT):
                w = [last, twi[-1]] + ([ps_user[pb]] if kt == 0 else [])
                mmt = p.mm(B(4 + pb), wi[kt][:, fo * 128:(fo + 1) * 128],
                           ybuf[:, kt * TC:(kt + 1) * TC], start=(kt == 0),
                           stop=(kt == PT - 1), waits=w)
            et = p.copy("vector", buf[:, fo * TC:(fo + 1) * TC], B(4 + pb),
                        waits=[mmt, st["hts"][fo] if c >= 2 else None])
            ps_user[pb] = et
            etoks.append(et)
        gts = []
        for fo in range(IT):
            pb = fo % 2
            mmt = None
            for kt in range(PT):
                w = [last, twi2[-1]] + ([ps_user[pb]] if kt == 0 else [])
                mmt = p.mm(B(4 + pb), wi2[kt][:, fo * 128:(fo + 1) * 128],
                           ybuf[:, kt * TC:(kt + 1) * TC], start=(kt == 0),
                           stop=(kt == PT - 1), waits=w)
            et = p.act(sgb[:, fo * TC:(fo + 1) * TC], B(4 + pb), AF.Silu,
                       waits=[mmt, st["sg"]])
            ps_user[pb] = et
            gts.append(et)
        st["sg"] = p.dma("sync", sg_d[:, :, c * TC:(c + 1) * TC],
                         sgb[:].rearrange("p (a k) -> p a k", a=IT), waits=gts)
        if c > 0:
            hn = p.copy("vector", halo_n[:], buf[:].rearrange(
                "p (a k) -> p a k", a=IT)[:, :, 0], waits=etoks + [st["hw"]])
            conv_emit(c - 1, buf_prev, etoks_prev, hn)
        buf_prev, etoks_prev = buf, etoks
    hn2 = p.memset("vector", halo_n[:], 0.0, waits=etoks_prev + [st["hw"]])
    conv_emit(NCH - 1, buf_prev, etoks_prev, hn2)
    es1.close()
    p.barrier()

    wt = wi + wi2
    twt = [p.dma("gpsimd", wt[i][:],
                 dram["W_time"].rearrange("(a p) k -> a p k", p=128)[i])
           for i in range(IT)]
    wst = sb(nc, es, f"{tag}_wst", [128, 16 * 32], BF16)
    twst = p.dma("gpsimd", wst[:], dram["W_state_pf"])
    wob = [sb(nc, es, f"{tag}_wob{i}", [128, D], BF16) for i in range(2)]
    wo_d = dram["W_out"].rearrange("(a p) k -> a p k", p=128)
    bsel = sb(nc, es, f"{tag}_bsel", [32, 16 * 128], BF16)
    tbsel = p.dma("sync", bsel[:], dram["cd_bsel"])
    alog, talog = load_vec_pf(nc, es, p, f"{tag}_al", dram["A_log_pf"], 256)
    btv, tbt = load_vec_pf(nc, es, p, f"{tag}_bt", dram["b_time"], IT)
    skv, tskv = load_vec_pf(nc, es, p, f"{tag}_sk", dram["skip"], IT)
    A_sb = sb(nc, es, f"{tag}_A", [128, 256], F32)
    ta1 = p.act(A_sb[:], alog[:], AF.Exp, waits=[talog])
    ta2 = p.ts("vector", A_sb[:], A_sb[:], -1.0, None, ALU.mult, waits=[ta1])
    carry = sb(nc, es, f"{tag}_carry", [128, 256], F32)
    tc0 = p.memset("vector", carry[:], 0.0)
    hb2 = hbuf
    gb2 = sgb
    HT = 8
    dtb = sb(nc, es, f"{tag}_dt", [128, HT * TC], F32)
    dhb = sb(nc, es, f"{tag}_dh", [128, HT * TC], BF16)
    yac = sb(nc, es, f"{tag}_yac", [128, HT * TC], F32)
    ygb = sb(nc, es, f"{tag}_yg", [128, IT * TC], BF16)
    spsb = sb(nc, es, f"{tag}_spsb", [32, TC], BF16)
    bpl = [sb(nc, es, f"{tag}_bpl{i}", [128, TC], F32) for i in range(2)]
    dab = [sb(nc, es, f"{tag}_da{i}", [128, TC], F32) for i in range(2)]
    dbb = [sb(nc, es, f"{tag}_db{i}", [128, TC], F32) for i in range(2)]
    mem = [sb(nc, es, f"{tag}_mem{i}", [128, TC], F32) for i in range(2)]
    expd = sb(nc, es, f"{tag}_expd", [128, TC], F32)
    oxb = [sb(nc, es, f"{tag}_oxb{i}", [128, TC], BF16) for i in range(2)]
    xo = dram["xo"]
    u = {i: [] for i in range(8)}
    h_user = None
    g_user = None
    sp_user = None
    bpl_user = [None, None]
    da_user = [None, None]
    db_user = [None, None]
    mem_user = [None, None]
    yg_user = [None] * IT
    exp_user = None
    oxb_user = [None, None]
    wob_user = [None, None]
    carry_rd = tc0
    for c in range(NCH):
        cs = slice(c * TC, (c + 1) * TC)
        hlt = p.dma("sync", hb2[:].rearrange("p (a k) -> p a k", a=IT),
                    sh_d[:, :, cs], waits=[h_user])
        glt = p.dma("sync", gb2[:].rearrange("p (a k) -> p a k", a=IT),
                    sg_d[:, :, cs], waits=[g_user])
        xbi = xs.load(c)
        mmt = None
        for kt in range(IT):
            w = [hlt, twst] + (u[6] if kt == 0 else [])
            mmt = p.mm(PSUM[0:32, 6 * TC:6 * TC + TC],
                       wst[:, kt * 32:(kt + 1) * 32],
                       hb2[:, kt * TC:(kt + 1) * TC], start=(kt == 0),
                       stop=(kt == IT - 1), waits=w)
        spt = p.copy("vector", spsb[:], PSUM[0:32, 6 * TC:6 * TC + TC],
                     waits=[mmt, sp_user])
        u[6] = [spt]
        for half in range(2):
            dt_user = [None] * HT
            dh_user = [None] * HT
            yac_user = [None] * HT
            for f_ in range(HT):
                fo = half * HT + f_
                pb = fo % 2
                mmt = None
                for kt in range(IT):
                    w = [hlt, twt[-1]] + (u[4 + pb] if kt == 0 else [])
                    mmt = p.mm(B(4 + pb), wt[kt][:, fo * 128:(fo + 1) * 128],
                               hb2[:, kt * TC:(kt + 1) * TC], start=(kt == 0),
                               stop=(kt == IT - 1), waits=w)
                e1 = p.act(expd[:], B(4 + pb), AF.Exp, bias=btv[:, fo:fo + 1],
                           waits=[mmt, tbt, exp_user])
                u[4 + pb] = [e1]
                e2 = p.act(dtb[:, f_ * TC:(f_ + 1) * TC], expd[:], AF.Ln,
                           bias=C.sc(1.0),
                           waits=[e1, C.tok_scal, yg_user[fo]])
                exp_user = e2
                dt_user[f_] = e2
                e3 = p.tt("vector", dhb[:, f_ * TC:(f_ + 1) * TC],
                          dtb[:, f_ * TC:(f_ + 1) * TC],
                          hb2[:, f_ * TC:(f_ + 1) * TC][:, :] if False else
                          hb2[:, fo * TC:(fo + 1) * TC], ALU.mult,
                          waits=[e2, yg_user[fo]])
                dh_user[f_] = e3
            for n in range(16):
                nb = n % 2
                bt_ = p.mm(PSUM[:, (6 + nb) * TC:(7 + nb) * TC],
                           bsel[:, n * 128:(n + 1) * 128], spsb[:], True, True,
                           waits=[spt, tbsel] + u[6 + nb])
                bev = p.act(bpl[nb][:], PSUM[:, (6 + nb) * TC:(7 + nb) * TC],
                            AF.Identity, waits=[bt_, bpl_user[nb]])
                u[6 + nb] = [bev]
                bpl_user[nb] = None
                for f_ in range(HT):
                    fo = half * HT + f_
                    ib = f_ % 2
                    da_t = p.act(dab[ib][:], dtb[:, f_ * TC:(f_ + 1) * TC],
                                 AF.Exp,
                                 scale=A_sb[:, fo * 16 + n:fo * 16 + n + 1],
                                 waits=[dt_user[f_], ta2, da_user[ib]])
                    db_t = p.tt("vector", dbb[ib][:],
                                dhb[:, f_ * TC:(f_ + 1) * TC], bpl[nb][:],
                                ALU.mult, waits=[dh_user[f_], bev, db_user[ib]])
                    sc_t = p.scan("vector", mem[ib][:], dab[ib][:], dbb[ib][:],
                                  carry[:, fo * 16 + n:fo * 16 + n + 1],
                                  waits=[da_t, db_t, mem_user[ib], carry_rd])
                    da_user[ib] = sc_t
                    db_user[ib] = sc_t
                    cu = p.copy("vector",
                                carry[:, fo * 16 + n:fo * 16 + n + 1],
                                mem[ib][:, TC - 1:TC], waits=[sc_t])
                    carry_rd = cu
                    if n == 0:
                        ya = p.stt("vector", yac[:, f_ * TC:(f_ + 1) * TC],
                                   hb2[:, fo * TC:(fo + 1) * TC],
                                   skv[:, fo:fo + 1], mem[ib][:], ALU.mult,
                                   ALU.add, waits=[sc_t, tskv, hlt,
                                                   yac_user[f_], yg_user[fo]])
                    else:
                        ya = p.tt("vector", yac[:, f_ * TC:(f_ + 1) * TC],
                                  yac[:, f_ * TC:(f_ + 1) * TC], mem[ib][:],
                                  ALU.add, waits=[sc_t, yac_user[f_]])
                    yac_user[f_] = ya
                    mem_user[ib] = ya
                    bpl_user[nb] = db_t
            for f_ in range(HT):
                fo = half * HT + f_
                t = p.tt("vector", ygb[:, fo * TC:(fo + 1) * TC],
                         yac[:, f_ * TC:(f_ + 1) * TC],
                         gb2[:, fo * TC:(fo + 1) * TC], ALU.mult,
                         waits=[yac_user[f_], glt, yg_user[fo]])
                yg_user[fo] = t
        out_mm = [None] * PT
        for kt in range(IT):
            wb = wob[kt % 2]
            wlt = p.dma("gpsimd", wb[:], wo_d[kt], waits=[wob_user[kt % 2]])
            for fo in range(PT):
                w = [wlt, yg_user[kt]]
                if kt == 0:
                    w += u[fo] + [("vector", p.counts["vector"])] if False \
                        else u[fo]
                out_mm[fo] = p.mm(B(fo), wb[:, fo * 128:(fo + 1) * 128],
                                  ygb[:, kt * TC:(kt + 1) * TC],
                                  start=(kt == 0), stop=(kt == IT - 1),
                                  waits=w)
            wob_user[kt % 2] = out_mm[-1]
        for fo in range(PT):
            pb = fo % 2
            xap, xtk = xs.tile(xbi, fo)
            rt = p.tt("vector", oxb[pb][:], B(fo), xap, ALU.add,
                      waits=[out_mm[fo], xtk, oxb_user[pb]])
            u[fo] = [rt]
            oxb_user[pb] = p.dma("sync", xo[fo * 128:(fo + 1) * 128, cs],
                                 oxb[pb][:], waits=[rt])
        xs.use(xbi, rt)
        h_user = out_mm[-1]
        g_user = out_mm[-1]
        for fo in range(IT):
            yg_user[fo] = out_mm[-1]
        sp_user = bt_


def build_l1_attn_b_program(in_dtype="bf16"):
    nc = bass.Bass("TRN2", target_bir_lowering=False)
    es = contextlib.ExitStack()
    xin = nc.dram_tensor("xin", [D, T], F32 if in_dtype == "f32" else BF16,
                         kind="ExternalInput").ap()
    dr = {}
    dr["W_in"] = nc.dram_tensor("W_in", [D, 4 * D], F32, kind="ExternalInput").ap()
    dr["W_time"] = nc.dram_tensor("W_time", [2 * D, 2 * D], F32,
                                  kind="ExternalInput").ap()
    dr["W_state_pf"] = nc.dram_tensor("W_state_pf", [128, 512], F32,
                                      kind="ExternalInput").ap()
    dr["W_out"] = nc.dram_tensor("W_out", [2 * D, D], F32,
                                 kind="ExternalInput").ap()
    for n, sz in [("conv_w_pf", 2 * D * 3), ("conv_b", 2 * D),
                  ("b_time", 2 * D), ("skip", 2 * D), ("A_log_pf", 2 * D * 16)]:
        dr[n] = nc.dram_tensor(n, [128, sz // 128], F32,
                               kind="ExternalInput").ap()
    dr["cd_bsel"] = nc.dram_tensor("cd_bsel", [32, 2048], BF16,
                                   kind="ExternalInput").ap()
    dr["xo"] = nc.dram_tensor("xo", [D, T], BF16, kind="ExternalOutput").ap()
    dr["s_h"] = nc.dram_tensor("s_h", [2 * D, T], BF16, kind="Internal").ap()
    dr["s_g"] = nc.dram_tensor("s_g", [2 * D, T], BF16, kind="Internal").ap()
    with es:
        p, sems = make_prog(nc, es)
        C = Consts(nc, es, p)
        dr["xin"] = xin
        dr["cast"] = (in_dtype == "f32")
        emit_l1_attn_b(nc, es, p, C, dr, "sb")
        p.add("gpsimd", lambda e: e.engine_nop(), waits=p.all_tokens(), inc=1)
        p.finalize(sems)
    return nc


FFN_PHASES = 3


def emit_l0_ffn(nc, es, p, C, dram, tag, psum=None):
    PSUM = psum if psum is not None else ps(nc, es, f"{tag}_psum", [128, 4096])
    B = lambda i, w=TC: PSUM[:, i * TC:i * TC + w]
    xs = XStream(nc, es, p, f"{tag}xs", dram["xin"],
                 q="gpsimd")
    iwv, t_iwv = load_vec_pf(nc, es, p, f"{tag}_iw",
                             dram["integration_weight"], PT)
    es_w = contextlib.ExitStack()

    def ldw(name, nt, width, es=es_w):
        t = [sb(nc, es, f"{tag}_{name}{i}", [128, width], BF16)
             for i in range(nt)]
        dv = dram[name].rearrange("(a p) k -> a p k", p=128)
        tk = None
        for i in range(nt):
            tk = p.dma("gpsimd", t[i][:], dv[i])
        return t, tk

    rw1, t_rw1 = ldw("router_w1", PT, 256)
    hw1, t_hw1 = ldw("halt_w1", PT, 256)
    sw1, t_sw1 = ldw("shared_w1", 16, 64)
    w1a, t_w1a = ldw("W1_all", 16, 512)
    w2a, t_w2a = ldw("W2_all", 4, 3072)
    rw2, t_rw2 = ldw("router_w2", 2, 8)
    hw2, t_hw2 = ldw("halt_w2", 2, 1)
    sw2 = sb(nc, es_w, f"{tag}_sw2", [64, 3072], BF16)
    t_sw2 = p.dma("gpsimd", sw2[:], dram["shared_w2"])
    b2a = sb(nc, es_w, f"{tag}_b2a", [8, 3072], BF16)
    t_b2a = p.dma("gpsimd", b2a[:], dram["expert_b2"])
    epat = sb(nc, es_w, f"{tag}_epat", [8, 4 * 128], BF16)
    t_epat = p.dma("sync", epat[:], dram["cd_epat"])
    rb2r = sb(nc, es_w, f"{tag}_rb2r", [1, 8], BF16)
    t_rb2r = p.dma("gpsimd", rb2r[:],
                   dram["router_b2"].rearrange("(o k) -> o k", o=1))
    rb1, t_rb1 = load_vec_pf(nc, es_w, p, f"{tag}_rb1", dram["router_b1"], 2)
    hb1, t_hb1 = load_vec_pf(nc, es_w, p, f"{tag}_hb1", dram["halt_b1"], 2)
    b1a, t_b1a = load_vec_pf(nc, es_w, p, f"{tag}_b1a", dram["expert_b1"], 4)
    sb2v, t_sb2v = load_vec_pf(nc, es_w, p, f"{tag}_sb2", dram["shared_b2"], 24)
    muv, t_muv = load_vec_pf(nc, es_w, p, f"{tag}_mu", dram["mu"], PT)
    sb1c = sb(nc, es_w, f"{tag}_sb1c", [128, 1], F32)
    t_sb1c = p.dma("sync", sb1c[0:64, :],
                   dram["shared_b1"].rearrange("(k o) -> k o", o=1))
    hb2c = sb(nc, es_w, f"{tag}_hb2c", [1, 1], F32)
    t_hb2c = p.dma("sync", hb2c[:],
                   dram["halt_b2"].rearrange("(o k) -> o k", o=1))
    swsc = sb(nc, es_w, f"{tag}_swsc", [1, 1], F32)
    t_swsc = p.dma("sync", swsc[:],
                   dram["shared_weight"].rearrange("(o k) -> o k", o=1))
    swrow = sb(nc, es_w, f"{tag}_swrow", [1, 1], F32)
    t_sig = p.act(swrow[:], swsc[:], AF.Sigmoid, waits=[t_swsc])
    swcol = sb(nc, es_w, f"{tag}_swcol", [128, 1], F32)
    t_swp = p.mm(PSUM[:, 7 * TC:7 * TC + 1], C.ones_row[:], swrow[:], True,
                 True, waits=[t_sig, C.tok_ones_row])
    t_swc = p.act(swcol[:], PSUM[:, 7 * TC:7 * TC + 1], AF.Identity,
                  waits=[t_swp])
    maskT = [sb(nc, es_w, f"{tag}_mT{j}", [8, T], BF16) for j in range(2)]
    wrow = [sb(nc, es_w, f"{tag}_wr{j}", [1, T], BF16) for j in range(2)]
    sy = dram["s_y"].rearrange("(a p) t -> p a t", p=128)
    si = dram["s_i"].rearrange("(a p) t -> p a t", p=128)

    es_r = contextlib.ExitStack()
    ln = LNChunk(nc, es_r, p, C, f"{tag}ln", rpl_ps=B(2), mpl_ps=B(3),
                 mps_ap=PSUM[0:1, 0:TC], qps_ap=PSUM[0:1, TC:2 * TC])
    ybuf = sb(nc, es_r, f"{tag}_yb", [128, PT * TC], BF16)
    r1b = sb(nc, es_r, f"{tag}_r1b", [128, 2 * TC], BF16)
    pr8 = sb(nc, es_r, f"{tag}_pr8", [128, 8], F32)
    sc8 = sb(nc, es_r, f"{tag}_sc8", [128, 10], F32)
    mk8 = [sb(nc, es_r, f"{tag}_mk8{j}", [128, 8], F32) for j in range(2)]
    mtp = PSUM[0:8, 6 * TC + 256:6 * TC + 384]
    wtp = PSUM[0:1, 6 * TC + 384:6 * TC + 512]
    ps_user = [None, None]
    r1_user = [None, None]
    l8_user = None
    pr_user = None
    mt_user = None
    b1_war = []
    for c in range(NCH):
        bi = xs.load(c)
        ln.stats_and_planes([xs.tile(bi, i) for i in range(PT)])
        last = None
        for i in range(PT):
            xap, xtk = xs.tile(bi, i)
            last = ln.norm(i, xap, xtk, ybuf[:, i * TC:(i + 1) * TC],
                           war=b1_war)
        xs.use(bi, last)
        spt = p.dma("sync", sy[:, :, c * TC:(c + 1) * TC],
                    ybuf[:].rearrange("p (a k) -> p a k", a=PT), waits=[last])
        for fo in range(2):
            pb = fo % 2
            mmt = None
            for kt in range(PT):
                w = [last, t_rw1] + ([ps_user[pb]] if kt == 0 else [])
                mmt = p.mm(B(4 + pb), rw1[kt][:, fo * 128:(fo + 1) * 128],
                           ybuf[:, kt * TC:(kt + 1) * TC], start=(kt == 0),
                           stop=(kt == PT - 1), waits=w)
            et = p.act(r1b[:, fo * TC:(fo + 1) * TC], B(4 + pb), AF.Gelu,
                       bias=rb1[:, fo:fo + 1], waits=[mmt, t_rb1, r1_user[pb]])
            ps_user[pb] = et
            r1_user[pb] = et
        b1_war = [mmt, spt]
        for tsub in range(4):
            mm1 = None
            for kt in range(2):
                w = [r1_user[kt], t_rw2] +                     ([l8_user, mt_user] if kt == 0 else [])
                mm1 = p.mm(PSUM[:, 6 * TC:6 * TC + 8],
                           r1b[:, kt * TC + tsub * 128:
                               kt * TC + tsub * 128 + 128],
                           rw2[kt][:], start=(kt == 0), stop=False, waits=w)
            mm1 = p.mm(PSUM[:, 6 * TC:6 * TC + 8], C.ones_row_bf[:], rb2r[:],
                       start=False, stop=True,
                       waits=[t_rb2r, C.tok_ones_row_bf])
            s1 = p.reduce("vector", sc8[:, 0:1], PSUM[:, 6 * TC:6 * TC + 8],
                          ALU.max, waits=[mm1, pr_user], negate=True)
            s2 = p.act(pr8[:], PSUM[:, 6 * TC:6 * TC + 8], AF.Exp,
                       bias=sc8[:, 0:1], accum_out=sc8[:, 1:2], waits=[s1])
            l8_user = s2
            s3 = p.recip(sc8[:, 2:3], sc8[:, 1:2], waits=[s2])
            s4 = p.ts("vector", pr8[:], pr8[:], sc8[:, 2:3], None, ALU.mult,
                      waits=[s3])
            m1 = p.reduce("vector", sc8[:, 3:4], pr8[:], ALU.max, waits=[s4])
            k1 = p.ts("vector", mk8[0][:], pr8[:], sc8[:, 3:4], None,
                      ALU.is_equal, waits=[m1, mt_user])
            q2 = p.stt("vector", pr8[:], mk8[0][:], -1e9, pr8[:], ALU.mult,
                       ALU.add, waits=[k1])
            m2 = p.reduce("vector", sc8[:, 4:5], pr8[:], ALU.max, waits=[q2])
            k2 = p.ts("vector", mk8[1][:], pr8[:], sc8[:, 4:5], None,
                      ALU.is_equal, waits=[m2])
            ws_ = p.tt("vector", sc8[:, 5:6], sc8[:, 3:4], sc8[:, 4:5],
                       ALU.add, waits=[m2])
            wi_ = p.recip(sc8[:, 6:7], sc8[:, 5:6], waits=[ws_])
            w0_ = p.tt("vector", sc8[:, 7:8], sc8[:, 3:4], sc8[:, 6:7],
                       ALU.mult, waits=[wi_])
            w1_ = p.tt("vector", sc8[:, 8:9], sc8[:, 4:5], sc8[:, 6:7],
                       ALU.mult, waits=[w0_])
            pr_user = w1_
            off = c * TC + tsub * 128
            toks = []
            for j in range(2):
                tt1 = p.transpose(mtp, mk8[j][:], C.identF[:],
                                  waits=[k1 if j == 0 else k2, C.tok_identF,
                                         mt_user, toks[-1] if toks else None])
                ev = p.copy("vector", maskT[j][:, off:off + 128], mtp,
                            waits=[tt1])
                toks.append(ev)
                tt2 = p.transpose(wtp, sc8[:, 7 + j:8 + j], C.identF[:],
                                  waits=[w0_ if j == 0 else w1_, C.tok_identF,
                                         ev])
                ev2 = p.copy("vector", wrow[j][:, off:off + 128], wtp,
                             waits=[tt2])
                toks.append(ev2)
            mt_user = toks[-1]
    es_r.close()
    p.barrier()
    if FFN_PHASES < 2:
        es_w.close()
        return

    cswr = sb(nc, es_w, f"{tag}_cswr", [1, 1], F32)
    tcsw = p.ts("vector", cswr[:], swrow[:], -1.0, 1.0, ALU.mult, ALU.add)
    tw0 = p.ts("vector", wrow[0][:], wrow[0][:], cswr[0:1, 0:1], None,
               ALU.mult, waits=[tcsw])
    tw1 = p.ts("vector", wrow[1][:], wrow[1][:], cswr[0:1, 0:1], None,
               ALU.mult, waits=[tcsw, tw0])

    es_i = contextlib.ExitStack()
    integ = sb(nc, es_i, f"{tag}_ig", [128, PT * TC], BF16)
    vst = sb(nc, es_i, f"{tag}_vs", [128, PT * TC], BF16)
    xm = sb(nc, es_i, f"{tag}_xm", [128, PT * TC], BF16)
    sa = sb(nc, es_i, f"{tag}_sa", [128, PT * TC], BF16)
    sp_ = sb(nc, es_i, f"{tag}_sp", [128, PT * TC], BF16)
    sg = sb(nc, es_i, f"{tag}_sg", [128, PT * TC], BF16)
    vn = sb(nc, es_i, f"{tag}_vn", [128, PT * TC], BF16)
    Gb = sb(nc, es_i, f"{tag}_G", [128, PT * TC], BF16)
    vmix = sb(nc, es_i, f"{tag}_vm", [128, PT * TC], BF16)
    he = sb(nc, es_i, f"{tag}_he", [128, 4 * TC], BF16)
    hj = sb(nc, es_i, f"{tag}_hj", [128, 4 * TC], BF16)
    hsh = sb(nc, es_i, f"{tag}_hsh", [64, TC], BF16)
    h1b = sb(nc, es_i, f"{tag}_h1b", [128, 2 * TC], BF16)
    hrow = sb(nc, es_i, f"{tag}_hrow", [1, TC], F32)
    row01 = sb(nc, es_i, f"{tag}_r01", [1, 128], F32)
    t_r01 = p.ts("vector", row01[:], C.ones_row[:], 0.1, None, ALU.mult,
                 waits=[C.tok_ones_row])
    _xpd = sb(nc, es_i, f"{tag}_xpd", [128, TC], F32)
    expd = [_xpd, _xpd]
    tmp = [sb(nc, es_i, f"{tag}_tmp{i}", [128, TC], BF16) for i in range(3)]
    hpl = sb(nc, es_i, f"{tag}_hpl", [128, TC], BF16)
    h01 = sb(nc, es_i, f"{tag}_h01", [128, TC], BF16)
    wpl = sb(nc, es_i, f"{tag}_wpl", [128, TC], BF16)
    u = {i: [] for i in range(8)}
    yc_user = None
    st_user = None
    gate_user = {"sa": None, "sp": None, "sg": None}
    vn_user = None
    G_user = None
    vm_user = None
    xm_user = None
    he_user = None
    hj_user = None
    hsh_user = None
    h1_user = [None, None]
    hrow_user = None
    expd_user = [None, None]
    tmp_user = [None, None, None]
    hpl_user = None
    h01_user = None
    wpl_user = None
    spill_t = None
    last_ctx_mm = None
    for c in range(NCH):
        cs = slice(c * TC, (c + 1) * TC)
        it0 = p.dma("sync", integ[:].rearrange("p (a k) -> p a k", a=PT),
                    sy[:, :, cs], waits=[st_user, spill_t, last_ctx_mm])
        vz = p.memset("vector", vst[:], 0.0, waits=[st_user, last_ctx_mm])
        st_user = vz
        ctx = [integ[:, i * TC:(i + 1) * TC] for i in range(PT)] + \
              [vst[:, i * TC:(i + 1) * TC] for i in range(PT)]
        for it_ in range(2):
            xmt = []
            for i in range(PT):
                t = p.ts("vector", xm[:, i * TC:(i + 1) * TC],
                         integ[:, i * TC:(i + 1) * TC], muv[:, i:i + 1], None,
                         ALU.subtract, waits=[t_muv, it0 if it_ == 0 else None,
                                              st_user, xm_user])
                xmt.append(t)
            xm_user = xmt[-1]

            def gates_from_ctrl(mm_cb, bias_ap_fn, tagj):
                toks = {"sa": [], "sp": [], "sg": []}
                for fot in range(24):
                    pb = fot % 2
                    mmt = mm_cb(fot, B(4 + pb), u[4 + pb])
                    g = fot // 8
                    i = fot % 8
                    dst = (sa, sp_, sg)[g]
                    dsl = dst[:, i * TC:(i + 1) * TC]
                    bias = bias_ap_fn(fot)
                    if g == 1:
                        eb = expd[pb]
                        e1 = p.act(eb[:], B(4 + pb), AF.Exp, bias=bias,
                                   waits=[mmt, expd_user[0], expd_user[1],
                                          gate_user["sp"]])
                        u[4 + pb] = [e1]
                        e2 = p.act(dsl, eb[:], AF.Ln, bias=C.sc(1.0),
                                   waits=[e1, C.tok_scal])
                        expd_user[0] = e2
                        expd_user[1] = e2
                        toks["sp"].append(e2)
                    else:
                        nm = "sa" if g == 0 else "sg"
                        e1 = p.act(dsl, B(4 + pb), AF.Sigmoid, bias=bias,
                                   waits=[mmt, gate_user[nm]])
                        u[4 + pb] = [e1]
                        toks[nm].append(e1)
                return toks

            def inl_and_combine(toks, coef_kind):
                nonlocal G_user, vm_user, vn_user, wpl_user
                vnt = []
                for i in range(PT):
                    isl = slice(i * TC, (i + 1) * TC)
                    t1 = p.tt("vector", tmp[0][:], sa[:, isl], vst[:, isl],
                              ALU.mult, waits=[toks["sa"][i], st_user,
                                               tmp_user[0]])
                    t2 = p.tt("vector", tmp[1][:], sp_[:, isl], xm[:, isl],
                              ALU.mult, waits=[toks["sp"][i], xmt[i],
                                               tmp_user[1]])
                    t3 = p.tt("vector", vn[:, isl], tmp[0][:], tmp[1][:],
                              ALU.subtract, waits=[t1, t2, vn_user])
                    tmp_user[0] = t3
                    tmp_user[1] = t3
                    vnt.append(t3)
                    t4 = p.tt("vector", tmp[2][:], sg[:, isl], vn[:, isl],
                              ALU.mult, waits=[toks["sg"][i], t3, tmp_user[2]])
                    if coef_kind == "shared":
                        t5 = p.ts("vector", Gb[:, isl], tmp[2][:],
                                  swcol[:, 0:1], None, ALU.mult,
                                  waits=[t4, t_swc, G_user])
                        t6 = p.ts("vector", vmix[:, isl], vn[:, isl],
                                  swcol[:, 0:1], None, ALU.mult,
                                  waits=[t3, t_swc, vm_user])
                    else:
                        j = coef_kind
                        t5a = p.tt("vector", tmp[2][:], tmp[2][:], wpl[:],
                                   ALU.mult, waits=[t4, wpl_user])
                        t5 = p.tt("vector", Gb[:, isl], Gb[:, isl], tmp[2][:],
                                  ALU.add, waits=[t5a, G_user])
                        t6a = p.tt("vector", tmp[2][:], vn[:, isl], wpl[:],
                                   ALU.mult, waits=[t3, t5])
                        t6 = p.tt("vector", vmix[:, isl], vmix[:, isl],
                                  tmp[2][:], ALU.add, waits=[t6a, vm_user])
                        wpl_user = t6a
                    tmp_user[2] = t6
                    G_user = None
                    vm_user = None
                return vnt

            mmt = None
            for kt in range(16):
                w = [t_sw1] + ([st_user, it0, vz] if kt == 0 else []) + \
                    (u[6] if kt == 0 else [])
                mmt = p.mm(PSUM[0:64, 6 * TC:7 * TC], sw1[kt][:, 0:64],
                           ctx[kt], start=(kt == 0), stop=(kt == 15), waits=w)
            hsh_t = p.act(hsh[:], PSUM[0:64, 6 * TC:7 * TC], AF.Gelu,
                          bias=sb1c[0:64, 0:1], waits=[mmt, t_sb1c, hsh_user])
            u[6] = [hsh_t]
            hsh_user = None

            def mm_shared(fot, bank, uguard):
                w = [hsh_t, t_sw2] + uguard
                m1 = p.mm(bank, sw2[:, fot * 128:(fot + 1) * 128], hsh[:],
                          start=True, stop=True, waits=w)
                return m1
            tk_sh = gates_from_ctrl(mm_shared,
                                    lambda fot: sb2v[:, fot:fot + 1], "sh")
            vn_user = None
            vns = inl_and_combine(tk_sh, "shared")
            vn_user = vns[-1]
            hsh_user = mmt
            het = []
            for fot in range(4):
                pb = fot % 2
                mmt = None
                for kt in range(16):
                    w = [t_w1a] + ([it0, vz, vn_user] if kt == 0 else []) + \
                        (u[4 + pb] if kt == 0 else [])
                    mmt = p.mm(B(4 + pb), w1a[kt][:, fot * 128:(fot + 1) * 128],
                               ctx[kt], start=(kt == 0), stop=(kt == 15),
                               waits=w)
                e = p.act(he[:, fot * TC:(fot + 1) * TC], B(4 + pb), AF.Gelu,
                          bias=b1a[:, fot:fot + 1],
                          waits=[mmt, t_b1a, he_user])
                u[4 + pb] = [e]
                het.append(e)
            he_user = None
            for j in range(2):
                wm = p.mm(B(2), C.ones_row_bf[:], wrow[j][0:1, cs], True, True,
                          waits=[tw1, C.tok_ones_row_bf] + u[2])
                wev = p.act(wpl[:], B(2), AF.Identity, waits=[wm, wpl_user])
                u[2] = [wev]
                hjt = []
                for fot in range(4):
                    mm_ = p.mm(B(3), epat[:, fot * 128:(fot + 1) * 128],
                               maskT[j][:, cs], True, True,
                               waits=[t_epat, mt_user] + u[3])
                    t = p.tt("vector", hj[:, fot * TC:(fot + 1) * TC],
                             he[:, fot * TC:(fot + 1) * TC], B(3, w=TC),
                             ALU.mult, waits=[mm_, het[fot], hj_user])
                    u[3] = [t]
                    hjt.append(t)
                hj_user = None

                def mm_exp(fot, bank, uguard, j=j, hjt=hjt):
                    m1 = None
                    for kt in range(4):
                        w = [hjt[kt], t_w2a] + (uguard if kt == 0 else [])
                        m1 = p.mm(bank, w2a[kt][:, fot * 128:(fot + 1) * 128],
                                  hj[:, kt * TC:(kt + 1) * TC],
                                  start=(kt == 0), stop=False, waits=w)
                    m1 = p.mm(bank, b2a[:, fot * 128:(fot + 1) * 128],
                              maskT[j][:, cs], start=False, stop=True,
                              waits=[t_b2a, mt_user])
                    return m1
                tk_j = gates_from_ctrl(mm_exp, lambda fot: 0.0, f"e{j}")
                vns = inl_and_combine(tk_j, j)
                vn_user = vns[-1]
                hj_user = m1_last = None
            for fo in range(2):
                pb = fo % 2
                mmt = None
                for kt in range(PT):
                    w = [t_hw1, st_user, it0] + (u[4 + pb] if kt == 0 else [])
                    mmt = p.mm(B(4 + pb), hw1[kt][:, fo * 128:(fo + 1) * 128],
                               integ[:, kt * TC:(kt + 1) * TC],
                               start=(kt == 0), stop=(kt == PT - 1), waits=w)
                e = p.act(h1b[:, fo * TC:(fo + 1) * TC], B(4 + pb), AF.Gelu,
                          bias=hb1[:, fo:fo + 1], waits=[mmt, t_hb1,
                                                         h1_user[pb]])
                u[4 + pb] = [e]
                h1_user[pb] = e
            mmt = None
            for kt in range(2):
                w = [h1_user[kt], t_hw2] + (u[6] if kt == 0 else [])
                mmt = p.mm(PSUM[0:1, 6 * TC:7 * TC], hw2[kt][:],
                           h1b[:, kt * TC:(kt + 1) * TC], start=(kt == 0),
                           stop=(kt == 1), waits=w)
            hr1 = p.act(hrow[:, 0:TC], PSUM[0:1, 6 * TC:7 * TC], AF.Sigmoid,
                        bias=hb2c[0:1, 0:1], waits=[mmt, t_hb2c, hrow_user])
            u[6] = [hr1]
            hrow_user = hr1
            hm1 = p.mm(B(2), C.ones_row[:], hrow[:, 0:TC], True, True,
                       waits=[hr1, C.tok_ones_row] + u[2])
            hpe = p.act(hpl[:], B(2), AF.Identity, waits=[hm1, hpl_user])
            u[2] = [hpe]
            hm2 = p.mm(B(3), row01[:], hrow[:, 0:TC], True, True,
                       waits=[hr1, t_r01] + u[3])
            h0e = p.act(h01[:], B(3), AF.Identity, waits=[hm2, h01_user])
            u[3] = [h0e]
            hpl_user = None
            h01_user = None
            upd = []
            for i in range(PT):
                isl = slice(i * TC, (i + 1) * TC)
                t1 = p.tt("vector", tmp[0][:], Gb[:, isl], h01[:],
                          ALU.mult, waits=[h0e, G_user, tmp_user[0]])
                t2 = p.tt("vector", integ[:, isl], integ[:, isl], tmp[0][:],
                          ALU.add, waits=[t1])
                tmp_user[0] = t2
                t3 = p.tt("vector", tmp[1][:], vmix[:, isl], vst[:, isl],
                          ALU.subtract, waits=[vm_user, tmp_user[1], t2])
                t4 = p.tt("vector", tmp[2][:], tmp[1][:], hpl[:], ALU.mult,
                          waits=[t3, hpe, tmp_user[2]])
                tmp_user[1] = t4
                t5 = p.tt("vector", vst[:, isl], vst[:, isl], tmp[2][:],
                          ALU.add, waits=[t4])
                tmp_user[2] = t5
                upd.append(t5)
            st_user = upd[-1]
            G_user = upd[-1]
            vm_user = upd[-1]
            hpl_user = upd[-1]
            h01_user = upd[-1]
            vn_user = upd[-1]
            xm_user = upd[-1]
            he_user = upd[-1]
            hj_user = upd[-1]
            hsh_user = upd[-1]
            hrow_user = upd[-1]
            gate_user = {"sa": upd[-1], "sp": upd[-1], "sg": upd[-1]}
            expd_user = [upd[-1], upd[-1]]
            tmp_user = [None, None, None]
        last_ctx_mm = mmt
        spill_t = p.dma("sync", si[:, :, cs],
                        integ[:].rearrange("p (a k) -> p a k", a=PT),
                        waits=[st_user] + upd)

    es_i.close()
    es_w.close()
    p.barrier()
    if FFN_PHASES < 3:
        return

    rf1, t_rf1 = ldw("refine_w1", PT, 2 * D, es=es)
    rf2, t_rf2 = ldw("refine_w2", 16, D, es=es)
    rfb1, t_rfb1 = load_vec_pf(nc, es, p, f"{tag}_rfb1", dram["refine_b1"], 16)
    rfb2, t_rfb2 = load_vec_pf(nc, es, p, f"{tag}_rfb2", dram["refine_b2"], PT)
    igf = sb(nc, es, f"{tag}_igf", [128, PT * TC], BF16)
    ycf = sb(nc, es, f"{tag}_ycf", [128, PT * TC], BF16)
    midb = sb(nc, es, f"{tag}_mid", [128, 16 * TC], BF16)
    oxb = [sb(nc, es, f"{tag}_oxf{i}", [128, TC], BF16) for i in range(2)]
    rscr = [sb(nc, es, f"{tag}_rs{i}", [128, TC], F32) for i in range(2)]
    xo = dram["xo"]
    ig_user = None
    ycf_user = None
    mid_user = [None] * 16
    oxb_user = [None, None]
    rs_user = [None, None]
    uF = {i: [] for i in range(8)}
    for c in range(NCH):
        cs = slice(c * TC, (c + 1) * TC)
        ilt = p.dma("sync", igf[:].rearrange("p (a k) -> p a k", a=PT),
                    si[:, :, cs], waits=[ig_user, spill_t if c == 0 else None])
        ylt = p.dma("sync", ycf[:].rearrange("p (a k) -> p a k", a=PT),
                    sy[:, :, cs], waits=[ycf_user])
        xbi = xs.load(c)
        mts = []
        for fot in range(16):
            pb = fot % 2
            mmt = None
            for kt in range(PT):
                w = [ilt, t_rf1] + (uF[4 + pb] if kt == 0 else [])
                mmt = p.mm(B(4 + pb), rf1[kt][:, fot * 128:(fot + 1) * 128],
                           igf[:, kt * TC:(kt + 1) * TC], start=(kt == 0),
                           stop=(kt == PT - 1), waits=w)
            e = p.act(midb[:, fot * TC:(fot + 1) * TC], B(4 + pb), AF.Gelu,
                      bias=rfb1[:, fot:fot + 1],
                      waits=[mmt, t_rfb1, mid_user[fot]])
            uF[4 + pb] = [e]
            mts.append(e)
        for fo in range(PT):
            pb = fo % 2
            mmt = None
            for kt in range(16):
                w = [mts[kt], t_rf2] + (uF[6 + pb] if kt == 0 else [])
                mmt = p.mm(B(6 + pb), rf2[kt][:, fo * 128:(fo + 1) * 128],
                           midb[:, kt * TC:(kt + 1) * TC], start=(kt == 0),
                           stop=(kt == 15), waits=w)
            e = p.act(rscr[pb][:], B(6 + pb), AF.Identity,
                      bias=rfb2[:, fo:fo + 1], waits=[mmt, t_rfb2,
                                                      rs_user[pb]])
            uF[6 + pb] = [e]
            xap, xtk = xs.tile(xbi, fo)
            t1 = p.ts("vector", rscr[pb][:], rscr[pb][:], iwv[:, fo:fo + 1],
                      None, ALU.mult, waits=[e, t_iwv])
            t2 = p.tt("vector", rscr[pb][:], rscr[pb][:],
                      ycf[:, fo * TC:(fo + 1) * TC], ALU.add,
                      waits=[t1, ylt])
            rt = p.tt("vector", oxb[pb][:], rscr[pb][:], xap, ALU.add,
                      waits=[t2, xtk, oxb_user[pb]])
            rs_user[pb] = rt
            oxb_user[pb] = p.dma("sync", xo[fo * 128:(fo + 1) * 128, cs],
                                 oxb[pb][:], waits=[rt])
        xs.use(xbi, rt)
        ig_user = mts[-1]
        for fot in range(16):
            mid_user[fot] = mmt
        ycf_user = rt


def build_l0_ffn_program(in_dtype="bf16"):
    nc = bass.Bass("TRN2", target_bir_lowering=False)
    es = contextlib.ExitStack()
    xin = nc.dram_tensor("xin", [D, T], F32 if in_dtype == "f32" else BF16,
                         kind="ExternalInput").ap()
    dr = {}
    shapes = {
        "router_w1": [D, 256], "router_w2": [256, 8], "halt_w1": [D, 256],
        "halt_w2": [256, 1], "shared_w1": [2 * D, 64], "shared_w2": [64, 3 * D],
        "W1_all": [2 * D, 512], "W2_all": [512, 3 * D], "expert_b2": [8, 3 * D],
        "refine_w1": [D, 2 * D], "refine_w2": [2 * D, D],
        "router_b2": [8], "shared_b1": [64], "halt_b2": [1],
        "shared_weight": [1], "cd_epat": [8, 4 * 128],
    }
    for n, s in shapes.items():
        dt = BF16 if n == "cd_epat" else F32
        dr[n] = nc.dram_tensor(n, s, dt, kind="ExternalInput").ap()
    for n, npad in [("router_b1", 2), ("halt_b1", 2), ("expert_b1", 4),
                    ("shared_b2", 24), ("mu", PT), ("integration_weight", PT),
                    ("refine_b1", 16), ("refine_b2", PT)]:
        dr[n] = nc.dram_tensor(n, [128, npad], F32, kind="ExternalInput").ap()
    dr["xo"] = nc.dram_tensor("xo", [D, T], BF16, kind="ExternalOutput").ap()
    dr["s_y"] = nc.dram_tensor("s_y", [D, T], BF16, kind="Internal").ap()
    dr["s_i"] = nc.dram_tensor("s_i", [D, T], BF16, kind="Internal").ap()
    with es:
        p, sems = make_prog(nc, es)
        C = Consts(nc, es, p)
        dr["xin"] = xin
        dr["cast"] = (in_dtype == "f32")
        emit_l0_ffn(nc, es, p, C, dr, "ff")
        p.add("gpsimd", lambda e: e.engine_nop(), waits=p.all_tokens(), inc=1)
        p.finalize(sems)
    return nc


QC_AA = 256


def _host_dmask():
    k = np.arange(128)[:, None]
    q = np.arange(QC_AA)[None, :]
    m0 = (k <= q).astype(np.float32)
    m1 = (k <= q - 128).astype(np.float32)
    return np.concatenate([m0, m1], axis=1).astype(BF)


def _host_epat():
    epat = np.zeros((8, 512), np.float32)
    for t in range(4):
        for a in range(2):
            epat[2 * t + a, t * 128 + a * 64:t * 128 + (a + 1) * 64] = 1.0
    return epat.astype(BF)


def _host_bsel():
    bsel = np.zeros((32, 16 * 128), np.float32)
    for n in range(16):
        bsel[16 + n, n * 128:(n + 1) * 128] = 1.0
    return bsel.astype(BF)


_PROGRAMS = {}


def _prog(name, builder):
    if name not in _PROGRAMS:
        _PROGRAMS[name] = builder()
    return _PROGRAMS[name]


def kernel(**g):
    return kernel_fused(**g)


def kernel_unfused(**g):
    CO = host_consts()
    ncores = 8
    xts = [np.ascontiguousarray(np.asarray(g["L0_attn_a_X"][b]).T)
           for b in range(ncores)]

    def run(nc, maps):
        res = run_bass_kernel_spmd(nc, maps, core_ids=list(range(ncores)))
        return [r["xo"] for r in res.results]

    nc = _prog("aa", lambda: build_l0_attn_a_program(in_dtype="f32"))
    wm = {
        "wqT": _c(g["L0_attn_a_q_proj"].T), "wkT": _c(g["L0_attn_a_k_proj"].T),
        "wvT": _c(g["L0_attn_a_v_proj"].T), "woT": _c(g["L0_attn_a_o_proj"].T),
        "cd_dmask": _host_dmask(), **CO,
    }
    xs_ = run(nc, [{"xin": xts[b], **wm} for b in range(ncores)])

    def ffn_maps():
        return {
            "router_w1": g['L0_ffn_router_w1'], "router_w2": g['L0_ffn_router_w2'],
            "halt_w1": g['L0_ffn_halt_w1'], "halt_w2": g['L0_ffn_halt_w2'],
            "shared_w1": g['L0_ffn_shared_w1'], "shared_w2": g['L0_ffn_shared_w2'],
            "W1_all": _c(np.asarray(g['L0_ffn_expert_w1']).transpose(1, 0, 2)
                         .reshape(2048, 512)),
            "W2_all": _c(np.asarray(g['L0_ffn_expert_w2']).reshape(512, 3072)),
            "expert_b2": g['L0_ffn_expert_b2'],
            "refine_w1": g['L0_ffn_refine_w1'], "refine_w2": g['L0_ffn_refine_w2'],
            "router_b2": g['L0_ffn_router_b2'], "shared_b1": g['L0_ffn_shared_b1'],
            "halt_b2": np.asarray(g['L0_ffn_halt_b2']).reshape(1),
            "shared_weight": np.asarray(g['L0_ffn_shared_weight']).reshape(1),
            "cd_epat": _host_epat(),
            "router_b1": _pf(g['L0_ffn_router_b1']),
            "halt_b1": _pf(g['L0_ffn_halt_b1']),
            "expert_b1": _pf(np.asarray(g['L0_ffn_expert_b1']).reshape(-1)),
            "shared_b2": _pf(g['L0_ffn_shared_b2']),
            "mu": _pf(g['L0_ffn_mu']),
            "integration_weight": _pf(g['L0_ffn_integration_weight']),
            "refine_b1": _pf(g['L0_ffn_refine_b1']),
            "refine_b2": _pf(g['L0_ffn_refine_b2']), **CO,
        }

    nc = _prog("ffn", lambda: build_l0_ffn_program(in_dtype="bf16"))
    fm = ffn_maps()
    xs_ = run(nc, [{"xin": xs_[b], **fm} for b in range(ncores)])

    nc = _prog("ab", lambda: build_l0_attn_b_program(in_dtype="bf16"))
    wm = {
        "Wq": g['L0_attn_b_Wq'], "Wk": g['L0_attn_b_Wk'],
        "Wv": g['L0_attn_b_Wv'], "Wo": g['L0_attn_b_Wo'],
        "bq": g['L0_attn_b_bq'], "bk": g['L0_attn_b_bk'],
        "bv": g['L0_attn_b_bv'], "bo": _pf(g['L0_attn_b_bo']), **CO,
    }
    xs_ = run(nc, [{"xin": xs_[b], **wm} for b in range(ncores)])

    nc = _prog("ffn", lambda: build_l0_ffn_program(in_dtype="bf16"))
    xs_ = run(nc, [{"xin": xs_[b], **fm} for b in range(ncores)])

    nc = _prog("l1aa", lambda: build_l1_attn_a_program(in_dtype="bf16"))
    wm = {
        "Wq": g['L1_attn_a_Wq'], "Wk": g['L1_attn_a_Wk'],
        "Wv": g['L1_attn_a_Wv'],
        "omegaT": _c(np.asarray(g['L1_attn_a_omega']).T), **CO,
    }
    xs_ = run(nc, [{"xin": xs_[b], **wm} for b in range(ncores)])

    nc = _prog("l1f", lambda: build_l1_ffn_program(final=False, in_dtype="bf16"))
    wm = {"lw": g['L1_ffn_weight'], "lb": _pf(g['L1_ffn_bias']), **CO}
    xs_ = run(nc, [{"xin": xs_[b], **wm} for b in range(ncores)])

    nc = _prog("ssm", lambda: build_l1_attn_b_program(in_dtype="bf16"))
    wm = {
        "W_in": g['L1_attn_b_W_in'], "W_time": g['L1_attn_b_W_time'],
        "W_out": g['L1_attn_b_W_out'],
        "W_state_pf": _c(np.asarray(g['L1_attn_b_W_state'])
                         .reshape(16, 128, 32).transpose(1, 0, 2)
                         .reshape(128, 512)),
        "conv_w_pf": _pf2(g['L1_attn_b_conv_w'], 3),
        "conv_b": _pf(g['L1_attn_b_conv_b']),
        "b_time": _pf(g['L1_attn_b_b_time']),
        "skip": _pf(g['L1_attn_b_skip']),
        "A_log_pf": _pf2(g['L1_attn_b_A_log'], 16),
        "cd_bsel": _host_bsel(), **CO,
    }
    xs_ = run(nc, [{"xin": xs_[b], **wm} for b in range(ncores)])

    nc = _prog("l1ff", lambda: build_l1_ffn_program(final=True, in_dtype="bf16"))
    wm = {"lw": g['L1_ffn_weight'], "lb": _pf(g['L1_ffn_bias']), **CO}
    xs_ = run(nc, [{"xin": xs_[b], **wm} for b in range(ncores)])

    out = np.stack([np.ascontiguousarray(np.asarray(x, np.float32).T)
                    for x in xs_], axis=0)
    return out


def build_fused_program():
    nc = bass.Bass("TRN2", target_bir_lowering=False)
    es = contextlib.ExitStack()

    def din(name, shape, dtype=F32):
        return nc.dram_tensor("in_" + name, shape, dtype,
                              kind="ExternalInput").ap()

    xin = din("x0", [D, T])
    xout = nc.dram_tensor("xf", [D, T], F32, kind="ExternalOutput").ap()
    hand = [nc.dram_tensor(f"h{i}", [D, T], BF16, kind="Internal").ap()
            for i in range(7)]

    aa = {"xin": xin, "cast": True, "xo": hand[0],
          "wqT": din("aa_wqT", [D, D]), "wkT": din("aa_wkT", [D, D]),
          "wvT": din("aa_wvT", [D, D]), "woT": din("aa_woT", [D, D]),
          "cd_dmask": din("cd_dmask", [128, 512], BF16),
          "s_k": nc.dram_tensor("scr_aa_sk", [D, T], BF16, kind="Internal").ap(),
          "s_y": nc.dram_tensor("scr_aa_sy", [D, T], BF16, kind="Internal").ap()}

    def ffn_dram(i, x_i, x_o):
        d = {"xin": x_i, "xo": x_o, "cast": False}
        for n, s in [("router_w1", [D, 256]), ("router_w2", [256, 8]),
                     ("halt_w1", [D, 256]), ("halt_w2", [256, 1]),
                     ("shared_w1", [2 * D, 64]), ("shared_w2", [64, 3 * D]),
                     ("W1_all", [2 * D, 512]), ("W2_all", [512, 3 * D]),
                     ("expert_b2", [8, 3 * D]), ("refine_w1", [D, 2 * D]),
                     ("refine_w2", [2 * D, D]), ("router_b2", [8]),
                     ("shared_b1", [64]), ("halt_b2", [1]),
                     ("shared_weight", [1])]:
            d[n] = din(f"ff{i}_{n}" if i == 0 else f"ff1_{n}", s) \
                if False else din(f"ff{i}_{n}", s)
        d["cd_epat"] = din(f"ff{i}_cd_epat", [8, 512], BF16)
        for n, npad in [("router_b1", 2), ("halt_b1", 2), ("expert_b1", 4),
                        ("shared_b2", 24), ("mu", PT),
                        ("integration_weight", PT), ("refine_b1", 16),
                        ("refine_b2", PT)]:
            d[n] = din(f"ff{i}_{n}", [128, npad])
        d["s_y"] = nc.dram_tensor(f"scr_ff{i}_sy", [D, T], BF16, kind="Internal").ap()
        d["s_i"] = nc.dram_tensor(f"scr_ff{i}_si", [D, T], BF16, kind="Internal").ap()
        return d

    ff0 = ffn_dram(0, hand[0], hand[1])
    ab = {"xin": hand[1], "xo": hand[2], "cast": False,
          "Wq": din("ab_Wq", [D, D]), "Wk": din("ab_Wk", [D, D]),
          "Wv": din("ab_Wv", [D, D]), "Wo": din("ab_Wo", [D, D]),
          "bq": din("ab_bq", [D]), "bk": din("ab_bk", [D]),
          "bv": din("ab_bv", [D]), "bo": din("ab_bo", [128, PT]),
          "s_q": nc.dram_tensor("scr_ab_sq", [D, T], BF16, kind="Internal").ap(),
          "s_v": nc.dram_tensor("scr_ab_sv", [T, D], BF16, kind="Internal").ap()}
    ff1 = ffn_dram(1, hand[2], hand[3])
    pa = {"xin": hand[3], "xo": hand[4], "cast": False,
          "Wq": din("pa_Wq", [D, D]), "Wk": din("pa_Wk", [D, D]),
          "Wv": din("pa_Wv", [D, D]), "omegaT": din("pa_omegaT", [64, 256]),
          "s_q": nc.dram_tensor("scr_pa_sq", [D, T], BF16, kind="Internal").ap(),
          "s_k": nc.dram_tensor("scr_pa_sk", [D, T], BF16, kind="Internal").ap(),
          "s_o": nc.dram_tensor("scr_pa_so", [D, T], BF16, kind="Internal").ap(),
          "s_v": nc.dram_tensor("scr_pa_sv", [T, D], BF16, kind="Internal").ap(),
          "s_row": nc.dram_tensor("scr_pa_srow", [T], F32, kind="Internal").ap()}
    l1f = {"lw": din("l1_lw", [D, D]), "lb": din("l1_lb", [128, PT])}
    sbd = {"xin": hand[5], "xo": hand[6], "cast": False,
           "W_in": din("sb_W_in", [D, 4 * D]),
           "W_time": din("sb_W_time", [2 * D, 2 * D]),
           "W_state_pf": din("sb_W_state_pf", [128, 512]),
           "W_out": din("sb_W_out", [2 * D, D]),
           "conv_w_pf": din("sb_conv_w_pf", [128, 48]),
           "conv_b": din("sb_conv_b", [128, 16]),
           "b_time": din("sb_b_time", [128, 16]),
           "skip": din("sb_skip", [128, 16]),
           "A_log_pf": din("sb_A_log_pf", [128, 256]),
           "cd_bsel": din("cd_bsel", [32, 2048], BF16),
           "s_h": nc.dram_tensor("scr_sb_sh", [2 * D, T], BF16, kind="Internal").ap(),
           "s_g": nc.dram_tensor("scr_sb_sg", [2 * D, T], BF16, kind="Internal").ap()}
    l2f = {"lw": din("l2_lw", [D, D]), "lb": din("l2_lb", [128, PT])}

    with es:
        p, sems = make_prog(nc, es)
        C = Consts(nc, es, p)
        PSUM = ps(nc, es, "psum", [128, 4096])

        def block(fn):
            with contextlib.ExitStack() as eb:
                fn(eb)
            p.barrier()

        block(lambda eb: emit_l0_attn_a(nc, eb, p, C, aa, "aa", psum=PSUM))
        block(lambda eb: emit_l0_ffn(nc, eb, p, C, ff0, "f0", psum=PSUM))
        block(lambda eb: emit_l0_attn_b(nc, eb, p, C, ab, "ab", psum=PSUM))
        block(lambda eb: emit_l0_ffn(nc, eb, p, C, ff1, "f1", psum=PSUM))
        block(lambda eb: emit_l1_attn_a(nc, eb, p, C, pa, "pa", psum=PSUM))

        def l1ffn_block(eb, dram_w, x_i, x_o, tag, final):
            xt = [sb(nc, eb, f"{tag}_xt{i}", [128, T], BF16) for i in range(PT)]
            xv = x_i.rearrange("(a p) t -> p a t", p=128)
            xtok = []
            for i in range(PT):
                xtok.append(p.dma("sync", xt[i][:], x_i[i * 128:(i + 1) * 128, :]))
            d = dict(dram_w)
            d["xo"] = x_o
            emit_l1_ffn(nc, eb, p, C, xt, xtok, d, tag, final=final, psum=PSUM)

        block(lambda eb: l1ffn_block(eb, l1f, hand[4], hand[5], "g0", False))
        block(lambda eb: emit_l1_attn_b(nc, eb, p, C, sbd, "sb", psum=PSUM))
        block(lambda eb: l1ffn_block(eb, l2f, hand[6], xout, "g1", True))
        p.add("gpsimd", lambda e: e.engine_nop(), waits=p.all_tokens(), inc=1)
        p.finalize(sems)
    return nc


def host_fused_maps(g):
    CO = host_consts()
    m = {
        "aa_wqT": _c(np.asarray(g["L0_attn_a_q_proj"]).T),
        "aa_wkT": _c(np.asarray(g["L0_attn_a_k_proj"]).T),
        "aa_wvT": _c(np.asarray(g["L0_attn_a_v_proj"]).T),
        "aa_woT": _c(np.asarray(g["L0_attn_a_o_proj"]).T),
        "cd_dmask": _host_dmask(),
        "ab_Wq": g['L0_attn_b_Wq'], "ab_Wk": g['L0_attn_b_Wk'],
        "ab_Wv": g['L0_attn_b_Wv'], "ab_Wo": g['L0_attn_b_Wo'],
        "ab_bq": g['L0_attn_b_bq'], "ab_bk": g['L0_attn_b_bk'],
        "ab_bv": g['L0_attn_b_bv'], "ab_bo": _pf(g['L0_attn_b_bo']),
        "pa_Wq": g['L1_attn_a_Wq'], "pa_Wk": g['L1_attn_a_Wk'],
        "pa_Wv": g['L1_attn_a_Wv'],
        "pa_omegaT": _c(np.asarray(g['L1_attn_a_omega']).T),
        "l1_lw": g['L1_ffn_weight'], "l1_lb": _pf(g['L1_ffn_bias']),
        "l2_lw": g['L1_ffn_weight'], "l2_lb": _pf(g['L1_ffn_bias']),
        "sb_W_in": g['L1_attn_b_W_in'], "sb_W_time": g['L1_attn_b_W_time'],
        "sb_W_out": g['L1_attn_b_W_out'],
        "sb_W_state_pf": _c(np.asarray(g['L1_attn_b_W_state'])
                            .reshape(16, 128, 32).transpose(1, 0, 2)
                            .reshape(128, 512)),
        "sb_conv_w_pf": _pf2(g['L1_attn_b_conv_w'], 3),
        "sb_conv_b": _pf(g['L1_attn_b_conv_b']),
        "sb_b_time": _pf(g['L1_attn_b_b_time']),
        "sb_skip": _pf(g['L1_attn_b_skip']),
        "sb_A_log_pf": _pf2(g['L1_attn_b_A_log'], 16),
        "cd_bsel": _host_bsel(),
    }
    for i in range(2):
        m.update({
            f"ff{i}_router_w1": g['L0_ffn_router_w1'],
            f"ff{i}_router_w2": g['L0_ffn_router_w2'],
            f"ff{i}_halt_w1": g['L0_ffn_halt_w1'],
            f"ff{i}_halt_w2": g['L0_ffn_halt_w2'],
            f"ff{i}_shared_w1": g['L0_ffn_shared_w1'],
            f"ff{i}_shared_w2": g['L0_ffn_shared_w2'],
            f"ff{i}_W1_all": _c(np.asarray(g['L0_ffn_expert_w1'])
                                .transpose(1, 0, 2).reshape(2048, 512)),
            f"ff{i}_W2_all": _c(np.asarray(g['L0_ffn_expert_w2'])
                                .reshape(512, 3072)),
            f"ff{i}_expert_b2": g['L0_ffn_expert_b2'],
            f"ff{i}_refine_w1": g['L0_ffn_refine_w1'],
            f"ff{i}_refine_w2": g['L0_ffn_refine_w2'],
            f"ff{i}_router_b2": g['L0_ffn_router_b2'],
            f"ff{i}_shared_b1": g['L0_ffn_shared_b1'],
            f"ff{i}_halt_b2": np.asarray(g['L0_ffn_halt_b2']).reshape(1),
            f"ff{i}_shared_weight": np.asarray(g['L0_ffn_shared_weight']).reshape(1),
            f"ff{i}_cd_epat": _host_epat(),
            f"ff{i}_router_b1": _pf(g['L0_ffn_router_b1']),
            f"ff{i}_halt_b1": _pf(g['L0_ffn_halt_b1']),
            f"ff{i}_expert_b1": _pf(np.asarray(g['L0_ffn_expert_b1']).reshape(-1)),
            f"ff{i}_shared_b2": _pf(g['L0_ffn_shared_b2']),
            f"ff{i}_mu": _pf(g['L0_ffn_mu']),
            f"ff{i}_integration_weight": _pf(g['L0_ffn_integration_weight']),
            f"ff{i}_refine_b1": _pf(g['L0_ffn_refine_b1']),
            f"ff{i}_refine_b2": _pf(g['L0_ffn_refine_b2']),
        })
    return {**{("in_" + k): v for k, v in m.items()}, **CO}


def kernel_fused(**g):
    nc = _prog("fused", build_fused_program)
    base = host_fused_maps(g)
    maps = [{"in_x0": np.ascontiguousarray(np.asarray(g["L0_attn_a_X"][b]).T),
             **base} for b in range(8)]
    res = run_bass_kernel_spmd(nc, maps, core_ids=list(range(8)))
    return np.stack([np.ascontiguousarray(np.asarray(r["xf"], np.float32).T)
                     for r in res.results], axis=0)
